# version 05
# speedup vs baseline: 1.0420x; 1.0035x over previous
import numpy as np
from contextlib import ExitStack

import concourse.bass as bass
import concourse.mybir as mybir
from concourse.bass_utils import run_bass_kernel_spmd

F32 = mybir.dt.float32
BF16 = mybir.dt.bfloat16
AF = mybir.ActivationFunctionType
ALU = mybir.AluOpType
AX = mybir.AxisListType

D = 4096
DC = 32
SEQ = 4096
BATCH = 2
NCORE = 8
TOK = 1024
HALF = 512
NSLOT = 8
TB = 128
NKT = 32
HKV = 8
G = 4
HD = 128
LN_EPS = 1e-5
RMS_EPS = 1.1920929e-07
ATT_SCALE = HD ** -0.5

PFIX = [8 * (s % 4 + 1) for s in range(NSLOT)]
NGRP = [p // 8 for p in PFIX]

RANKS = list(range(NCORE))


def slot_to_block(core, s):
    idx = s % 4
    if s < 4:
        return core + 8 * idx
    return (7 - core) + 8 * idx


def block_owner(b, kt):
    if b == 0:
        return kt % 8, kt // 8
    return 7 - (kt % 8), 4 + kt // 8



def host_build_inputs(inputs):
    x = np.asarray(inputs["L0_attn_a_hidden_states"], np.float32)
    wq_a = np.ascontiguousarray(np.asarray(inputs["L0_attn_a_w_q"], np.float32))
    wk_a = np.ascontiguousarray(np.asarray(inputs["L0_attn_a_w_k"], np.float32))
    wv_a = np.ascontiguousarray(np.asarray(inputs["L0_attn_a_w_v"], np.float32))
    wo_a = np.ascontiguousarray(np.asarray(inputs["L0_attn_a_w_o"], np.float32))
    Wq_b = np.asarray(inputs["L0_attn_b_Wq"], np.float32)
    bq_b = np.asarray(inputs["L0_attn_b_bq"], np.float32)
    Wo_b = np.asarray(inputs["L0_attn_b_Wo"], np.float32)
    M_b = np.ascontiguousarray(np.asarray(inputs["L0_attn_b_M"], np.float32))
    z_b = np.ascontiguousarray(np.asarray(inputs["L0_attn_b_z"], np.float32))
    wres = np.ascontiguousarray(np.asarray(inputs["L0_ffn_W_res"], np.float32))
    wpre = np.ascontiguousarray(np.asarray(inputs["L0_ffn_W_pre"], np.float32))
    wpost = np.ascontiguousarray(np.asarray(inputs["L0_ffn_W_post"], np.float32))
    hres_b = np.ascontiguousarray(np.asarray(inputs["L0_ffn_H_res_base"], np.float32))
    hpre_b = np.ascontiguousarray(np.asarray(inputs["L0_ffn_H_pre_base"], np.float32))
    hpost_b = np.ascontiguousarray(np.asarray(inputs["L0_ffn_H_post_base"], np.float32))
    gate = np.ascontiguousarray(np.asarray(inputs["L0_ffn_gate"], np.float32))
    wbr = np.ascontiguousarray(np.asarray(inputs["L0_ffn_W_branch"], np.float32))

    sel = np.concatenate([np.arange(4 * h * 128, 4 * h * 128 + 128) for h in range(HKV)])
    wqsel = np.ascontiguousarray(Wq_b[sel, :])
    bqsel = np.ascontiguousarray(bq_b[sel])

    import ml_dtypes
    bf = ml_dtypes.bfloat16

    ident = np.eye(128, dtype=np.float32)
    ipat = np.tile(np.eye(4, dtype=np.float32).reshape(1, 16), (128, 1)).copy()

    in_maps = []
    for c in range(NCORE):
        cols = []
        for s in range(NSLOT):
            b = s // 4
            blk = slot_to_block(c, s)
            cols.append(x[b, blk * TB:(blk + 1) * TB, :].T)
        x_fm = np.ascontiguousarray(np.concatenate(cols, axis=1), dtype=np.float32)

        maskt = np.zeros((NSLOT, 8, 128, 128), np.float32)
        triu = np.triu(np.ones((128, 128), np.float32))
        for s in range(NSLOT):
            blk = slot_to_block(c, s)
            prefix = blk + 1
            base = PFIX[s] - 8
            for j in range(8):
                p = base + j
                if p < prefix - 1:
                    maskt[s, j] = 1.0
                elif p == prefix - 1:
                    maskt[s, j] = triu
        maskt = np.ascontiguousarray(maskt.transpose(0, 2, 1, 3)).astype(bf)

        wqsel_cols = np.ascontiguousarray(wqsel[:, 512 * c:512 * (c + 1)])
        wob_rows = np.ascontiguousarray(Wo_b[512 * c:512 * (c + 1), :])

        wsl = np.concatenate([
            wk_a[512 * c:512 * (c + 1), :].reshape(512, 1024),
            wv_a[512 * c:512 * (c + 1), :].reshape(512, 1024),
            wbr[128 * c:128 * (c + 1), :].reshape(128, 1024),
            wq_a[512 * c:512 * (c + 1), :].reshape(2048, 1024),
            wo_a[512 * c:512 * (c + 1), :].reshape(2048, 1024),
        ], axis=0)
        wslabs = np.ascontiguousarray(wsl, dtype=np.float32)
        _ = (inputs["L0_attn_b_Wk"], inputs["L0_attn_b_bk"],
             inputs["L0_attn_b_Wv"], inputs["L0_attn_b_bv"])

        selt = np.zeros((128, 24 * 128), np.float32)
        for r in range(24):
            selt[r, r * 128:(r + 1) * 128] = 1.0
        selt = selt.astype(bf)
        baser = np.concatenate([hres_b.reshape(16), hpre_b, hpost_b]).reshape(1, 24).astype(np.float32)

        in_maps.append({
            "xin": x_fm, "selt": selt, "baser": baser, "wslabs": wslabs,
            "wqsel_cols": wqsel_cols, "bqsel": bqsel, "wob_rows": wob_rows,
            "m_in": M_b, "z_in": z_b,
            "wres": wres, "wpre": wpre, "wpost": wpost,
            "hres_b": hres_b.reshape(16).copy(), "hpre_b": hpre_b, "hpost_b": hpost_b,
            "gate": gate, "wbr": wbr,
            "ident": ident, "ipat": ipat, "maskt": maskt,
        })
    return in_maps


def host_gather_output(results):
    out = np.empty((BATCH, SEQ, D), np.float32)
    for c in range(NCORE):
        y = results[c]["yout"]
        for s in range(NSLOT):
            b = s // 4
            blk = slot_to_block(c, s)
            out[b, blk * TB:(blk + 1) * TB, :] = y[:, s * TB:(s + 1) * TB].T
    return out



class MK:
    def __init__(self, stage="full"):
        self.stage = stage
        self._kv_prev_pe = None
        nc = bass.Bass("TRN2", target_bir_lowering=False)
        self.nc = nc
        self.es = ExitStack()
        self.cnt = {}
        self.sems = {}
        self.dbg_outputs = {}
        self.wdone = {}

        T = lambda n, s, d=F32: nc.dram_tensor(n, s, d, kind="ExternalInput").ap()
        self.xin = T("xin", [D, TOK])
        self.wslabs = T("wslabs", [5248, 1024])
        self.wqsel_cols = T("wqsel_cols", [1024, 512])
        self.bqsel = T("bqsel", [1024])
        self.wob_rows = T("wob_rows", [512, D])
        self.m_in = T("m_in", [HKV, 128, 128])
        self.z_in = T("z_in", [HKV, 128])
        self.wres = T("wres", [16, D])
        self.wpre = T("wpre", [4, D])
        self.wpost = T("wpost", [4, D])
        self.hres_b = T("hres_b", [16])
        self.hpre_b = T("hpre_b", [4])
        self.hpost_b = T("hpost_b", [4])
        self.gate = T("gate", [1])
        self.wbr = T("wbr", [1024, 1024])
        self.ident = T("ident", [128, 128])
        self.ipat = T("ipat", [128, 16])
        self.maskt = T("maskt", [NSLOT, 128, 8, 128], BF16)
        self.selt = T("selt", [128, 24 * 128], BF16)
        self.baser = T("baser", [1, 24])

        self.yout = nc.dram_tensor("yout", [D, TOK], F32, kind="ExternalOutput").ap()
        I = lambda n, s, d=BF16: nc.dram_tensor(n, s, d, kind="Internal").ap()
        self.xn0b1 = I("xn0b1", [DC, 128, HALF])
        self.ag2ain = nc.dram_tensor("ag2ain", [1152, 1024], BF16, kind="Internal")
        self.ag2aout = nc.dram_tensor("ag2aout", [1152 * NCORE, 1024], BF16, kind="Internal")
        self.ag2bin = nc.dram_tensor("ag2bin", [4096, 1024], BF16, kind="Internal")
        self.ag2bout = nc.dram_tensor("ag2bout", [4096 * NCORE, 1024], BF16, kind="Internal")
        self.agin = nc.dram_tensor("agin", [3072, 1024], BF16, kind="Internal")
        self.agout = nc.dram_tensor("agout", [3072 * NCORE, 1024], BF16, kind="Internal")

        sb = lambda n, s, d: self.es.enter_context(nc.sbuf_tensor(n, s, d))
        self.R1 = sb("R1", [128, 16384], F32)
        self.XN = sb("XN", [128, 16384], BF16)
        self.QB = sb("QB", [128, 16384], BF16)
        self.AO = sb("AO", [128, 16384], BF16)
        self.WS = sb("WS", [128, 8192], BF16)
        self.PT = sb("PTS", [128, 4096], BF16)
        self.wrouter = sb("wrouter", [128, DC * 128], BF16)
        self.mz = sb("mz", [128, 2048], BF16)
        self.consts = sb("consts", [128, 640], F32)
        self.cbf = sb("cbf", [128, 256], BF16)
        self.selt_s = sb("selt_s", [128, 3072], BF16)
        self.bqs = sb("bqs", [128, 8], F32)
        self.mask8 = sb("mask8", [128, 8 * 128], BF16)
        self.kvA = None

        self.small = self.QB[:, 8192:16384].bitcast(F32)
        for name in ["sPe", "sAct", "sDve", "sPool", "sCc"]:
            self.sems[name] = self.es.enter_context(nc.semaphore(name))
            self.cnt[name] = 0

    def S(self, name):
        if name not in self.sems:
            self.sems[name] = self.es.enter_context(self.nc.semaphore(name))
            self.cnt[name] = 0
        return self.sems[name]

    ENG = {"sPe", "sAct", "sDve", "sPool", "sCc"}

    def inc(self, inst, name, k=None):
        self.S(name)
        if k is None:
            k = 1 if name in self.ENG else 16
        self.cnt[name] += k
        inst.then_inc(self.sems[name], k)
        return self.cnt[name]

    def val(self, name):
        self.S(name)
        return self.cnt[name]

    def wait(self, eng, name, v=None):
        self.S(name)
        eng.wait_ge(self.sems[name], self.cnt[name] if v is None else v)

    def W(self, eng, name, v=None):
        self.S(name)
        eng.wait_ge(self.sems[name], self.cnt[name] if v is None else v)

    def add_dbg(self, name, shape, dtype=F32):
        ap = self.nc.dram_tensor(name, shape, dtype, kind="ExternalOutput").ap()
        self.dbg_outputs[name] = ap
        return ap

    AG2AR = 1152 * 1024
    AG2BR = 4096 * 1024

    def w_slab(self, which, k, w):
        base = {"wk": 0, "wv": 512, "wbr": 1024}[which]
        rank, j = (k, 0) if which == "wbr" else (k // 4, k % 4)
        off = rank * self.AG2AR + (base + j * 128) * 1024
        return bass.AP(self.ag2aout, off, [[w, 128], [1, w]])

    def w_window(self, which, oct_, kb):
        base = {"wq": 0, "wo": 2048}[which]
        off = kb * self.AG2BR + (base * 1024) + oct_ * 1024
        return bass.AP(self.ag2bout, off, [[4096, 128], [512 * 1024, 4], [1, 1024]])


    def emit_ln(self, blk, xsrc, dst, eps, extra_dst_dram=None, act_waits=(),
                dve_waits=(), pe_waits=()):
        small = self.small
        XB = lambda k: self.PT[:, (k % 4) * 1024 + 512:(k % 4) * 1024 + 1024]
        pSx, pSx2 = self.pE, self.pF

        cast_base = self.val("sDve")
        sq_base = self.val("sAct")
        pe_base = self.val("sPe")

        @blk.vector
        def _(v):
            for (s, val) in dve_waits:
                v.wait_ge(self.S(s), val)
            for k in range(DC):
                if k >= 4:
                    v.wait_ge(self.S("sPe"), pe_base + (k - 4) + 1)
                    v.wait_ge(self.S("sAct"), sq_base + (k - 4) + 1)
                self.inc(v.tensor_copy(XB(k), xsrc[:, k * 512:(k + 1) * 512]), "sDve")

        @blk.scalar
        def _(s):
            for (sm, val) in act_waits:
                s.wait_ge(self.S(sm), val)
            for k in range(DC):
                x2 = self.PT[:, (k % 4) * 1024:(k % 4) * 1024 + 512]
                s.wait_ge(self.S("sDve"), cast_base + k + 1)
                if k >= 4:
                    s.wait_ge(self.S("sPe"), pe_base + (k - 4) + 1)
                self.inc(s.activation(x2, XB(k), AF.Square), "sAct")

        @blk.tensor
        def _(t):
            for (sm, val) in pe_waits:
                t.wait_ge(self.S(sm), val)
            for k in range(DC):
                x2 = self.PT[:, (k % 4) * 1024:(k % 4) * 1024 + 512]
                t.wait_ge(self.S("sDve"), cast_base + k + 1)
                t.matmul(pSx[:, 0:512], self.onesb[:, 0:128], XB(k),
                         start=(k == 0), stop=(k == DC - 1))
                t.wait_ge(self.S("sAct"), sq_base + k + 1)
                self.inc(t.matmul(pSx2[:, 0:512], self.onesb[:, 0:128], x2,
                                  start=(k == 0), stop=(k == DC - 1)), "sPe")
        pe_done = self.val("sPe")

        SL = lambda i: small[:, i * 512:(i + 1) * 512]
        m_rep, var, sv, r_rep, A_rep = SL(0), SL(1), SL(2), SL(3), SL(4)

        @blk.vector
        def _(v):
            v.wait_ge(self.S("sPe"), pe_done)
            self.inc(v.tensor_scalar_mul(m_rep, pSx[:, 0:512], 1.0 / D), "sDve")
            self.wait(v, "sDve")
            self.inc(v.tensor_mul(SL(5), m_rep, m_rep), "sDve")
            self.wait(v, "sDve")
            self.inc(v.scalar_tensor_tensor(var, pSx2[:, 0:512], 1.0 / D, SL(5),
                                            ALU.mult, ALU.subtract), "sDve")
        var_done = self.val("sDve")

        @blk.scalar
        def _(s):
            s.wait_ge(self.S("sDve"), var_done)
            self.inc(s.activation(sv, var, AF.Ln, bias=eps), "sAct")
            self.wait(s, "sAct")
            self.inc(s.activation(r_rep, sv, AF.Exp, scale=-0.5), "sAct")
        r_done = self.val("sAct")

        @blk.vector
        def _(v):
            v.wait_ge(self.S("sAct"), r_done)
            v.wait_ge(self.S("sDve"), var_done)
            self.inc(v.scalar_tensor_tensor(A_rep, m_rep, -1.0, r_rep, ALU.mult, ALU.mult), "sDve")
            for k in range(DC):
                tmp = SL(5) if k % 2 == 0 else SL(6)
                self.wait(v, "sDve")
                self.inc(v.tensor_mul(tmp, xsrc[:, k * 512:(k + 1) * 512], r_rep), "sDve")
                self.wait(v, "sDve")
                self.inc(v.tensor_add(dst[:, k * 512:(k + 1) * 512], tmp, A_rep), "sDve")
        xn_done = self.val("sDve")

        if extra_dst_dram is not None:
            @blk.sync
            def _(sy):
                sy.wait_ge(self.S("sDve"), xn_done)
                dram_ap = bass.AP(extra_dst_dram.tensor, 0,
                                  [[512, 128], [65536, 32], [1, 512]])
                src_ap = bass.AP(dst, dst[0:1, 0:1].offset,
                                 [[dst[0:1, 0:1].ap[0][0], 128], [512, 32], [1, 512]])
                self.inc(sy.dma_start(dram_ap, src_ap), "cXnS")
        return xn_done

    def emit_coop_cast(self, blk):
        slabs = []
        for j in range(4):
            slabs.append((j * 128, 128, 1024, 0))
        for j in range(4):
            slabs.append((512 + j * 128, 128, 1024, 0))
        slabs.append((1024, 128, 1024, 0))
        for j in range(4):
            slabs.append((1152 + j * 512, 512, 4096, 1))
        for j in range(4):
            slabs.append((3200 + j * 512, 512, 4096, 1))
        inv = []

        w0_base = self.val("cAgW0")
        w1_base = self.val("cAgW1")

        na = sum(1 for s in slabs if s[3] == 0)

        @blk.gpsimd
        def _(g):
            for i, (r0, nr, w, grp) in enumerate(slabs):
                p = i % 2
                if i >= 2:
                    base = w0_base if p == 0 else w1_base
                    g.wait_ge(self.S(f"cAgW{p}"), base + 16 * (i // 2))
                srcw = bass.AP(self.wslabs.tensor, r0 * 1024, [[w, 128], [1, w]])
                inv.append(self.inc(g.dma_start(self.WS[:, p * 4096:p * 4096 + w], srcw),
                                    f"cAgC{p}"))
                if i == na - 1:
                    pass

        agaw = []
        @blk.scalar
        def _(s_):
            for i, (r0, nr, w, grp) in enumerate(slabs):
                p = i % 2
                s_.wait_ge(self.S(f"cAgC{p}"), inv[i])
                if grp == 0:
                    dst = bass.AP(self.ag2ain, r0 * 1024, [[w, 128], [1, w]])
                else:
                    dst = bass.AP(self.ag2bin, (r0 - 1152) * 1024, [[w, 128], [1, w]])
                self.inc(s_.dma_start(dst, self.WS[:, p * 4096:p * 4096 + w]), f"cAgW{p}")
                if i == na - 1:
                    agaw.append([("cAgW0", self.val("cAgW0")), ("cAgW1", self.val("cAgW1"))])
        ag2b_waits = [("cAgW0", self.val("cAgW0")), ("cAgW1", self.val("cAgW1"))]

        @blk.gpsimd
        def _(g):
            for (sm, vv) in agaw[0]:
                g.wait_ge(self.S(sm), vv)
            self.inc(g.collective_compute(
                "AllGather", ALU.bypass,
                replica_groups=[list(range(NCORE))],
                ins=[self.ag2ain.ap().opt()],
                outs=[self.ag2aout.ap().opt()]), "sCc")
            for (sm, vv) in ag2b_waits:
                g.wait_ge(self.S(sm), vv)
            self.inc(g.collective_compute(
                "AllGather", ALU.bypass,
                replica_groups=[list(range(NCORE))],
                ins=[self.ag2bin.ap().opt()],
                outs=[self.ag2bout.ap().opt()]), "sCc")

    def emit_weight_casts(self, blk, group, gate_pe=None):
        slabs = []
        for (src, dst, n, w) in group:
            for k in range(n):
                slabs.append((src, dst, k, w, src.tensor.name))

        @blk.gpsimd
        def _(g):
            if gate_pe is not None:
                g.wait_ge(self.S("sPe"), gate_pe)
            for i, (src, dst, k, w, nm) in enumerate(slabs):
                p = i % 2
                buf = self.WS[:, p * 4096:p * 4096 + w]
                if self.val(f"cWw{p}") > 0:
                    g.wait_ge(self.S(f"cWw{p}"), self.val(f"cWw{p}"))
                cv = self.inc(g.dma_start(buf, src[k * 128:(k + 1) * 128, 0:w]), f"cWc{p}")
                g.wait_ge(self.S(f"cWc{p}"), cv)
                self.inc(g.dma_start(dst[k, :, :], buf), f"cWw{p}")
                self.wdone[nm] = [("cWw0", self.val("cWw0")), ("cWw1", self.val("cWw1"))]

    def emit_setup(self, blk):
        nc = self.nc
        C = self.consts

        @blk.sync
        def _(sy):
            self.inc(sy.dma_start(C[:, 0:128], self.ident), "cLdC")
            self.inc(sy.dma_start(C[:, 128:144], self.ipat), "cLdC")
            self.inc(sy.dma_start(C[0:1, 150:151], bass.AP(self.gate.tensor, 0, [[1, 1], [1, 1]])), "cLdC")
            self.inc(sy.dma_start(C[0:16, 146:147], bass.AP(self.hres_b.tensor, 0, [[1, 16], [1, 1]])), "cLdC")
            self.inc(sy.dma_start(C[32:36, 146:147], bass.AP(self.hpre_b.tensor, 0, [[1, 4], [1, 1]])), "cLdC")
            self.inc(sy.dma_start(C[64:68, 146:147], bass.AP(self.hpost_b.tensor, 0, [[1, 4], [1, 1]])), "cLdC")
            for m in range(8):
                self.inc(sy.dma_start(self.bqs[:, m:m + 1],
                                      bass.AP(self.bqsel.tensor, m * 128, [[1, 128], [1, 1]])), "cLdC")
            for h in range(HKV):
                self.inc(sy.dma_start(C[:, 256 + h:257 + h],
                                      bass.AP(self.z_in.tensor, h * 128, [[1, 128], [1, 1]])), "cLdC")
            self.inc(sy.dma_start(self.selt_s[:, :], self.selt), "cLdC")
            self.inc(sy.dma_start(C[0:1, 384:408], self.baser), "cLdC")
            self.inc(sy.dma_start(self.R1[0:16, 0:4096], self.wres), "cLdC")
            self.inc(sy.dma_start(self.R1[32:36, 0:4096], self.wpre), "cLdC")
            self.inc(sy.dma_start(self.R1[64:68, 0:4096], self.wpost), "cLdC")
        ld_all = self.val("cLdC")

        if self.stage == "s0a":
            return
        @blk.gpsimd
        def _(g):
            for h in range(HKV):
                self.inc(g.dma_start(self.mz[:, h * 128:(h + 1) * 128], self.m_in[h]), "cMZ")
        self.m_done = self.val("cMZ")
        if self.stage == "s0b":
            return

        @blk.vector
        def _(v):
            v.memset(self.cbf[:, 0:128], 1.0)
            v.memset(C[0:1, 512:640], 1.0)
            v.memset(C[:, 148:149], LN_EPS)
            v.memset(C[:, 149:150], RMS_EPS)
            v.wait_ge(self.S("cLdC"), ld_all)
            v.tensor_copy(self.cbf[:, 128:256], C[:, 0:128])
            v.memzero(self.wrouter[:, :])
            for h in range(HKV):
                col = C[:, 256 + h:257 + h]
                colb = bass.AP(col.tensor, col.offset, [[col.ap[0][0], 128], [0, 128]])
                ins = v.tensor_copy(self.mz[:, 1024 + h * 128:1024 + (h + 1) * 128], colb)
            self.inc(ins, "sDve")
        setup_dve = self.val("sDve")
        self.onesb = self.cbf[:, 0:128]
        if self.stage == "s0c":
            self.identb = self.cbf[:, 128:256]
            self.ident_f = C[:, 0:128]
            return
        self.identb = self.cbf[:, 128:256]
        self.ident_f = C[:, 0:128]
        self.epsln = C[:, 148:149]
        self.epsrms = C[:, 149:150]

        pe_base = self.val("sPe")
        rt_evac0 = self.val("sDve")
        with (nc.psum_tensor("prtA", [128, 128], F32) as prtA,
              nc.psum_tensor("prtB", [128, 128], F32) as prtB):
            prts = [prtA, prtB]

            @blk.tensor
            def _(t):
                t.wait_ge(self.S("sDve"), setup_dve)
                for k in range(DC):
                    if k >= 2:
                        t.wait_ge(self.S("sDve"), rt_evac0 + 3 * (k - 2) + 3)
                    self.inc(t.transpose(prts[k % 2][:, :],
                                         self.R1[:, k * 128:(k + 1) * 128],
                                         self.ident_f), "sPe")

            @blk.vector
            def _(v):
                for k in range(DC):
                    v.wait_ge(self.S("sPe"), pe_base + k + 1)
                    pt = prts[k % 2]
                    v.tensor_copy(self.wrouter[:, k * 128:k * 128 + 16], pt[:, 0:16])
                    v.tensor_copy(self.wrouter[:, k * 128 + 32:k * 128 + 36], pt[:, 32:36])
                    self.inc(v.tensor_copy(self.wrouter[:, k * 128 + 64:k * 128 + 68], pt[:, 64:68]), "sDve", 3)
        self.router_done = self.val("sDve")

        with nc.psum_tensor("pst", [128, 512], F32) as pst:
            g0 = self.val("sDve")
            @blk.scalar
            def _(s):
                s.wait_ge(self.S("cLdC"), ld_all)
                self.inc(s.activation(C[0:1, 151:152], C[0:1, 150:151], AF.Exp, scale=-1.0), "sAct")
            ge = self.val("sAct")
            @blk.vector
            def _(v):
                v.wait_ge(self.S("sAct"), ge)
                self.inc(v.tensor_scalar_add(C[0:1, 152:153], C[0:1, 151:152], 1.0), "sDve")
                self.wait(v, "sDve")
                self.inc(v.reciprocal(C[0:1, 153:154], C[0:1, 152:153]), "sDve")
                self.wait(v, "sDve")
                self.inc(v.tensor_scalar(C[0:1, 154:155], C[0:1, 153:154], -1.0, 1.0,
                                         ALU.mult, ALU.add), "sDve")
            gdve = self.val("sDve")
            @blk.tensor
            def _(t):
                t.wait_ge(self.S("sDve"), gdve)
                t.matmul(pst[:, 0:1], self.onesb[0:1, 0:128].bitcast(BF16) if False else C[0:1, 512:640], C[0:1, 153:154], start=True, stop=True)
                t.matmul(pst[:, 4:5], C[0:1, 512:640], C[0:1, 154:155], start=True, stop=True)
                self.inc(t.matmul(pst[:, 8:32], C[0:1, 512:640], C[0:1, 384:408], start=True, stop=True), "sPe")
            gpe = self.val("sPe")
            @blk.vector
            def _(v):
                v.wait_ge(self.S("sPe"), gpe)
                v.tensor_copy(C[:, 144:145], pst[:, 0:1])
                v.tensor_copy(C[:, 145:146], pst[:, 4:5])
                v.tensor_copy(C[:, 416:440], pst[:, 8:32])
                self.inc(v.tensor_scalar_mul(C[:, 440:456], pst[:, 8:24], 10.0), "sDve")
            self.setup2_done = self.val("sDve")

    def emit_wprep(self, blk):
        nc = self.nc
        agin = self.agin.ap()
        with (nc.psum_tensor("pwpA", [128, 128], F32) as pwpA,
              nc.psum_tensor("pwpB", [128, 128], F32) as pwpB):
            pwps = [pwpA, pwpB]
            for kk in range(4):
                pe0 = self.val("sPe")
                dv0 = self.val("sDve")
                ldv = []

                @blk.sync
                def _(sy, kk=kk, pe0=pe0, ldv=ldv):
                    for mm in range(8):
                        if mm >= 4:
                            sy.wait_ge(self.S("sPe"), pe0 + mm - 4 + 1)
                        ldv.append(self.inc(sy.dma_start(
                            self.R1[:, 8192 + (mm % 4) * 128:8192 + (mm % 4) * 128 + 128],
                            self.wqsel_cols[mm * 128:(mm + 1) * 128, kk * 128:(kk + 1) * 128]),
                            f"cWq{mm % 4}"))

                @blk.tensor
                def _(t, kk=kk, dv0=dv0, ldv=ldv):
                    if kk == 0:
                        t.wait_ge(self.S("sDve"), self.setup2_done)
                    for mm in range(8):
                        t.wait_ge(self.S(f"cWq{mm % 4}"), ldv[mm])
                        if kk > 0 and mm == 0:
                            t.wait_ge(self.S("sDve"), dv0)
                        if mm >= 2:
                            t.wait_ge(self.S("sDve"), dv0 + mm - 2 + 1)
                        self.inc(t.transpose(pwps[mm % 2][:, :],
                                             self.R1[:, 8192 + (mm % 4) * 128:8192 + (mm % 4) * 128 + 128],
                                             self.ident_f), "sPe")

                @blk.vector
                def _(v, kk=kk, pe0=pe0):
                    if kk > 0:
                        v.wait_ge(self.S("cPrS"), self.val("cPrS"))
                    for mm in range(8):
                        v.wait_ge(self.S("sPe"), pe0 + mm + 1)
                        self.inc(v.tensor_copy(self.PT[:, mm * 128:(mm + 1) * 128],
                                               pwps[mm % 2][:, :]), "sDve")
                dv_done = self.val("sDve")

                @blk.sync
                def _(sy, kk=kk, dv_done=dv_done):
                    sy.wait_ge(self.S("sDve"), dv_done)
                    self.inc(sy.dma_start(agin[2048 + kk * 128:2048 + (kk + 1) * 128, :],
                                          self.PT[:, 0:1024]), "cPrS")

            for mc in range(4):
                pe0 = self.val("sPe")

                @blk.sync
                def _(sy, mc=mc):
                    self.inc(sy.dma_start(self.R1[:, 8192:8192 + 4096],
                                          self.wob_rows[mc * 128:(mc + 1) * 128, :]), "cWo")

                ld_done = self.val("cWo")

                @blk.vector
                def _(v, mc=mc, ld_done=ld_done, pe0=pe0):
                    v.wait_ge(self.S("cWo"), ld_done)
                    if mc >= 1:
                        v.wait_ge(self.S("sPe"), pe0)
                        v.wait_ge(self.S("cPoS"), self.val("cPoS"))
                    src = self.R1[:, 8192:8192 + 4096]
                    vw = lambda g: bass.AP(src.tensor, src.offset + g * 128,
                                           [[src.ap[0][0], 128], [512, 8], [1, 128]])
                    fold = self.R1[:, 12288:12288 + 1024]
                    self.inc(v.tensor_add(fold, vw(0), vw(1)), "sDve")
                    self.wait(v, "sDve")
                    self.inc(v.tensor_add(fold, fold, vw(2)), "sDve")
                    self.wait(v, "sDve")
                    self.inc(v.tensor_add(fold, fold, vw(3)), "sDve")
                fold_done = self.val("sDve")

                @blk.tensor
                def _(t, mc=mc, fold_done=fold_done):
                    t.wait_ge(self.S("sDve"), fold_done)
                    fold = self.R1[:, 12288:12288 + 1024]
                    for jc in range(8):
                        if jc >= 2:
                            t.wait_ge(self.S("sDve"), fold_done + jc - 2 + 1)
                        self.inc(t.transpose(pwps[jc % 2][:, :],
                                             fold[:, jc * 128:(jc + 1) * 128],
                                             self.ident_f), "sPe")

                @blk.vector
                def _(v, mc=mc, pe0=pe0):
                    for jc in range(8):
                        v.wait_ge(self.S("sPe"), pe0 + jc + 1)
                        self.inc(v.tensor_copy(self.PT[:, 1024 + jc * 128:1024 + (jc + 1) * 128],
                                               pwps[jc % 2][:, :]), "sDve")
                ev_done = self.val("sDve")

                @blk.sync
                def _(sy, mc=mc, ev_done=ev_done):
                    sy.wait_ge(self.S("sDve"), ev_done)
                    base = 2560 * 1024 + mc * 128
                    dst = bass.AP(agin.tensor, base, [[512, 128], [65536, 8], [1, 128]])
                    srcp = bass.AP(self.PT, self.PT[0:1, 1024:1025].offset,
                                   [[self.PT[0:1, 0:1].ap[0][0], 128], [128, 8], [1, 128]])
                    self.inc(sy.dma_start(dst, srcp), "cPoS")

    def emit_kv_proj(self, blk, half, xnsrc, xn_wait=None):
        agin = self.agin.ap()
        KVB = 24576
        R1b = self.R1[:, :].bitcast(BF16)
        for pr in range(2):
            s0, s1 = half * 4 + 2 * pr, half * 4 + 2 * pr + 1
            tb0, tb1 = 2 * pr, 2 * pr + 1
            pe0 = self.val("sPe")
            ev0 = self.val("sDve")
            prev_pe = self._kv_prev_pe
            slabv = []

            @blk.sync
            def _(sy, pr=pr, half=half, pe0=pe0, slabv=slabv, prev_pe=prev_pe):
                if pr == 0 and half == 0:
                    sy.wait_ge(self.S("sCc"), 1)
                if xn_wait is not None and pr == 0:
                    sy.wait_ge(self.S("sDve"), xn_wait)
                if prev_pe is not None:
                    sy.wait_ge(self.S("sPe"), prev_pe)
                for k in range(DC):
                    if k >= 3:
                        sy.wait_ge(self.S("sPe"), pe0 + (k - 3) + 1)
                    sl = KVB + (k % 3) * 2048
                    self.inc(sy.dma_start(R1b[:, sl:sl + 1024], self.w_slab("wk", k, 1024)), f"cSl{k % 3}")
                    slabv.append(self.inc(sy.dma_start(R1b[:, sl + 1024:sl + 2048], self.w_slab("wv", k, 1024)), f"cSl{k % 3}"))

            @blk.tensor
            def _(t, pr=pr, tb0=tb0, tb1=tb1, ev0=ev0, slabv=slabv):
                if xn_wait is not None:
                    t.wait_ge(self.S("sDve"), xn_wait)
                if pr == 1:
                    t.wait_ge(self.S("sDve"), ev0)
                for k in range(DC):
                    t.wait_ge(self.S(f"cSl{k % 3}"), slabv[k])
                    sl = KVB + (k % 3) * 2048
                    st, sp = (k == 0), (k == DC - 1)
                    xt0 = xnsrc[:, k * 512 + tb0 * 128:k * 512 + tb0 * 128 + 128]
                    xt1 = xnsrc[:, k * 512 + tb1 * 128:k * 512 + tb1 * 128 + 128]
                    t.matmul(self.pAB[:, 0:512], xt0, R1b[:, sl:sl + 512], start=st, stop=sp)
                    t.matmul(self.pAB[:, 512:1024], xt0, R1b[:, sl + 512:sl + 1024], start=st, stop=sp)
                    t.matmul(self.pCD[:, 0:512], xt0, R1b[:, sl + 1024:sl + 1536], start=st, stop=sp)
                    t.matmul(self.pCD[:, 512:1024], xt0, R1b[:, sl + 1536:sl + 2048], start=st, stop=sp)
                    t.matmul(self.pEF[:, 0:512], xt1, R1b[:, sl:sl + 512], start=st, stop=sp)
                    t.matmul(self.pEF[:, 512:1024], xt1, R1b[:, sl + 512:sl + 1024], start=st, stop=sp)
                    t.matmul(self.pGH[:, 0:512], xt1, R1b[:, sl + 1024:sl + 1536], start=st, stop=sp)
                    ins = t.matmul(self.pGH[:, 512:1024], xt1, R1b[:, sl + 1536:sl + 2048], start=st, stop=sp)
                    self.inc(ins, "sPe")
            self._kv_prev_pe = self.val("sPe")
            pe_done = self.val("sPe")

            @blk.vector
            def _(v, s0=s0, s1=s1, pe_done=pe_done):
                v.wait_ge(self.S("sPe"), pe_done)
                v.wait_ge(self.S("cVs0"), self.val("cVs0"))
                v.wait_ge(self.S("cVs1"), self.val("cVs1"))
                v.tensor_copy(self.QB[:, s0 * 1024:(s0 + 1) * 1024], self.pAB[:, :])
                self.inc(v.tensor_copy(self.PT[:, 2048:3072], self.pCD[:, :]), "sDve")
                v.tensor_copy(self.QB[:, s1 * 1024:(s1 + 1) * 1024], self.pEF[:, :])
                self.inc(v.tensor_copy(self.PT[:, 3072:4096], self.pGH[:, :]), "sDve")

            dv_now = self.val("sDve")
            @blk.sync
            def _(sy, s0=s0, s1=s1, dv_now=dv_now):
                sy.wait_ge(self.S("sDve"), dv_now - 1)
                self.inc(sy.dma_start(agin[1024 + s0 * 128:1024 + (s0 + 1) * 128, :],
                                      self.PT[:, 2048:3072]), "cVs0")
                sy.wait_ge(self.S("sDve"), dv_now)
                self.inc(sy.dma_start(agin[1024 + s1 * 128:1024 + (s1 + 1) * 128, :],
                                      self.PT[:, 3072:4096]), "cVs1")

    def emit_k_transpose(self, blk):
        nc = self.nc
        agin = self.agin.ap()
        with nc.psum_tensor("ptrb", [128, 1024], BF16) as ptrb:
            pe0 = self.val("sPe")
            dv0 = self.val("sDve")
            stv = []

            @blk.tensor
            def _(t):
                for s in range(NSLOT):
                    if s == 0:
                        t.wait_ge(self.S("sDve"), dv0)
                    if s >= 1:
                        t.wait_ge(self.S("sDve"), dv0 + s)
                    for h in range(HKV):
                        self.inc(t.transpose(ptrb[:, h * 128:(h + 1) * 128],
                                             self.QB[:, s * 1024 + h * 128:s * 1024 + (h + 1) * 128],
                                             self.identb), "sPe")

            @blk.sync
            def _(sy):
                for s in range(NSLOT):
                    sy.wait_ge(self.S("sDve"), dv0 + s + 1)
                    stv.append(self.inc(sy.dma_start(agin[s * 128:(s + 1) * 128, :],
                                                     self.PT[:, (s % 2) * 1024:(s % 2) * 1024 + 1024]),
                                        f"cKb{s % 2}"))

            @blk.vector
            def _(v):
                for s in range(NSLOT):
                    v.wait_ge(self.S("sPe"), pe0 + 8 * (s + 1))
                    if s >= 2:
                        v.wait_ge(self.S(f"cKb{s % 2}"), stv[s - 2])
                    self.inc(v.tensor_copy(self.PT[:, (s % 2) * 1024:(s % 2) * 1024 + 1024],
                                           ptrb[:, :]), "sDve")

    def emit_qproj(self, blk, xnsrc, qp, xn_wait=None, extra_pe_waits=()):
        pe0 = self.val("sPe")
        dv0 = self.val("sDve")
        wv = []

        @blk.sync
        def _(sy):
            sy.wait_ge(self.S("sCc"), 2)
            i = 0
            for oct_ in range(4):
                for kb in range(8):
                    if i >= 2:
                        sy.wait_ge(self.S("sPe"), pe0 + (i - 1))
                    srcw = self.w_window("wq", oct_, kb)
                    wv.append(self.inc(sy.dma_start(
                        self.WS[:, (i % 2) * 4096:(i % 2) * 4096 + 4096], srcw), f"cQw{i % 2}"))
                    i += 1

        @blk.tensor
        def _(t):
            if xn_wait is not None:
                t.wait_ge(self.S("sDve"), xn_wait)
            for (sm, vvv) in extra_pe_waits:
                t.wait_ge(self.S(sm), vvv)
            i = 0
            for oct_ in range(4):
                if oct_ >= 1:
                    t.wait_ge(self.S("sDve"), dv0 + 8 * oct_)
                for kb in range(8):
                    t.wait_ge(self.S(f"cQw{i % 2}"), wv[i])
                    base = (i % 2) * 4096
                    for m in range(8):
                        for k in range(4):
                            st = (kb == 0 and k == 0)
                            sp = (kb == 7 and k == 3)
                            ins = t.matmul(qp[:, m * 512:(m + 1) * 512],
                                           self.WS[:, base + k * 1024 + m * 128:base + k * 1024 + (m + 1) * 128],
                                           xnsrc[:, (kb * 4 + k) * 512:(kb * 4 + k) * 512 + 512],
                                           start=st, stop=sp)
                    self.inc(ins, "sPe")
                    i += 1

        @blk.vector
        def _(v):
            for oct_ in range(4):
                v.wait_ge(self.S("sPe"), pe0 + 8 * (oct_ + 1))
                for m in range(8):
                    self.inc(v.tensor_copy(
                        self.QB[:, (oct_ * 8 + m) * 512:(oct_ * 8 + m) * 512 + 512],
                        qp[:, m * 512:(m + 1) * 512]), "sDve")

    def emit_prologue(self, blk):
        nc = self.nc
        self.emit_coop_cast(blk)
        self.emit_setup(blk)
        if self.stage in ("s1", "s0a", "s0b", "s0c"):
            return
        self.emit_wprep(blk)
        if self.stage == "s2":
            return

        wprep_pe = self.val("sPe")
        wprep_dve = self.val("sDve")

        with (nc.psum_tensor("pAB", [128, 1024], F32) as pAB,
              nc.psum_tensor("pCD", [128, 1024], F32) as pCD,
              nc.psum_tensor("pEF", [128, 1024], F32) as pEF,
              nc.psum_tensor("pGH", [128, 1024], F32) as pGH):
            self.pAB, self.pCD, self.pEF, self.pGH = pAB, pCD, pEF, pGH
            self.pE, self.pF = pEF[:, 0:512], pEF[:, 512:1024]

            @blk.sync
            def _(sy):
                sy.wait_ge(self.S("sPe"), wprep_pe)
                sy.wait_ge(self.S("sDve"), wprep_dve)
                src = bass.AP(self.xin.tensor, 0, [[1024, 128], [131072, 32], [1, 512]])
                self.inc(sy.dma_start(self.R1[:, :], src), "cX0")
            x0_ld = self.val("cX0")

            @blk.vector
            def _(v):
                v.wait_ge(self.S("cX0"), x0_ld)
            pw = [("cPrS", self.val("cPrS")), ("cPoS", self.val("cPoS"))]
            self.emit_ln(blk, self.R1, self.AO, self.epsln,
                         act_waits=pw, dve_waits=pw)
            xn0b0_done = self.val("sDve")
            if self.stage == "s3":
                return
            self.emit_kv_proj(blk, 0, self.AO, xn_wait=xn0b0_done)
            if self.stage == "s4":
                return
            kvb0_evac = self.val("sDve")
            kv_waits = [("cVs0", self.val("cVs0")), ("cVs1", self.val("cVs1"))]

            kv0_pe = self._kv_prev_pe
            @blk.sync
            def _(sy):
                sy.wait_ge(self.S("sDve"), xn0b0_done)
                sy.wait_ge(self.S("sPe"), kv0_pe)
                src = bass.AP(self.xin.tensor, 512, [[1024, 128], [131072, 32], [1, 512]])
                self.inc(sy.dma_start(self.R1[:, :], src), "cX0")
            x1_ld = self.val("cX0")

            @blk.vector
            def _(v):
                v.wait_ge(self.S("cX0"), x1_ld)
            self.emit_ln(blk, self.R1, self.XN, self.epsln, extra_dst_dram=self.xn0b1,
                         act_waits=kv_waits, dve_waits=kv_waits,
                         pe_waits=[("sDve", kvb0_evac)])

            xn0b1_done = self.val("sDve")
            self.emit_kv_proj(blk, 1, self.XN, xn_wait=xn0b1_done)

        if self.stage == "s5":
            return
        self.emit_k_transpose(blk)
        ktr_dve = self.val("sDve")
        if self.stage == "s6":
            return

        if self.stage == "s6b":
            return
        ag_waits = [(s, self.val(s)) for s in ["cPrS", "cPoS", "cVs0", "cVs1", "cKb0", "cKb1"]]
        @blk.gpsimd
        def _(g):
            for (sm, vv) in ag_waits:
                g.wait_ge(self.S(sm), vv)
            self.inc(g.collective_compute(
                "AllGather", ALU.bypass,
                replica_groups=[list(range(NCORE))],
                ins=[self.agin.ap().opt()],
                outs=[self.agout.ap().opt()]), "sCc")
        if self.stage == "s7":
            return
        with nc.psum_tensor("qp", [128, 4096], F32) as qp:
            self.emit_qproj(blk, self.AO, qp, xn_wait=ktr_dve)

    def emit_debug_stores(self, stores):
        nc = self.nc
        self.dbg_outputs_names = [s[0] for s in stores]
        with nc.Block() as blk:
            @blk.sync
            def _(sy):
                for (name, region, shape, dt) in stores:
                    ap = self.nc.dram_tensor(name, shape, dt, kind="ExternalOutput").ap()
                    self.inc(sy.dma_start(ap, region), "cDbg")
                sy.wait_ge(self.S("cDbg"), self.val("cDbg"))

    def build(self):
        nc = self.nc
        if self.stage == "full":
            nc2 = self.build_full()
            self.dbg_outputs_names = ["yout"]
            self.es.close()
            return nc2
        with nc.Block() as blk:
            self.emit_prologue(blk)
        if self.stage.startswith("s"):
            self.emit_debug_stores([
                ("d_mark", self.consts[:, 0:128], [128, 128], F32),
            ])
            self.es.close()
            return nc
        if self.stage == "xn0":
            self.emit_debug_stores([
                ("d_xn0b0", self.AO[:, :], [128, 16384], BF16),
                ("d_xn0b1", self.XN[:, :], [128, 16384], BF16),
            ])
            self.es.close()
            return nc
        if self.stage == "att":
            with nc.Block() as blk2:
                self.emit_attention(blk2, 0, q_ready=self.val("sDve"), wait_cc=True)
            self.emit_debug_stores([
                ("d_ao", self.AO[:, :], [128, 16384], BF16),
            ])
            self.es.close()
            return nc
        if self.stage == "kv":
            with nc.Block() as blk2:
                @blk2.gpsimd
                def _(g):
                    g.wait_ge(self.S("sCc"), self.val("sCc"))
            self.emit_debug_stores([
                ("d_agin", self.agin.ap(), [3072, 1024], BF16),
                ("d_q", self.QB[:, :], [128, 16384], BF16),
            ])
            self.es.close()
            return nc
        if self.stage == "full":
            nc2 = self.build_full()
            self.dbg_outputs_names = ["yout"]
            self.es.close()
            return nc2
        if self.stage in ("x1", "x2", "x3", "x4"):
            with nc.Block() as blk:
                pass
            self.es.close()
            return nc
        raise NotImplementedError(self.stage)

    def emit_attention(self, blk, half, q_ready, wait_cc=False, r1_wait=None):
        _sfx = str(half)
        nc = self.nc
        R1b = self.R1[:, :].bitcast(BF16)
        XNf = self.XN[:, :].bitcast(F32)
        AVACC = XNf[:, 0:4096]
        DEN = XNf[0:1, 4096:8192]
        PTf = self.PT[:, :].bitcast(F32)

        xnst = self.val("cXnS")
        kv_ctr = 0
        kv_gate = [None, None]
        ebank_prev = None
        ev_last = None
        last_bc_pe = [None, None]
        last_bc_tt = [None, None]
        first_pe_wait = q_ready
        RSTRIDE = 3072 * 1024

        with (nc.psum_tensor("pSCa" + _sfx, [128, 1024], F32) as pSCa,
              nc.psum_tensor("pSCb" + _sfx, [128, 1024], F32) as pSCb,
              nc.psum_tensor("pav" + _sfx, [128, 512], F32) as pav,
              nc.psum_tensor("pden" + _sfx, [128, 512], F32) as pden,
              nc.psum_tensor("pbc" + _sfx, [128, 512], F32) as pbc,
              nc.psum_tensor("pbcB" + _sfx, [128, 512], F32) as pbcB):
            pSC = [pSCa, pSCb]
            pbc2 = [pbc, pbcB]

            def emit_av(t, prev, ev2, jlist):
                for j in jlist:
                    jp = j // 2
                    if prev["lastg"]:
                        t.wait_ge(self.S("sDve"),
                                  max(prev["msk_base"] + j + 1,
                                      ev2 if (j == 0 and ev2 is not None) else 0))
                    else:
                        t.wait_ge(self.S("sAct"), prev["exp_base"] + jp + 1)
                        if j == 0 and ev2 is not None:
                            t.wait_ge(self.S("sDve"), ev2)
                    ro = j if prev["b"] == 0 else 7 - j
                    lhsT = R1b[:, prev["vbase"] + ro * 1024 + prev["h"] * 128:
                               prev["vbase"] + ro * 1024 + (prev["h"] + 1) * 128]
                    pt_t = self.PT[:, j * 512:(j + 1) * 512]
                    t.matmul(pav[:, :], lhsT, pt_t, start=(j == 0), stop=(j == 7))
                    self.inc(t.matmul(pden[:, :], self.onesb[:, 0:128], pt_t,
                                      start=(j == 0), stop=(j == 7)), "sPe")

            def emit_evac(prev, av_end):
                @blk.vector
                def _(v, prev=prev, av_end=av_end):
                    if prev["first_evac"]:
                        v.wait_ge(self.S("cXnS"), xnst)
                    v.wait_ge(self.S("sPe"), av_end)
                    dst = AVACC[:, prev["h"] * 512:(prev["h"] + 1) * 512]
                    dden = DEN[0:1, prev["h"] * 512:(prev["h"] + 1) * 512]
                    self.wait(v, "sDve")
                    if prev["g"] == 0:
                        self.inc(v.tensor_copy(dst, pav[:, :]), "sDve")
                        self.inc(v.tensor_copy(dden, pden[0:1, :]), "sDve")
                    else:
                        self.inc(v.tensor_add(dst, dst, pav[:, :]), "sDve")
                        self.inc(v.tensor_add(dden, dden, pden[0:1, :]), "sDve")
                return self.val("sDve")

            for si in range(4):
                s = half * 4 + si
                b = half
                ngrp = NGRP[s]

                @blk.sync
                def _(sy, s=s, si=si):
                    if wait_cc and si == 0:
                        sy.wait_ge(self.S("sCc"), 3)
                    sy.wait_ge(self.S("sDve"), self.val("sDve"))
                    self.inc(sy.dma_start(
                        self.mask8[:, :],
                        bass.AP(self.maskt.tensor, s * 131072, [[1024, 128], [1, 1024]])),
                        "cMsk")
                mask_ld = self.val("cMsk")

                prev = None
                for g in range(ngrp):
                    p = kv_ctr % 2
                    kv_ctr += 1
                    kbase = p * 16384
                    vbase = p * 16384 + 8192
                    lastg = (g == ngrp - 1)
                    slotp = g if b == 0 else 4 + g

                    gate = kv_gate[p]
                    first_kv = (si == 0 and g == 0)
                    @blk.sync
                    def _(sy, kbase=kbase, vbase=vbase, slotp=slotp, gate=gate,
                          first_kv=first_kv):
                        if first_kv and r1_wait is not None:
                            sy.wait_ge(self.S(r1_wait[0]), r1_wait[1])
                        if gate is not None:
                            sy.wait_ge(self.S("sPe"), gate)
                        kdst = R1b[:, kbase:kbase + 8192].rearrange("p (r c) -> p r c", r=8)
                        vdst = R1b[:, vbase:vbase + 8192].rearrange("p (r c) -> p r c", r=8)
                        ksrc = bass.AP(self.agout, slotp * 128 * 1024,
                                       [[1024, 128], [RSTRIDE, 8], [1, 1024]])
                        vsrc = bass.AP(self.agout, 1024 * 1024 + slotp * 128 * 1024,
                                       [[1024, 128], [RSTRIDE, 8], [1, 1024]])
                        self.inc(sy.dma_start(kdst, ksrc), f"cKV{p}")
                        self.inc(sy.dma_start(vdst, vsrc), f"cKV{p}")
                    kv_ld = self.val(f"cKV{p}")

                    for h in range(HKV):
                        pe0 = self.val("sPe")
                        ebu = self.val("sAct")

                        @blk.tensor
                        def _(t, si=si, b=b, h=h, kbase=kbase, kv_ld=kv_ld, p=p,
                              prev=prev, ev2=ev_last, ebu=ebu, ebp=ebank_prev,
                              fpw=first_pe_wait, h0=(h == 0)):
                            if fpw is not None:
                                t.wait_ge(self.S("sDve"), fpw)
                            if h0:
                                t.wait_ge(self.S(f"cKV{p}"), kv_ld)

                            def sc_pair(jp):
                                bank = jp % 2
                                if jp >= 2:
                                    t.wait_ge(self.S("sAct"), ebu + (jp - 2) + 1)
                                elif ebp is not None:
                                    t.wait_ge(self.S("sAct"), ebp + 3 + jp)
                                for jj in range(2):
                                    j = 2 * jp + jj
                                    ro = j if b == 0 else 7 - j
                                    lhsT = R1b[:, kbase + ro * 1024 + h * 128:
                                               kbase + ro * 1024 + (h + 1) * 128]
                                    rhs = bass.AP(self.QB, (4 * h) * 512 + si * 128,
                                                  [[self.QB[0:1, 0:1].ap[0][0], 128], [512, 4], [1, 128]])
                                    self.inc(t.matmul(pSC[bank][:, jj * 512:(jj + 1) * 512],
                                                      lhsT, rhs, start=True, stop=True), "sPe")

                            if prev is None:
                                for jp in range(4):
                                    sc_pair(jp)
                            else:
                                emit_av(t, prev, ev2, [0, 1, 2, 3, 4, 5])
                                sc_pair(0)
                                emit_av(t, prev, ev2, [6, 7])
                                sc_pair(1)
                                sc_pair(2)
                                sc_pair(3)
                        first_pe_wait = None
                        if prev is not None and prev["h"] == HKV - 1:
                            kv_gate[prev["p"]] = pe0 + 10

                        @blk.scalar
                        def _(s_, pe0=pe0, gates=((8, 12, 14, 16) if prev is not None
                                                  else (2, 4, 6, 8))):
                            for jp in range(4):
                                s_.wait_ge(self.S("sPe"), pe0 + gates[jp])
                                self.inc(s_.activation(self.PT[:, jp * 1024:(jp + 1) * 1024],
                                                       pSC[jp % 2][:, :], AF.Exp,
                                                       scale=ATT_SCALE), "sAct")
                        ebank_prev = ebu

                        msk_base = self.val("sDve")
                        if lastg:
                            @blk.vector
                            def _(v, ebu=ebu, mask_ld=mask_ld):
                                v.wait_ge(self.S("cMsk"), mask_ld)
                                for j in range(8):
                                    jp = j // 2
                                    v.wait_ge(self.S("sAct"), ebu + jp + 1)
                                    pt_t = self.PT[:, j * 512:(j + 1) * 512]
                                    mview = bass.AP(self.mask8,
                                                    self.mask8[0:1, 0:1].offset + j * 128,
                                                    [[self.mask8[0:1, 0:1].ap[0][0], 128], [0, 4], [1, 128]])
                                    self.inc(v.tensor_mul(pt_t, pt_t, mview), "sDve")

                        if prev is not None:
                            ev_last = emit_evac(prev, pe0 + 10)

                        prev = {"h": h, "g": g, "p": p, "b": b, "vbase": vbase,
                                "lastg": lastg, "msk_base": msk_base, "exp_base": ebu,
                                "first_evac": (si == 0 and g == 0 and h == 0)}

                fl0 = self.val("sPe")
                @blk.tensor
                def _(t, prev=prev, ev2=ev_last):
                    emit_av(t, prev, ev2, [0, 1, 2, 3, 4, 5, 6, 7])
                kv_gate[prev["p"]] = fl0 + 8
                ev_last = emit_evac(prev, fl0 + 8)

                for h in range(HKV):
                    q = h % 2
                    @blk.vector
                    def _(v, h=h, q=q, lbp=last_bc_pe[q]):
                        if lbp is not None:
                            v.wait_ge(self.S("sPe"), lbp)
                        self.wait(v, "sDve")
                        self.inc(v.reciprocal(PTf[0:1, q * 512:q * 512 + 512],
                                              DEN[0:1, h * 512:(h + 1) * 512]), "sDve")
                    rc_done = self.val("sDve")

                    @blk.tensor
                    def _(t, q=q, rc_done=rc_done, lbt=last_bc_tt[q]):
                        t.wait_ge(self.S("sDve"), rc_done)
                        if lbt is not None:
                            t.wait_ge(self.S("sDve"), lbt)
                        self.inc(t.matmul(pbc2[q][:, :], self.consts[0:1, 512:640],
                                          PTf[0:1, q * 512:q * 512 + 512],
                                          start=True, stop=True), "sPe")
                    last_bc_pe[q] = self.val("sPe")

                    @blk.vector
                    def _(v, h=h, si=si, q=q, lbp=last_bc_pe[q]):
                        v.wait_ge(self.S("sPe"), lbp)
                        out = bass.AP(self.AO, (4 * h) * 512 + si * 128,
                                      [[self.AO[0:1, 0:1].ap[0][0], 128], [512, 4], [1, 128]])
                        self.inc(v.tensor_mul(out, AVACC[:, h * 512:(h + 1) * 512],
                                              pbc2[q][:, :]), "sDve")
                    last_bc_tt[q] = self.val("sDve")
        return self.val("sDve")

    def emit_oproj(self, blk, half, pp, ao_ready, r1_free_pe):
        wv = []
        pe0 = self.val("sPe")
        dv0 = self.val("sDve")

        @blk.sync
        def _(sy):
            sy.wait_ge(self.S("sPe"), r1_free_pe)
            src = bass.AP(self.xin.tensor, half * 512,
                          [[1024, 128], [131072, 32], [1, 512]])
            self.inc(sy.dma_start(self.R1[:, :], src), "cX0")
            i = 0
            for oct_ in range(4):
                for kb in range(8):
                    if i >= 2:
                        sy.wait_ge(self.S("sPe"), pe0 + (i - 1))
                    srcw = self.w_window("wo", oct_, kb)
                    wv.append(self.inc(sy.dma_start(
                        self.WS[:, (i % 2) * 4096:(i % 2) * 4096 + 4096], srcw), f"cQw{i % 2}"))
                    i += 1
        x0_ld = self.val("cX0")

        @blk.tensor
        def _(t):
            t.wait_ge(self.S("sDve"), ao_ready)
            i = 0
            for oct_ in range(4):
                if oct_ >= 1:
                    t.wait_ge(self.S("sDve"), dv0 + 8 * oct_)
                for kb in range(8):
                    t.wait_ge(self.S(f"cQw{i % 2}"), wv[i])
                    base = (i % 2) * 4096
                    for m in range(8):
                        for k in range(4):
                            st = (kb == 0 and k == 0)
                            sp = (kb == 7 and k == 3)
                            ins = t.matmul(pp[:, m * 512:(m + 1) * 512],
                                           self.WS[:, base + k * 1024 + m * 128:base + k * 1024 + (m + 1) * 128],
                                           self.AO[:, (kb * 4 + k) * 512:(kb * 4 + k) * 512 + 512],
                                           start=st, stop=sp)
                    self.inc(ins, "sPe")
                    i += 1

        @blk.vector
        def _(v):
            v.wait_ge(self.S("cX0"), x0_ld)
            for oct_ in range(4):
                v.wait_ge(self.S("sPe"), pe0 + 8 * (oct_ + 1))
                for m in range(8):
                    mg = oct_ * 8 + m
                    r1c = self.R1[:, mg * 512:(mg + 1) * 512]
                    self.inc(v.tensor_add(r1c, r1c, pp[:, m * 512:(m + 1) * 512]), "sDve")
        return self.val("sDve")

    def emit_ffn(self, blk, half, pp, xn_ready, pp_free_dve, ws_free_pe=0):
        self.ws_free_pe = ws_free_pe
        self.y_ev = []
        self.bb_free = 0
        C = self.consts
        small = self.small
        RT, RMS = pp[:, 0:512], pp[:, 512:1024]
        TRa, TRb = pp[:, 1024:1152], pp[:, 1536:1664]
        BC, Y = pp[:, 2048:2560], pp[:, 2560:3584]
        T_in, T_r = small[:, 0:512], small[:, 512:1024]
        SC = lambda a, b: small[:, a:b]
        Z = SC(1024, 1088)
        UP, VP = SC(1088, 1104), SC(1104, 1120)
        AT = SC(1152, 1216)
        ET = SC(1216, 1280)
        RC = SC(1280, 1284)
        SS = SC(1288, 1304)
        HH = SC(1312, 1376)
        B1 = SC(1376, 1440)
        PRER, POSTR = SC(1440, 1456), SC(1456, 1472)
        PREE = SC(1472, 1488)
        PRES, PRERS = SC(1488, 1492), SC(1492, 1496)
        HPRE, HPOST = SC(1496, 1512), SC(1512, 1528)
        POSTE = SC(1528, 1544)
        POSTS, POSTRS = SC(1544, 1548), SC(1548, 1552)
        Bt_in = small[:, 1536 + 512:1536 + 1024] if False else small[:, 2048:2560]
        XPRE = self.AO[:, 0:4096]
        YB = self.AO[:, 4096:8192]
        BB = lambda r: self.AO[:, 8192 + r * 512:8192 + (r + 1) * 512]
        BROWS = self.PT[:, 3584:4096]

        pe0 = self.val("sPe")
        act0 = self.val("sAct")

        @blk.scalar
        def _(s_):
            s_.wait_ge(self.S("sDve"), xn_ready)
            for k in range(DC):
                x2t = self.PT[:, 2048 + (k % 2) * 512:2048 + (k % 2) * 512 + 512]
                if k >= 2:
                    s_.wait_ge(self.S("sPe"), pe0 + (k - 2) + 1)
                self.inc(s_.activation(x2t, self.XN[:, k * 512:(k + 1) * 512], AF.Square),
                         "sAct")

        @blk.tensor
        def _(t):
            t.wait_ge(self.S("sDve"), pp_free_dve)
            for k in range(DC):
                x2t = self.PT[:, 2048 + (k % 2) * 512:2048 + (k % 2) * 512 + 512]
                t.wait_ge(self.S("sAct"), act0 + k + 1)
                t.matmul(RT, self.wrouter[:, k * 128:(k + 1) * 128],
                         self.XN[:, k * 512:(k + 1) * 512],
                         start=(k == 0), stop=(k == DC - 1))
                self.inc(t.matmul(RMS, self.onesb[:, 0:128], x2t,
                                  start=(k == 0), stop=(k == DC - 1)), "sPe")
        rt_done = self.val("sPe")

        @blk.vector
        def _(v):
            v.wait_ge(self.S("sPe"), rt_done)
            self.wait(v, "sDve")
            v.tensor_copy(T_in[0:16, :], RT[0:16, :])
            v.tensor_copy(T_in[32:36, :], RT[32:36, :])
            v.tensor_copy(T_in[64:68, :], RT[64:68, :])
            self.inc(v.tensor_copy(T_r[0:1, :], RMS[0:1, :]), "sDve")
        tin_done = self.val("sDve")

        tr_dve = []
        for a in range(4):
            pe_a = self.val("sPe")
            @blk.tensor
            def _(t, a=a, tin_done=tin_done):
                t.wait_ge(self.S("sDve"), tin_done)
                if a >= 1:
                    t.wait_ge(self.S("sDve"), tr_dve[a - 1])
                t.transpose(TRa, T_in[:, a * 128:(a + 1) * 128], self.ident_f)
                self.inc(t.transpose(TRb, T_r[:, a * 128:(a + 1) * 128], self.ident_f), "sPe")
            pe_a1 = self.val("sPe")

            @blk.scalar
            def _(s_, a=a, pe_a1=pe_a1):
                s_.wait_ge(self.S("sPe"), pe_a1)
                self.inc(s_.activation(RC[:, a:a + 1], bass.AP(TRb.tensor, TRb.offset, [[TRb.ap[0][0], 128], [1, 1]]),
                                       AF.Ln, bias=C[:, 149:150], scale=1.0 / D), "sAct")
                self.wait(s_, "sAct")
                self.inc(s_.activation(RC[:, a:a + 1], RC[:, a:a + 1], AF.Exp, scale=-0.5), "sAct")
            rc_done = self.val("sAct")

            @blk.vector
            def _(v, a=a, rc_done=rc_done):
                v.wait_ge(self.S("sAct"), rc_done)
                self.inc(v.tensor_scalar(Z[:, a * 16:(a + 1) * 16],
                                         bass.AP(TRa.tensor, TRa.offset, [[TRa.ap[0][0], 128], [1, 16]]),
                                         RC[:, a:a + 1], 10.0, ALU.mult, ALU.mult), "sDve")
                self.wait(v, "sDve")
                self.inc(v.tensor_add(Z[:, a * 16:(a + 1) * 16], Z[:, a * 16:(a + 1) * 16],
                                      C[:, 440:456]), "sDve")
                self.inc(v.tensor_scalar(PRER[:, a * 4:(a + 1) * 4],
                                         bass.AP(TRa.tensor, TRa.offset + 32, [[TRa.ap[0][0], 128], [1, 4]]),
                                         RC[:, a:a + 1], None, ALU.mult), "sDve")
                self.wait(v, "sDve")
                self.inc(v.tensor_add(PRER[:, a * 4:(a + 1) * 4], PRER[:, a * 4:(a + 1) * 4],
                                      C[:, 432:436]), "sDve")
                self.inc(v.tensor_scalar(POSTR[:, a * 4:(a + 1) * 4],
                                         bass.AP(TRa.tensor, TRa.offset + 64, [[TRa.ap[0][0], 128], [1, 4]]),
                                         RC[:, a:a + 1], None, ALU.mult), "sDve")
                self.wait(v, "sDve")
                self.inc(v.tensor_add(POSTR[:, a * 4:(a + 1) * 4], POSTR[:, a * 4:(a + 1) * 4],
                                      C[:, 436:440]), "sDve")
            tr_dve.append(self.val("sDve"))
        z_ready0 = self.val("sDve")

        zs = Z[0:1, 0:1]
        zst = zs.ap[0][0]
        MU = SC(1552, 1568)
        mu_aij = bass.AP(Z.tensor, MU.offset, [[zst, 128], [4, 4], [1, 4], [0, 4]])
        z_aij = bass.AP(Z.tensor, Z.offset, [[zst, 128], [16, 4], [4, 4], [1, 4]])
        z_aji = bass.AP(Z.tensor, Z.offset, [[zst, 128], [16, 4], [1, 4], [4, 4]])
        et_red = bass.AP(Z.tensor, ET.offset, [[zst, 128], [16, 4], [4, 4], [1, 4]])
        up_aij = bass.AP(Z.tensor, UP.offset, [[zst, 128], [4, 4], [1, 4], [0, 4]])
        up_aji = bass.AP(Z.tensor, UP.offset, [[zst, 128], [4, 4], [0, 4], [1, 4]])
        vp_aij = bass.AP(Z.tensor, VP.offset, [[zst, 128], [4, 4], [0, 4], [1, 4]])

        @blk.vector
        def _(v):
            v.wait_ge(self.S("sDve"), z_ready0)
            self.inc(v.reduce_max(MU[:, :], z_aij, axis=AX.X), "sDve")
            self.wait(v, "sDve")
            self.inc(v.tensor_tensor(Z[:, :], Z[:, :], mu_aij, ALU.subtract), "sDve")
        z_ready = self.val("sDve")

        for it in range(50):
            if it > 0:
                @blk.vector
                def _(v, w=lnv_done):
                    v.wait_ge(self.S("sAct"), w)
                    self.inc(v.tensor_tensor(ET[:, :], z_aij, vp_aij, ALU.subtract), "sDve")
            as_done = self.val("sDve")

            @blk.scalar
            def _(s_, it=it, as_done=as_done):
                s_.wait_ge(self.S("sDve"), z_ready if it == 0 else as_done)
                src = Z[:, :] if it == 0 else ET[:, :]
                self.inc(s_.activation(ET[:, :], src, AF.Exp), "sAct")
            e_done = self.val("sAct")

            @blk.vector
            def _(v, e_done=e_done):
                v.wait_ge(self.S("sAct"), e_done)
                self.inc(v.reduce_sum(SS[:, :], et_red, axis=AX.X), "sDve")
            s_done = self.val("sDve")

            @blk.scalar
            def _(s_, s_done=s_done):
                s_.wait_ge(self.S("sDve"), s_done)
                self.inc(s_.activation(UP[:, :], SS[:, :], AF.Ln), "sAct")
            lns_done = self.val("sAct")

            @blk.vector
            def _(v, lns_done=lns_done):
                v.wait_ge(self.S("sAct"), lns_done)
                self.inc(v.tensor_tensor(ET[:, :], z_aji, up_aji, ALU.subtract), "sDve")
            as2_done = self.val("sDve")

            @blk.scalar
            def _(s_, as2_done=as2_done):
                s_.wait_ge(self.S("sDve"), as2_done)
                self.inc(s_.activation(ET[:, :], ET[:, :], AF.Exp), "sAct")
            e2_done = self.val("sAct")

            @blk.vector
            def _(v, e2_done=e2_done):
                v.wait_ge(self.S("sAct"), e2_done)
                self.inc(v.reduce_sum(SS[:, :], et_red, axis=AX.X), "sDve")
            s2_done = self.val("sDve")

            @blk.scalar
            def _(s_, s2_done=s2_done):
                s_.wait_ge(self.S("sDve"), s2_done)
                self.inc(s_.activation(VP[:, :], SS[:, :], AF.Ln), "sAct")
            lnv_done = self.val("sAct")
        sk_av = self.val("sDve")

        @blk.vector
        def _(v, w=lnv_done):
            v.wait_ge(self.S("sAct"), w)
            self.wait(v, "sDve")
            self.inc(v.tensor_tensor(B1[:, :], z_aij, vp_aij, ALU.subtract), "sDve")
            self.wait(v, "sDve")
            self.inc(v.tensor_tensor(B1[:, :], B1[:, :], up_aij, ALU.subtract), "sDve")
        hfin = self.val("sDve")

        @blk.scalar
        def _(s_):
            s_.wait_ge(self.S("sDve"), hfin)
            self.inc(s_.activation(HH[:, :], B1[:, :], AF.Exp), "sAct")
        h_done = self.val("sAct")

        @blk.vector
        def _(v):
            v.wait_ge(self.S("sAct"), h_done)
            self.wait(v, "sDve")
            self.inc(v.tensor_scalar(B1[:, :], HH[:, :], C[:, 144:145], None, ALU.mult), "sDve")
            self.wait(v, "sDve")
            ipv = bass.AP(C, C[0:1, 128:129].offset, [[C[0:1, 0:1].ap[0][0], 128], [0, 4], [1, 16]])
            self.inc(v.tensor_scalar(HH[:, :], ipv, C[:, 145:146], None, ALU.mult), "sDve")
            self.wait(v, "sDve")
            self.inc(v.tensor_add(B1[:, :], B1[:, :], HH[:, :]), "sDve")
        bres_done = self.val("sDve")

        PM = SC(1304, 1312)
        @blk.vector
        def _(v):
            v.wait_ge(self.S("sDve"), z_ready)
            self.inc(v.reduce_max(PM[:, 0:4],
                                  bass.AP(PRER.tensor, PRER.offset, [[PRER.ap[0][0], 128], [4, 4], [1, 4]]),
                                  axis=AX.X), "sDve")
            self.inc(v.reduce_max(PM[:, 4:8],
                                  bass.AP(POSTR.tensor, POSTR.offset, [[POSTR.ap[0][0], 128], [4, 4], [1, 4]]),
                                  axis=AX.X), "sDve")
            self.wait(v, "sDve")
            pmv = bass.AP(PM.tensor, PM.offset, [[PM.ap[0][0], 128], [1, 4], [0, 4]])
            pmv2 = bass.AP(PM.tensor, PM.offset + 4, [[PM.ap[0][0], 128], [1, 4], [0, 4]])
            self.inc(v.tensor_tensor(PRER[:, :], PRER[:, :], pmv, ALU.subtract), "sDve")
            self.inc(v.tensor_tensor(POSTR[:, :], POSTR[:, :], pmv2, ALU.subtract), "sDve")
        sm_stab = self.val("sDve")

        @blk.scalar
        def _(s_):
            s_.wait_ge(self.S("sDve"), sm_stab)
            self.inc(s_.activation(PREE[:, :], PRER[:, :], AF.Exp), "sAct")
            self.inc(s_.activation(POSTE[:, :], POSTR[:, :], AF.Exp), "sAct")
        pp_e = self.val("sAct")

        @blk.vector
        def _(v):
            v.wait_ge(self.S("sAct"), pp_e)
            self.inc(v.reduce_sum(PRES[:, :],
                                  bass.AP(PREE.tensor, PREE.offset, [[PREE.ap[0][0], 128], [4, 4], [1, 4]]),
                                  axis=AX.X), "sDve")
            self.inc(v.reduce_sum(POSTS[:, :],
                                  bass.AP(POSTE.tensor, POSTE.offset, [[POSTE.ap[0][0], 128], [4, 4], [1, 4]]),
                                  axis=AX.X), "sDve")
            self.wait(v, "sDve")
            self.inc(v.reciprocal(PRERS[:, :], PRES[:, :]), "sDve")
            self.inc(v.reciprocal(POSTRS[:, :], POSTS[:, :]), "sDve")
            self.wait(v, "sDve")
            pre_rs_v = bass.AP(PRERS.tensor, PRERS.offset, [[PRERS.ap[0][0], 128], [1, 4], [0, 4]])
            post_rs_v = bass.AP(POSTRS.tensor, POSTRS.offset, [[POSTRS.ap[0][0], 128], [1, 4], [0, 4]])
            self.inc(v.tensor_tensor(HPRE[:, :], PREE[:, :], pre_rs_v, ALU.mult), "sDve")
            self.inc(v.tensor_tensor(HPOST[:, :], POSTE[:, :], post_rs_v, ALU.mult), "sDve")
        hp_done = self.val("sDve")

        @blk.vector
        def _(v):
            v.wait_ge(self.S("sDve"), max(bres_done, hp_done))
            for a in range(4):
                bt = Bt_in[:, a * 128:(a + 1) * 128]
                v.tensor_copy(bt[:, 0:16], B1[:, a * 16:(a + 1) * 16])
                v.tensor_copy(bt[:, 16:20], HPRE[:, a * 4:(a + 1) * 4])
                v.tensor_copy(bt[:, 20:24], HPOST[:, a * 4:(a + 1) * 4])
            self.inc(v.tensor_copy(Bt_in[:, 24:25], B1[:, 0:1]), "sDve")
        bt_done = self.val("sDve")

        br_pe0 = self.val("sPe")
        br_dv0 = self.val("sDve")
        @blk.tensor
        def _(t):
            t.wait_ge(self.S("sDve"), bt_done)
            for a in range(4):
                if a >= 1:
                    t.wait_ge(self.S("sDve"), br_dv0 + a)
                self.inc(t.transpose(TRa, Bt_in[:, a * 128:(a + 1) * 128], self.ident_f), "sPe")

        @blk.vector
        def _(v):
            for a in range(4):
                v.wait_ge(self.S("sPe"), br_pe0 + a + 1)
                self.inc(v.tensor_copy(BROWS[0:24, a * 128:(a + 1) * 128],
                                       bass.AP(TRa.tensor, TRa.offset, [[TRa.ap[0][0], 24], [1, 128]])), "sDve")
        brows_done = self.val("sDve")

        bb_pe0 = self.val("sPe")
        bb_dv0 = self.val("sDve")
        @blk.tensor
        def _(t):
            t.wait_ge(self.S("sDve"), brows_done)
            for n in range(4):
                if n >= 1:
                    t.wait_ge(self.S("sDve"), bb_dv0 + n)
                self.inc(t.matmul(BC, self.selt_s[:, (16 + n) * 128:(17 + n) * 128],
                                  self.PT[:, 3584:4096], start=True, stop=True), "sPe")

        @blk.vector
        def _(v):
            self.wait(v, "sDve")
            for n in range(4):
                v.wait_ge(self.S("sPe"), bb_pe0 + n + 1)
                self.inc(v.tensor_copy(BB(n), BC[:, :]), "sDve")
        bbpre_done = self.val("sDve")

        @blk.vector
        def _(v):
            v.wait_ge(self.S("sDve"), bbpre_done)
            for c in range(8):
                t0 = self.PT[:, 0:512]
                t1 = self.PT[:, 512:1024]
                t2 = self.PT[:, 1024:1536]
                t3 = self.PT[:, 1536:2048]
                self.wait(v, "sDve")
                v.tensor_mul(t0, self.XN[:, (0 * 8 + c) * 512:(0 * 8 + c) * 512 + 512], BB(0))
                v.tensor_mul(t1, self.XN[:, (1 * 8 + c) * 512:(1 * 8 + c) * 512 + 512], BB(1))
                v.tensor_mul(t2, self.XN[:, (2 * 8 + c) * 512:(2 * 8 + c) * 512 + 512], BB(2))
                self.inc(v.tensor_mul(t3, self.XN[:, (3 * 8 + c) * 512:(3 * 8 + c) * 512 + 512], BB(3)), "sDve")
                self.wait(v, "sDve")
                v.tensor_add(t0, t0, t1)
                self.inc(v.tensor_add(t2, t2, t3), "sDve")
                self.wait(v, "sDve")
                self.inc(v.tensor_add(XPRE[:, c * 512:(c + 1) * 512], t0, t2), "sDve")
        xpre_done = self.val("sDve")

        @blk.sync
        def _(sy):
            sy.wait_ge(self.S("sPe"), self.ws_free_pe)
            src = bass.AP(self.ag2aout, 1024 * 1024,
                          [[1024, 128], [self.AG2AR, 8], [1, 1024]])
            self.inc(sy.dma_start(self.WS[:, 0:8192], src), "cWbr")
        wbr_ld = self.val("cWbr")

        y_pe0 = self.val("sPe")
        y_dv0 = self.val("sDve")
        @blk.tensor
        def _(t):
            t.wait_ge(self.S("cWbr"), wbr_ld)
            t.wait_ge(self.S("sDve"), xpre_done)
            for co in range(8):
                yb = Y[:, (co % 2) * 512:(co % 2) * 512 + 512]
                if co >= 2:
                    t.wait_ge(self.S("sDve"), y_dv0 + co - 1)
                for kc in range(8):
                    ins = t.matmul(yb, self.WS[:, kc * 1024 + co * 128:kc * 1024 + (co + 1) * 128],
                                   XPRE[:, kc * 512:(kc + 1) * 512],
                                   start=(kc == 0), stop=(kc == 7))
                self.inc(ins, "sPe")

        @blk.vector
        def _(v):
            for co in range(8):
                v.wait_ge(self.S("sPe"), y_pe0 + co + 1)
                yb = Y[:, (co % 2) * 512:(co % 2) * 512 + 512]
                self.inc(v.tensor_copy(YB[:, co * 512:(co + 1) * 512], yb), "sDve")
                self.y_ev.append(self.val("sDve"))
        y_done = self.val("sDve")

        for i in range(4):
            bm_pe0 = self.val("sPe")
            bbf = self.bb_free
            bm_dv0 = self.val("sDve")
            @blk.tensor
            def _(t, i=i, bbf=bbf, bm_dv0=bm_dv0):
                if bbf:
                    t.wait_ge(self.S("sDve"), bbf)
                rows = [i * 4 + 0, i * 4 + 1, i * 4 + 2, i * 4 + 3, 20 + i]
                for ri, r in enumerate(rows):
                    if ri >= 1:
                        t.wait_ge(self.S("sDve"), bm_dv0 + ri)
                    self.inc(t.matmul(BC, self.selt_s[:, r * 128:(r + 1) * 128],
                                      self.PT[:, 3584:4096], start=True, stop=True), "sPe")

            @blk.vector
            def _(v, i=i):
                self.wait(v, "sDve")
                for ri in range(5):
                    v.wait_ge(self.S("sPe"), bm_pe0 + ri + 1)
                    self.inc(v.tensor_copy(BB(ri), BC[:, :]), "sDve")
            bb_done = self.val("sDve")

            @blk.vector
            def _(v, i=i, bb_done=bb_done, y_done=y_done):
                v.wait_ge(self.S("sDve"), max(bb_done, y_done))
                for c in range(8):
                    t0 = self.PT[:, 0:512]
                    t1 = self.PT[:, 512:1024]
                    t2 = self.PT[:, 1024:1536]
                    t3 = self.PT[:, 1536:2048]
                    t4 = self.PT[:, 2048:2560]
                    self.wait(v, "sDve")
                    v.tensor_mul(t0, self.XN[:, (0 * 8 + c) * 512:(0 * 8 + c) * 512 + 512], BB(0))
                    v.tensor_mul(t1, self.XN[:, (1 * 8 + c) * 512:(1 * 8 + c) * 512 + 512], BB(1))
                    v.tensor_mul(t2, self.XN[:, (2 * 8 + c) * 512:(2 * 8 + c) * 512 + 512], BB(2))
                    v.tensor_mul(t3, self.XN[:, (3 * 8 + c) * 512:(3 * 8 + c) * 512 + 512], BB(3))
                    self.inc(v.tensor_mul(t4, YB[:, c * 512:(c + 1) * 512], BB(4)), "sDve")
                    self.wait(v, "sDve")
                    mg = i * 8 + c
                    r1c = self.R1[:, mg * 512:(mg + 1) * 512]
                    self.inc(v.tensor_add(r1c, r1c, t0), "sDve")
                    self.wait(v, "sDve")
                    v.tensor_add(t1, t1, t2)
                    self.inc(v.tensor_add(t3, t3, t4), "sDve")
                    self.wait(v, "sDve")
                    self.inc(v.tensor_add(t1, t1, t3), "sDve")
                    self.wait(v, "sDve")
                    self.inc(v.tensor_add(r1c, r1c, t1), "sDve")
            self.bb_free = self.val("sDve")
        return self.val("sDve")

    def emit_attnb(self, blk, half, pp, xn_ready, pp_free_dve, ws_free_pe2=0):
        self.ws_free_pe2 = ws_free_pe2
        self.elu_act = []
        _sfx = str(half)
        agout = self.agout.ap()
        PTf = self.PT[:, :].bitcast(F32)
        SIGQ = self.AO[:, 0:4096]
        RETB = self.AO[:, 4096:8192]
        RSTRIDE = 3072 * 1024

        pe0 = self.val("sPe")
        slabv = []
        @blk.sync
        def _(sy):
            sy.wait_ge(self.S("sPe"), self.ws_free_pe2)
            for kc in range(DC):
                if kc >= 3:
                    sy.wait_ge(self.S("sPe"), pe0 + (kc - 3) + 1)
                rank, kl = kc // 4, kc % 4
                src = bass.AP(self.agout, rank * RSTRIDE + (2048 + kl * 128) * 1024,
                              [[1024, 128], [1, 1024]])
                slabv.append(self.inc(sy.dma_start(
                    self.WS[:, (kc % 3) * 1024:(kc % 3) * 1024 + 1024], src), f"cSb{kc % 3}"))

        @blk.tensor
        def _(t):
            t.wait_ge(self.S("sDve"), max(xn_ready, pp_free_dve))
            for kc in range(DC):
                t.wait_ge(self.S(f"cSb{kc % 3}"), slabv[kc])
                for m in range(8):
                    ins = t.matmul(pp[:, m * 512:(m + 1) * 512],
                                   self.WS[:, (kc % 3) * 1024 + m * 128:(kc % 3) * 1024 + (m + 1) * 128],
                                   self.XN[:, kc * 512:(kc + 1) * 512],
                                   start=(kc == 0), stop=(kc == DC - 1))
                self.inc(ins, "sPe")
        qs_done = self.val("sPe")

        elu_dv = []
        for m in range(8):
            @blk.vector
            def _(v, m=m, qs_done=qs_done):
                v.wait_ge(self.S("sPe"), qs_done)
                self.wait(v, "sDve")
                if m >= 2:
                    v.wait_ge(self.S("sAct"), self.elu_act[m - 2])
                u = PTf[:, (m % 2) * 1024:(m % 2) * 1024 + 512]
                r = PTf[:, (m % 2) * 1024 + 512:(m % 2) * 1024 + 1024]
                self.inc(v.tensor_scalar(u, pp[:, m * 512:(m + 1) * 512],
                                         self.bqs[:, m:m + 1], None, ALU.add), "sDve")
                self.wait(v, "sDve")
                self.inc(v.tensor_scalar(r, u, 0.0, None, ALU.max), "sDve")
                self.wait(v, "sDve")
                self.inc(v.tensor_sub(u, u, r), "sDve")
            t_dv = self.val("sDve")

            @blk.scalar
            def _(s_, m=m, t_dv=t_dv):
                s_.wait_ge(self.S("sDve"), t_dv)
                u = PTf[:, (m % 2) * 1024:(m % 2) * 1024 + 512]
                self.inc(s_.activation(u, u, AF.Exp), "sAct")
            self.elu_act.append(self.val("sAct"))
            e_act = self.val("sAct")

            @blk.vector
            def _(v, m=m, e_act=e_act):
                v.wait_ge(self.S("sAct"), e_act)
                u = PTf[:, (m % 2) * 1024:(m % 2) * 1024 + 512]
                r = PTf[:, (m % 2) * 1024 + 512:(m % 2) * 1024 + 1024]
                self.inc(v.tensor_add(SIGQ[:, m * 512:(m + 1) * 512], u, r), "sDve")
            elu_dv.append(self.val("sDve"))
        sig_done = self.val("sDve")

        NUM = [pp[:, 0:512], pp[:, 1024:1536]]
        DEN = [pp[:, 512:1024], pp[:, 1536:2048]]
        ret_dv = []
        for h in range(8):
            par = h % 2
            pe_h0 = self.val("sPe")
            @blk.tensor
            def _(t, h=h, par=par):
                if h == 0:
                    t.wait_ge(self.S("cMZ"), self.m_done)
                t.wait_ge(self.S("sDve"), elu_dv[min(2 * h + 1, 7)])
                if h >= 2:
                    t.wait_ge(self.S("sDve"), ret_dv[h - 2])

                t.matmul(NUM[par], self.mz[:, h * 128:(h + 1) * 128],
                         SIGQ[:, h * 512:(h + 1) * 512], start=True, stop=True)
                self.inc(t.matmul(DEN[par], self.mz[:, 1024 + h * 128:1024 + (h + 1) * 128],
                                  SIGQ[:, h * 512:(h + 1) * 512], start=True, stop=True), "sPe")
            nd_pe = self.val("sPe")

            @blk.vector
            def _(v, h=h, par=par, nd_pe=nd_pe):
                v.wait_ge(self.S("sPe"), nd_pe)
                dn = PTf[:, 1536:2048] if par else PTf[:, 512:1024]
                self.wait(v, "sDve")
                self.inc(v.tensor_scalar(dn, DEN[par], 1e-6, None, ALU.add), "sDve")
                self.wait(v, "sDve")
                self.inc(v.reciprocal(dn, dn), "sDve")
                self.wait(v, "sDve")
                self.inc(v.tensor_mul(RETB[:, h * 512:(h + 1) * 512], NUM[par], dn), "sDve")
            ret_dv.append(self.val("sDve"))
        ret_done = self.val("sDve")

        opj_pe0 = self.val("sPe")
        winv = []
        @blk.sync
        def _(sy):
            i = 0
            for oct_ in range(4):
                for jc in range(8):
                    if i >= 2:
                        sy.wait_ge(self.S("sPe"), opj_pe0 + (i - 1))
                    src = bass.AP(self.agout, (oct_ * 2) * RSTRIDE + (2560 + jc * 64) * 1024,
                                  [[512, 128], [RSTRIDE, 2], [1, 512]])
                    winv.append(self.inc(sy.dma_start(
                        self.WS[:, 4096 + (i % 2) * 1024:4096 + (i % 2) * 1024 + 1024], src),
                        f"cWf{i % 2}"))
                    i += 1

        dv0 = self.val("sDve")
        @blk.tensor
        def _(t):
            t.wait_ge(self.S("sDve"), ret_done)
            i = 0
            for oct_ in range(4):
                if oct_ >= 1:
                    t.wait_ge(self.S("sDve"), dv0 + 8 * oct_)
                for jc in range(8):
                    t.wait_ge(self.S(f"cWf{i % 2}"), winv[i])
                    base = 4096 + (i % 2) * 1024
                    for m in range(8):
                        mc = oct_ * 8 + m
                        col = ((mc // 4) - oct_ * 2) * 512 + (mc % 4) * 128
                        ins = t.matmul(pp[:, m * 512:(m + 1) * 512],
                                       self.WS[:, base + col:base + col + 128],
                                       RETB[:, jc * 512:(jc + 1) * 512],
                                       start=(jc == 0), stop=(jc == 7))
                    self.inc(ins, "sPe")
                    i += 1
        ob_pe0 = opj_pe0

        @blk.vector
        def _(v):
            for oct_ in range(4):
                v.wait_ge(self.S("sPe"), opj_pe0 + 8 * (oct_ + 1))
                for m in range(8):
                    mg = oct_ * 8 + m
                    r1c = self.R1[:, mg * 512:(mg + 1) * 512]
                    self.inc(v.tensor_add(r1c, r1c, pp[:, m * 512:(m + 1) * 512]), "sDve")
        return self.val("sDve")

    def emit_pipe(self, blk, half, ao_ready, r1_free_pe):
        nc = self.nc
        with nc.psum_tensor("pp%d" % half, [128, 4096], F32) as pp:
            self.pE, self.pF = pp[:, 0:512], pp[:, 512:1024]
            x1_done = self.emit_oproj(blk, half, pp, ao_ready, r1_free_pe)
            op_evac = self.val("sDve")
            xn1 = self.emit_ln(blk, self.R1, self.XN, self.epsln,
                               pe_waits=[("sDve", op_evac)])
            x2_done = self.emit_ffn(blk, half, pp, xn1, self.val("sDve"),
                                    ws_free_pe=self.val("sPe"))
            xn2 = self.emit_ln(blk, self.R1, self.XN, self.epsln,
                               pe_waits=[("sDve", x2_done)],
                               dve_waits=[("sDve", x2_done)])
            x3_done = self.emit_attnb(blk, half, pp, xn2, self.val("sDve"),
                                      ws_free_pe2=self.val("sPe"))
            xn3 = self.emit_ln(blk, self.R1, self.XN, self.epsln,
                               pe_waits=[("sDve", x3_done)],
                               dve_waits=[("sDve", x3_done)])
            x4_done = self.emit_ffn(blk, half, pp, xn3, self.val("sDve"),
                                    ws_free_pe=self.val("sPe"))

        @blk.sync
        def _(sy, half=half):
            sy.wait_ge(self.S("sDve"), x4_done)
            dst = bass.AP(self.yout.tensor, half * 512,
                          [[1024, 128], [131072, 32], [1, 512]])
            src = bass.AP(self.R1, self.R1[0:1, 0:1].offset,
                          [[self.R1[0:1, 0:1].ap[0][0], 128], [512, 32], [1, 512]])
            self.inc(sy.dma_start(dst, src), "cOut")
        return self.val("cOut")

    def build_full(self):
        nc = self.nc
        with nc.Block() as blk:
            self.emit_prologue(blk)
            q0_ready = self.val("sDve")
            self.emit_attention(blk, 0, q_ready=q0_ready, wait_cc=True)
            att0_pe = self.val("sPe")
            ao0 = self.val("sDve")
            self.emit_pipe(blk, 0, ao_ready=ao0, r1_free_pe=att0_pe)

            xnld0 = self.val("cXn1")
            @blk.sync
            def _(sy):
                sy.wait_ge(self.S("sDve"), self.val("sDve"))
                src = bass.AP(self.xn0b1.tensor, 0, [[512, 128], [65536, 32], [1, 512]])
                dst = bass.AP(self.XN, self.XN[0:1, 0:1].offset,
                              [[self.XN[0:1, 0:1].ap[0][0], 128], [512, 32], [1, 512]])
                self.inc(sy.dma_start(dst, src), "cXn1")
            xn1_ld = self.val("cXn1")

            @blk.vector
            def _(v):
                v.wait_ge(self.S("cXn1"), xn1_ld)
                self.inc(v.engine_nop(), "sDve")
            xnb1_ready = self.val("sDve")

            with nc.psum_tensor("qp1", [128, 4096], F32) as qp1:
                self.emit_qproj(blk, self.XN, qp1, xn_wait=xnb1_ready)
            q1_ready = self.val("sDve")
            self.emit_attention(blk, 1, q_ready=q1_ready, wait_cc=False,
                                r1_wait=("cOut", self.val("cOut")))
            att1_pe = self.val("sPe")
            ao1 = self.val("sDve")
            self.emit_pipe(blk, 1, ao_ready=ao1, r1_free_pe=att1_pe)

        with nc.Block() as blk2:
            @blk2.sync
            def _(sy):
                sy.wait_ge(self.S("cOut"), self.val("cOut"))
        return nc



def kernel(**inputs):
    m = MK("full")
    nc = m.build()
    in_maps = host_build_inputs(inputs)
    res = run_bass_kernel_spmd(nc, in_maps, core_ids=list(range(NCORE)))
    return host_gather_output(res.results)


# version 06
# speedup vs baseline: 1.0477x; 1.0055x over previous
import numpy as np
from contextlib import ExitStack

import concourse.bass as bass
import concourse.mybir as mybir
from concourse.bass_utils import run_bass_kernel_spmd

F32 = mybir.dt.float32
BF16 = mybir.dt.bfloat16
AF = mybir.ActivationFunctionType
ALU = mybir.AluOpType
AX = mybir.AxisListType

D = 4096
DC = 32
SEQ = 4096
BATCH = 2
NCORE = 8
TOK = 1024
HALF = 512
NSLOT = 8
TB = 128
NKT = 32
HKV = 8
G = 4
HD = 128
LN_EPS = 1e-5
RMS_EPS = 1.1920929e-07
ATT_SCALE = HD ** -0.5

PFIX = [8 * (s % 4 + 1) for s in range(NSLOT)]
NGRP = [p // 8 for p in PFIX]

RANKS = list(range(NCORE))


def slot_to_block(core, s):
    idx = s % 4
    if s < 4:
        return core + 8 * idx
    return (7 - core) + 8 * idx


def block_owner(b, kt):
    if b == 0:
        return kt % 8, kt // 8
    return 7 - (kt % 8), 4 + kt // 8



def host_build_inputs(inputs):
    x = np.asarray(inputs["L0_attn_a_hidden_states"], np.float32)
    wq_a = np.ascontiguousarray(np.asarray(inputs["L0_attn_a_w_q"], np.float32))
    wk_a = np.ascontiguousarray(np.asarray(inputs["L0_attn_a_w_k"], np.float32))
    wv_a = np.ascontiguousarray(np.asarray(inputs["L0_attn_a_w_v"], np.float32))
    wo_a = np.ascontiguousarray(np.asarray(inputs["L0_attn_a_w_o"], np.float32))
    Wq_b = np.asarray(inputs["L0_attn_b_Wq"], np.float32)
    bq_b = np.asarray(inputs["L0_attn_b_bq"], np.float32)
    Wo_b = np.asarray(inputs["L0_attn_b_Wo"], np.float32)
    M_b = np.ascontiguousarray(np.asarray(inputs["L0_attn_b_M"], np.float32))
    z_b = np.ascontiguousarray(np.asarray(inputs["L0_attn_b_z"], np.float32))
    wres = np.ascontiguousarray(np.asarray(inputs["L0_ffn_W_res"], np.float32))
    wpre = np.ascontiguousarray(np.asarray(inputs["L0_ffn_W_pre"], np.float32))
    wpost = np.ascontiguousarray(np.asarray(inputs["L0_ffn_W_post"], np.float32))
    hres_b = np.ascontiguousarray(np.asarray(inputs["L0_ffn_H_res_base"], np.float32))
    hpre_b = np.ascontiguousarray(np.asarray(inputs["L0_ffn_H_pre_base"], np.float32))
    hpost_b = np.ascontiguousarray(np.asarray(inputs["L0_ffn_H_post_base"], np.float32))
    gate = np.ascontiguousarray(np.asarray(inputs["L0_ffn_gate"], np.float32))
    wbr = np.ascontiguousarray(np.asarray(inputs["L0_ffn_W_branch"], np.float32))

    sel = np.concatenate([np.arange(4 * h * 128, 4 * h * 128 + 128) for h in range(HKV)])
    wqsel = np.ascontiguousarray(Wq_b[sel, :])
    bqsel = np.ascontiguousarray(bq_b[sel])

    import ml_dtypes
    bf = ml_dtypes.bfloat16

    ident = np.eye(128, dtype=np.float32)
    ipat = np.tile(np.eye(4, dtype=np.float32).reshape(1, 16), (128, 1)).copy()

    in_maps = []
    for c in range(NCORE):
        cols = []
        for s in range(NSLOT):
            b = s // 4
            blk = slot_to_block(c, s)
            cols.append(x[b, blk * TB:(blk + 1) * TB, :].T)
        x_fm = np.ascontiguousarray(np.concatenate(cols, axis=1), dtype=np.float32)

        maskt = np.zeros((NSLOT, 8, 128, 128), np.float32)
        triu = np.triu(np.ones((128, 128), np.float32))
        for s in range(NSLOT):
            blk = slot_to_block(c, s)
            prefix = blk + 1
            base = PFIX[s] - 8
            for j in range(8):
                p = base + j
                if p < prefix - 1:
                    maskt[s, j] = 1.0
                elif p == prefix - 1:
                    maskt[s, j] = triu
        maskt = np.ascontiguousarray(maskt.transpose(0, 2, 1, 3)).astype(bf)

        wqsel_cols = np.ascontiguousarray(wqsel[:, 512 * c:512 * (c + 1)])
        wob_rows = np.ascontiguousarray(Wo_b[512 * c:512 * (c + 1), :])

        wsl = np.concatenate([
            wk_a[512 * c:512 * (c + 1), :].reshape(512, 1024),
            wv_a[512 * c:512 * (c + 1), :].reshape(512, 1024),
            wbr[128 * c:128 * (c + 1), :].reshape(128, 1024),
            wq_a[512 * c:512 * (c + 1), :].reshape(2048, 1024),
            wo_a[512 * c:512 * (c + 1), :].reshape(2048, 1024),
        ], axis=0)
        wslabs = np.ascontiguousarray(wsl, dtype=np.float32)
        _ = (inputs["L0_attn_b_Wk"], inputs["L0_attn_b_bk"],
             inputs["L0_attn_b_Wv"], inputs["L0_attn_b_bv"])

        selt = np.zeros((128, 24 * 128), np.float32)
        for r in range(24):
            selt[r, r * 128:(r + 1) * 128] = 1.0
        selt = selt.astype(bf)
        baser = np.concatenate([hres_b.reshape(16), hpre_b, hpost_b]).reshape(1, 24).astype(np.float32)

        in_maps.append({
            "xin": x_fm, "selt": selt, "baser": baser, "wslabs": wslabs,
            "wqsel_cols": wqsel_cols, "bqsel": bqsel, "wob_rows": wob_rows,
            "m_in": M_b, "z_in": z_b,
            "wres": wres, "wpre": wpre, "wpost": wpost,
            "hres_b": hres_b.reshape(16).copy(), "hpre_b": hpre_b, "hpost_b": hpost_b,
            "gate": gate, "wbr": wbr,
            "ident": ident, "ipat": ipat, "maskt": maskt,
        })
    return in_maps


def host_gather_output(results):
    out = np.empty((BATCH, SEQ, D), np.float32)
    for c in range(NCORE):
        y = results[c]["yout"]
        for s in range(NSLOT):
            b = s // 4
            blk = slot_to_block(c, s)
            out[b, blk * TB:(blk + 1) * TB, :] = y[:, s * TB:(s + 1) * TB].T
    return out



class MK:
    def __init__(self, stage="full"):
        self.stage = stage
        self._kv_prev_pe = None
        nc = bass.Bass("TRN2", target_bir_lowering=False)
        self.nc = nc
        self.es = ExitStack()
        self.cnt = {}
        self.sems = {}
        self.dbg_outputs = {}
        self.wdone = {}

        T = lambda n, s, d=F32: nc.dram_tensor(n, s, d, kind="ExternalInput").ap()
        self.xin = T("xin", [D, TOK])
        self.wslabs = T("wslabs", [5248, 1024])
        self.wqsel_cols = T("wqsel_cols", [1024, 512])
        self.bqsel = T("bqsel", [1024])
        self.wob_rows = T("wob_rows", [512, D])
        self.m_in = T("m_in", [HKV, 128, 128])
        self.z_in = T("z_in", [HKV, 128])
        self.wres = T("wres", [16, D])
        self.wpre = T("wpre", [4, D])
        self.wpost = T("wpost", [4, D])
        self.hres_b = T("hres_b", [16])
        self.hpre_b = T("hpre_b", [4])
        self.hpost_b = T("hpost_b", [4])
        self.gate = T("gate", [1])
        self.wbr = T("wbr", [1024, 1024])
        self.ident = T("ident", [128, 128])
        self.ipat = T("ipat", [128, 16])
        self.maskt = T("maskt", [NSLOT, 128, 8, 128], BF16)
        self.selt = T("selt", [128, 24 * 128], BF16)
        self.baser = T("baser", [1, 24])

        self.yout = nc.dram_tensor("yout", [D, TOK], F32, kind="ExternalOutput").ap()
        I = lambda n, s, d=BF16: nc.dram_tensor(n, s, d, kind="Internal").ap()
        self.xn0b1 = I("xn0b1", [DC, 128, HALF])
        self.ag2ain = nc.dram_tensor("ag2ain", [1152, 1024], BF16, kind="Internal")
        self.ag2aout = nc.dram_tensor("ag2aout", [1152 * NCORE, 1024], BF16, kind="Internal")
        self.ag2bin = nc.dram_tensor("ag2bin", [4096, 1024], BF16, kind="Internal")
        self.ag2bout = nc.dram_tensor("ag2bout", [4096 * NCORE, 1024], BF16, kind="Internal")
        self.agin = nc.dram_tensor("agin", [3072, 1024], BF16, kind="Internal")
        self.agout = nc.dram_tensor("agout", [3072 * NCORE, 1024], BF16, kind="Internal")

        sb = lambda n, s, d: self.es.enter_context(nc.sbuf_tensor(n, s, d))
        self.R1 = sb("R1", [128, 16384], F32)
        self.XN = sb("XN", [128, 16384], BF16)
        self.QB = sb("QB", [128, 16384], BF16)
        self.AO = sb("AO", [128, 16384], BF16)
        self.WS = sb("WS", [128, 8192], BF16)
        self.PT = sb("PTS", [128, 4096], BF16)
        self.wrouter = sb("wrouter", [128, DC * 128], BF16)
        self.mz = sb("mz", [128, 2048], BF16)
        self.consts = sb("consts", [128, 640], F32)
        self.cbf = sb("cbf", [128, 256], BF16)
        self.selt_s = sb("selt_s", [128, 3072], BF16)
        self.bqs = sb("bqs", [128, 8], F32)
        self.mask8 = sb("mask8", [128, 8 * 128], BF16)
        self.kvA = None

        self.small = self.QB[:, 8192:16384].bitcast(F32)
        for name in ["sPe", "sAct", "sDve", "sPool", "sCc"]:
            self.sems[name] = self.es.enter_context(nc.semaphore(name))
            self.cnt[name] = 0

    def S(self, name):
        if name not in self.sems:
            self.sems[name] = self.es.enter_context(self.nc.semaphore(name))
            self.cnt[name] = 0
        return self.sems[name]

    ENG = {"sPe", "sAct", "sDve", "sPool", "sCc"}

    def inc(self, inst, name, k=None):
        self.S(name)
        if k is None:
            k = 1 if name in self.ENG else 16
        self.cnt[name] += k
        inst.then_inc(self.sems[name], k)
        return self.cnt[name]

    def val(self, name):
        self.S(name)
        return self.cnt[name]

    def wait(self, eng, name, v=None):
        self.S(name)
        eng.wait_ge(self.sems[name], self.cnt[name] if v is None else v)

    def W(self, eng, name, v=None):
        self.S(name)
        eng.wait_ge(self.sems[name], self.cnt[name] if v is None else v)

    def add_dbg(self, name, shape, dtype=F32):
        ap = self.nc.dram_tensor(name, shape, dtype, kind="ExternalOutput").ap()
        self.dbg_outputs[name] = ap
        return ap

    AG2AR = 1152 * 1024
    AG2BR = 4096 * 1024

    def w_slab(self, which, k, w):
        base = {"wk": 0, "wv": 512, "wbr": 1024}[which]
        rank, j = (k, 0) if which == "wbr" else (k // 4, k % 4)
        off = rank * self.AG2AR + (base + j * 128) * 1024
        return bass.AP(self.ag2aout, off, [[w, 128], [1, w]])

    def w_window(self, which, oct_, kb):
        base = {"wq": 0, "wo": 2048}[which]
        off = kb * self.AG2BR + (base * 1024) + oct_ * 1024
        return bass.AP(self.ag2bout, off, [[4096, 128], [512 * 1024, 4], [1, 1024]])


    def emit_ln(self, blk, xsrc, dst, eps, extra_dst_dram=None, act_waits=(),
                dve_waits=(), pe_waits=()):
        small = self.small
        XB = lambda k: self.PT[:, (k % 4) * 1024 + 512:(k % 4) * 1024 + 1024]
        pSx, pSx2 = self.pE, self.pF

        cast_base = self.val("sDve")
        sq_base = self.val("sAct")
        pe_base = self.val("sPe")

        @blk.vector
        def _(v):
            for (s, val) in dve_waits:
                v.wait_ge(self.S(s), val)
            for k in range(DC):
                if k >= 4:
                    v.wait_ge(self.S("sPe"), pe_base + (k - 4) + 1)
                    v.wait_ge(self.S("sAct"), sq_base + (k - 4) + 1)
                self.inc(v.tensor_copy(XB(k), xsrc[:, k * 512:(k + 1) * 512]), "sDve")

        @blk.scalar
        def _(s):
            for (sm, val) in act_waits:
                s.wait_ge(self.S(sm), val)
            for k in range(DC):
                x2 = self.PT[:, (k % 4) * 1024:(k % 4) * 1024 + 512]
                s.wait_ge(self.S("sDve"), cast_base + k + 1)
                if k >= 4:
                    s.wait_ge(self.S("sPe"), pe_base + (k - 4) + 1)
                self.inc(s.activation(x2, XB(k), AF.Square), "sAct")

        @blk.tensor
        def _(t):
            for (sm, val) in pe_waits:
                t.wait_ge(self.S(sm), val)
            for k in range(DC):
                x2 = self.PT[:, (k % 4) * 1024:(k % 4) * 1024 + 512]
                t.wait_ge(self.S("sDve"), cast_base + k + 1)
                t.matmul(pSx[:, 0:512], self.onesb[:, 0:128], XB(k),
                         start=(k == 0), stop=(k == DC - 1))
                t.wait_ge(self.S("sAct"), sq_base + k + 1)
                self.inc(t.matmul(pSx2[:, 0:512], self.onesb[:, 0:128], x2,
                                  start=(k == 0), stop=(k == DC - 1)), "sPe")
        pe_done = self.val("sPe")

        SL = lambda i: small[:, i * 512:(i + 1) * 512]
        m_rep, var, sv, r_rep, A_rep = SL(0), SL(1), SL(2), SL(3), SL(4)

        @blk.vector
        def _(v):
            v.wait_ge(self.S("sPe"), pe_done)
            self.inc(v.tensor_scalar_mul(m_rep, pSx[:, 0:512], 1.0 / D), "sDve")
            self.wait(v, "sDve")
            self.inc(v.tensor_mul(SL(5), m_rep, m_rep), "sDve")
            self.wait(v, "sDve")
            self.inc(v.scalar_tensor_tensor(var, pSx2[:, 0:512], 1.0 / D, SL(5),
                                            ALU.mult, ALU.subtract), "sDve")
        var_done = self.val("sDve")

        @blk.scalar
        def _(s):
            s.wait_ge(self.S("sDve"), var_done)
            self.inc(s.activation(sv, var, AF.Ln, bias=eps), "sAct")
            self.wait(s, "sAct")
            self.inc(s.activation(r_rep, sv, AF.Exp, scale=-0.5), "sAct")
        r_done = self.val("sAct")

        @blk.vector
        def _(v):
            v.wait_ge(self.S("sAct"), r_done)
            v.wait_ge(self.S("sDve"), var_done)
            self.inc(v.scalar_tensor_tensor(A_rep, m_rep, -1.0, r_rep, ALU.mult, ALU.mult), "sDve")
            for k in range(DC):
                tmp = SL(5) if k % 2 == 0 else SL(6)
                self.wait(v, "sDve")
                self.inc(v.tensor_mul(tmp, xsrc[:, k * 512:(k + 1) * 512], r_rep), "sDve")
                self.wait(v, "sDve")
                self.inc(v.tensor_add(dst[:, k * 512:(k + 1) * 512], tmp, A_rep), "sDve")
        xn_done = self.val("sDve")

        if extra_dst_dram is not None:
            @blk.sync
            def _(sy):
                sy.wait_ge(self.S("sDve"), xn_done)
                dram_ap = bass.AP(extra_dst_dram.tensor, 0,
                                  [[512, 128], [65536, 32], [1, 512]])
                src_ap = bass.AP(dst, dst[0:1, 0:1].offset,
                                 [[dst[0:1, 0:1].ap[0][0], 128], [512, 32], [1, 512]])
                self.inc(sy.dma_start(dram_ap, src_ap), "cXnS")
        return xn_done

    def emit_coop_cast(self, blk):
        slabs = []
        for j in range(4):
            slabs.append((j * 128, 128, 1024, 0))
        for j in range(4):
            slabs.append((512 + j * 128, 128, 1024, 0))
        slabs.append((1024, 128, 1024, 0))
        for j in range(4):
            slabs.append((1152 + j * 512, 512, 4096, 1))
        for j in range(4):
            slabs.append((3200 + j * 512, 512, 4096, 1))
        inv = []

        w0_base = self.val("cAgW0")
        w1_base = self.val("cAgW1")

        na = sum(1 for s in slabs if s[3] == 0)

        @blk.gpsimd
        def _(g):
            for i, (r0, nr, w, grp) in enumerate(slabs):
                p = i % 2
                if i >= 2:
                    base = w0_base if p == 0 else w1_base
                    g.wait_ge(self.S(f"cAgW{p}"), base + 16 * (i // 2))
                srcw = bass.AP(self.wslabs.tensor, r0 * 1024, [[w, 128], [1, w]])
                inv.append(self.inc(g.dma_start(self.WS[:, p * 4096:p * 4096 + w], srcw),
                                    f"cAgC{p}"))
                if i == na - 1:
                    pass

        agaw = []
        @blk.scalar
        def _(s_):
            for i, (r0, nr, w, grp) in enumerate(slabs):
                p = i % 2
                s_.wait_ge(self.S(f"cAgC{p}"), inv[i])
                if grp == 0:
                    dst = bass.AP(self.ag2ain, r0 * 1024, [[w, 128], [1, w]])
                else:
                    dst = bass.AP(self.ag2bin, (r0 - 1152) * 1024, [[w, 128], [1, w]])
                self.inc(s_.dma_start(dst, self.WS[:, p * 4096:p * 4096 + w]), f"cAgW{p}")
                if i == na - 1:
                    agaw.append([("cAgW0", self.val("cAgW0")), ("cAgW1", self.val("cAgW1"))])
        ag2b_waits = [("cAgW0", self.val("cAgW0")), ("cAgW1", self.val("cAgW1"))]

        @blk.gpsimd
        def _(g):
            for (sm, vv) in agaw[0]:
                g.wait_ge(self.S(sm), vv)
            self.inc(g.collective_compute(
                "AllGather", ALU.bypass,
                replica_groups=[list(range(NCORE))],
                ins=[self.ag2ain.ap().opt()],
                outs=[self.ag2aout.ap().opt()]), "sCc")
            for (sm, vv) in ag2b_waits:
                g.wait_ge(self.S(sm), vv)
            self.inc(g.collective_compute(
                "AllGather", ALU.bypass,
                replica_groups=[list(range(NCORE))],
                ins=[self.ag2bin.ap().opt()],
                outs=[self.ag2bout.ap().opt()]), "sCc")

    def emit_weight_casts(self, blk, group, gate_pe=None):
        slabs = []
        for (src, dst, n, w) in group:
            for k in range(n):
                slabs.append((src, dst, k, w, src.tensor.name))

        @blk.gpsimd
        def _(g):
            if gate_pe is not None:
                g.wait_ge(self.S("sPe"), gate_pe)
            for i, (src, dst, k, w, nm) in enumerate(slabs):
                p = i % 2
                buf = self.WS[:, p * 4096:p * 4096 + w]
                if self.val(f"cWw{p}") > 0:
                    g.wait_ge(self.S(f"cWw{p}"), self.val(f"cWw{p}"))
                cv = self.inc(g.dma_start(buf, src[k * 128:(k + 1) * 128, 0:w]), f"cWc{p}")
                g.wait_ge(self.S(f"cWc{p}"), cv)
                self.inc(g.dma_start(dst[k, :, :], buf), f"cWw{p}")
                self.wdone[nm] = [("cWw0", self.val("cWw0")), ("cWw1", self.val("cWw1"))]

    def emit_setup(self, blk):
        nc = self.nc
        C = self.consts

        @blk.sync
        def _(sy):
            self.inc(sy.dma_start(C[:, 0:128], self.ident), "cLdC")
            self.inc(sy.dma_start(C[:, 128:144], self.ipat), "cLdC")
            self.inc(sy.dma_start(C[0:1, 150:151], bass.AP(self.gate.tensor, 0, [[1, 1], [1, 1]])), "cLdC")
            self.inc(sy.dma_start(C[0:16, 146:147], bass.AP(self.hres_b.tensor, 0, [[1, 16], [1, 1]])), "cLdC")
            self.inc(sy.dma_start(C[32:36, 146:147], bass.AP(self.hpre_b.tensor, 0, [[1, 4], [1, 1]])), "cLdC")
            self.inc(sy.dma_start(C[64:68, 146:147], bass.AP(self.hpost_b.tensor, 0, [[1, 4], [1, 1]])), "cLdC")
            for m in range(8):
                self.inc(sy.dma_start(self.bqs[:, m:m + 1],
                                      bass.AP(self.bqsel.tensor, m * 128, [[1, 128], [1, 1]])), "cLdC")
            for h in range(HKV):
                self.inc(sy.dma_start(C[:, 256 + h:257 + h],
                                      bass.AP(self.z_in.tensor, h * 128, [[1, 128], [1, 1]])), "cLdC")
            self.inc(sy.dma_start(self.selt_s[:, :], self.selt), "cLdC")
            self.inc(sy.dma_start(C[0:1, 384:408], self.baser), "cLdC")
            self.inc(sy.dma_start(self.R1[0:16, 0:4096], self.wres), "cLdC")
            self.inc(sy.dma_start(self.R1[32:36, 0:4096], self.wpre), "cLdC")
            self.inc(sy.dma_start(self.R1[64:68, 0:4096], self.wpost), "cLdC")
        ld_all = self.val("cLdC")

        if self.stage == "s0a":
            return
        @blk.gpsimd
        def _(g):
            for h in range(HKV):
                self.inc(g.dma_start(self.mz[:, h * 128:(h + 1) * 128], self.m_in[h]), "cMZ")
        self.m_done = self.val("cMZ")
        if self.stage == "s0b":
            return

        @blk.vector
        def _(v):
            v.memset(self.cbf[:, 0:128], 1.0)
            v.memset(C[0:1, 512:640], 1.0)
            v.memset(C[:, 148:149], LN_EPS)
            v.memset(C[:, 149:150], RMS_EPS)
            v.wait_ge(self.S("cLdC"), ld_all)
            v.tensor_copy(self.cbf[:, 128:256], C[:, 0:128])
            v.memzero(self.wrouter[:, :])
            for h in range(HKV):
                col = C[:, 256 + h:257 + h]
                colb = bass.AP(col.tensor, col.offset, [[col.ap[0][0], 128], [0, 128]])
                ins = v.tensor_copy(self.mz[:, 1024 + h * 128:1024 + (h + 1) * 128], colb)
            self.inc(ins, "sDve")
        setup_dve = self.val("sDve")
        self.onesb = self.cbf[:, 0:128]
        if self.stage == "s0c":
            self.identb = self.cbf[:, 128:256]
            self.ident_f = C[:, 0:128]
            return
        self.identb = self.cbf[:, 128:256]
        self.ident_f = C[:, 0:128]
        self.epsln = C[:, 148:149]
        self.epsrms = C[:, 149:150]

        pe_base = self.val("sPe")
        rt_evac0 = self.val("sDve")
        with (nc.psum_tensor("prtA", [128, 128], F32) as prtA,
              nc.psum_tensor("prtB", [128, 128], F32) as prtB):
            prts = [prtA, prtB]

            @blk.tensor
            def _(t):
                t.wait_ge(self.S("sDve"), setup_dve)
                for k in range(DC):
                    if k >= 2:
                        t.wait_ge(self.S("sDve"), rt_evac0 + 3 * (k - 2) + 3)
                    self.inc(t.transpose(prts[k % 2][:, :],
                                         self.R1[:, k * 128:(k + 1) * 128],
                                         self.ident_f), "sPe")

            @blk.vector
            def _(v):
                for k in range(DC):
                    v.wait_ge(self.S("sPe"), pe_base + k + 1)
                    pt = prts[k % 2]
                    v.tensor_copy(self.wrouter[:, k * 128:k * 128 + 16], pt[:, 0:16])
                    v.tensor_copy(self.wrouter[:, k * 128 + 32:k * 128 + 36], pt[:, 32:36])
                    self.inc(v.tensor_copy(self.wrouter[:, k * 128 + 64:k * 128 + 68], pt[:, 64:68]), "sDve", 3)
        self.router_done = self.val("sDve")

        with nc.psum_tensor("pst", [128, 512], F32) as pst:
            g0 = self.val("sDve")
            @blk.scalar
            def _(s):
                s.wait_ge(self.S("cLdC"), ld_all)
                self.inc(s.activation(C[0:1, 151:152], C[0:1, 150:151], AF.Exp, scale=-1.0), "sAct")
            ge = self.val("sAct")
            @blk.vector
            def _(v):
                v.wait_ge(self.S("sAct"), ge)
                self.inc(v.tensor_scalar_add(C[0:1, 152:153], C[0:1, 151:152], 1.0), "sDve")
                self.wait(v, "sDve")
                self.inc(v.reciprocal(C[0:1, 153:154], C[0:1, 152:153]), "sDve")
                self.wait(v, "sDve")
                self.inc(v.tensor_scalar(C[0:1, 154:155], C[0:1, 153:154], -1.0, 1.0,
                                         ALU.mult, ALU.add), "sDve")
            gdve = self.val("sDve")
            @blk.tensor
            def _(t):
                t.wait_ge(self.S("sDve"), gdve)
                t.matmul(pst[:, 0:1], self.onesb[0:1, 0:128].bitcast(BF16) if False else C[0:1, 512:640], C[0:1, 153:154], start=True, stop=True)
                t.matmul(pst[:, 4:5], C[0:1, 512:640], C[0:1, 154:155], start=True, stop=True)
                self.inc(t.matmul(pst[:, 8:32], C[0:1, 512:640], C[0:1, 384:408], start=True, stop=True), "sPe")
            gpe = self.val("sPe")
            @blk.vector
            def _(v):
                v.wait_ge(self.S("sPe"), gpe)
                v.tensor_copy(C[:, 144:145], pst[:, 0:1])
                v.tensor_copy(C[:, 145:146], pst[:, 4:5])
                v.tensor_copy(C[:, 416:440], pst[:, 8:32])
                self.inc(v.tensor_scalar_mul(C[:, 440:456], pst[:, 8:24], 10.0), "sDve")
            self.setup2_done = self.val("sDve")

    def emit_wprep(self, blk):
        nc = self.nc
        agin = self.agin.ap()
        with (nc.psum_tensor("pwpA", [128, 128], F32) as pwpA,
              nc.psum_tensor("pwpB", [128, 128], F32) as pwpB):
            pwps = [pwpA, pwpB]
            for kk in range(4):
                pe0 = self.val("sPe")
                dv0 = self.val("sDve")
                ldv = []

                @blk.sync
                def _(sy, kk=kk, pe0=pe0, ldv=ldv):
                    for mm in range(8):
                        if mm >= 4:
                            sy.wait_ge(self.S("sPe"), pe0 + mm - 4 + 1)
                        ldv.append(self.inc(sy.dma_start(
                            self.R1[:, 8192 + (mm % 4) * 128:8192 + (mm % 4) * 128 + 128],
                            self.wqsel_cols[mm * 128:(mm + 1) * 128, kk * 128:(kk + 1) * 128]),
                            f"cWq{mm % 4}"))

                @blk.tensor
                def _(t, kk=kk, dv0=dv0, ldv=ldv):
                    if kk == 0:
                        t.wait_ge(self.S("sDve"), self.setup2_done)
                    for mm in range(8):
                        t.wait_ge(self.S(f"cWq{mm % 4}"), ldv[mm])
                        if kk > 0 and mm == 0:
                            t.wait_ge(self.S("sDve"), dv0)
                        if mm >= 2:
                            t.wait_ge(self.S("sDve"), dv0 + mm - 2 + 1)
                        self.inc(t.transpose(pwps[mm % 2][:, :],
                                             self.R1[:, 8192 + (mm % 4) * 128:8192 + (mm % 4) * 128 + 128],
                                             self.ident_f), "sPe")

                @blk.vector
                def _(v, kk=kk, pe0=pe0):
                    if kk > 0:
                        v.wait_ge(self.S("cPrS"), self.val("cPrS"))
                    for mm in range(8):
                        v.wait_ge(self.S("sPe"), pe0 + mm + 1)
                        self.inc(v.tensor_copy(self.PT[:, mm * 128:(mm + 1) * 128],
                                               pwps[mm % 2][:, :]), "sDve")
                dv_done = self.val("sDve")

                @blk.sync
                def _(sy, kk=kk, dv_done=dv_done):
                    sy.wait_ge(self.S("sDve"), dv_done)
                    self.inc(sy.dma_start(agin[2048 + kk * 128:2048 + (kk + 1) * 128, :],
                                          self.PT[:, 0:1024]), "cPrS")

            for mc in range(4):
                pe0 = self.val("sPe")

                @blk.sync
                def _(sy, mc=mc):
                    self.inc(sy.dma_start(self.R1[:, 8192:8192 + 4096],
                                          self.wob_rows[mc * 128:(mc + 1) * 128, :]), "cWo")

                ld_done = self.val("cWo")

                @blk.vector
                def _(v, mc=mc, ld_done=ld_done, pe0=pe0):
                    v.wait_ge(self.S("cWo"), ld_done)
                    if mc >= 1:
                        v.wait_ge(self.S("sPe"), pe0)
                        v.wait_ge(self.S("cPoS"), self.val("cPoS"))
                    src = self.R1[:, 8192:8192 + 4096]
                    vw = lambda g: bass.AP(src.tensor, src.offset + g * 128,
                                           [[src.ap[0][0], 128], [512, 8], [1, 128]])
                    fold = self.R1[:, 12288:12288 + 1024]
                    self.inc(v.tensor_add(fold, vw(0), vw(1)), "sDve")
                    self.wait(v, "sDve")
                    self.inc(v.tensor_add(fold, fold, vw(2)), "sDve")
                    self.wait(v, "sDve")
                    self.inc(v.tensor_add(fold, fold, vw(3)), "sDve")
                fold_done = self.val("sDve")

                @blk.tensor
                def _(t, mc=mc, fold_done=fold_done):
                    t.wait_ge(self.S("sDve"), fold_done)
                    fold = self.R1[:, 12288:12288 + 1024]
                    for jc in range(8):
                        if jc >= 2:
                            t.wait_ge(self.S("sDve"), fold_done + jc - 2 + 1)
                        self.inc(t.transpose(pwps[jc % 2][:, :],
                                             fold[:, jc * 128:(jc + 1) * 128],
                                             self.ident_f), "sPe")

                @blk.vector
                def _(v, mc=mc, pe0=pe0):
                    for jc in range(8):
                        v.wait_ge(self.S("sPe"), pe0 + jc + 1)
                        self.inc(v.tensor_copy(self.PT[:, 1024 + jc * 128:1024 + (jc + 1) * 128],
                                               pwps[jc % 2][:, :]), "sDve")
                ev_done = self.val("sDve")

                @blk.sync
                def _(sy, mc=mc, ev_done=ev_done):
                    sy.wait_ge(self.S("sDve"), ev_done)
                    base = 2560 * 1024 + mc * 128
                    dst = bass.AP(agin.tensor, base, [[512, 128], [65536, 8], [1, 128]])
                    srcp = bass.AP(self.PT, self.PT[0:1, 1024:1025].offset,
                                   [[self.PT[0:1, 0:1].ap[0][0], 128], [128, 8], [1, 128]])
                    self.inc(sy.dma_start(dst, srcp), "cPoS")

    def emit_kv_proj(self, blk, half, xnsrc, xn_wait=None):
        agin = self.agin.ap()
        KVB = 24576
        R1b = self.R1[:, :].bitcast(BF16)
        for pr in range(2):
            s0, s1 = half * 4 + 2 * pr, half * 4 + 2 * pr + 1
            tb0, tb1 = 2 * pr, 2 * pr + 1
            pe0 = self.val("sPe")
            ev0 = self.val("sDve")
            prev_pe = self._kv_prev_pe
            slabv = []

            @blk.sync
            def _(sy, pr=pr, half=half, pe0=pe0, slabv=slabv, prev_pe=prev_pe):
                if pr == 0 and half == 0:
                    sy.wait_ge(self.S("sCc"), 1)
                if xn_wait is not None and pr == 0:
                    sy.wait_ge(self.S("sDve"), xn_wait)
                if prev_pe is not None:
                    sy.wait_ge(self.S("sPe"), prev_pe)
                for k in range(DC):
                    if k >= 3:
                        sy.wait_ge(self.S("sPe"), pe0 + (k - 3) + 1)
                    sl = KVB + (k % 3) * 2048
                    self.inc(sy.dma_start(R1b[:, sl:sl + 1024], self.w_slab("wk", k, 1024)), f"cSl{k % 3}")
                    slabv.append(self.inc(sy.dma_start(R1b[:, sl + 1024:sl + 2048], self.w_slab("wv", k, 1024)), f"cSl{k % 3}"))

            @blk.tensor
            def _(t, pr=pr, tb0=tb0, tb1=tb1, ev0=ev0, slabv=slabv):
                if xn_wait is not None:
                    t.wait_ge(self.S("sDve"), xn_wait)
                if pr == 1:
                    t.wait_ge(self.S("sDve"), ev0)
                for k in range(DC):
                    t.wait_ge(self.S(f"cSl{k % 3}"), slabv[k])
                    sl = KVB + (k % 3) * 2048
                    st, sp = (k == 0), (k == DC - 1)
                    xt0 = xnsrc[:, k * 512 + tb0 * 128:k * 512 + tb0 * 128 + 128]
                    xt1 = xnsrc[:, k * 512 + tb1 * 128:k * 512 + tb1 * 128 + 128]
                    t.matmul(self.pAB[:, 0:512], xt0, R1b[:, sl:sl + 512], start=st, stop=sp)
                    t.matmul(self.pAB[:, 512:1024], xt0, R1b[:, sl + 512:sl + 1024], start=st, stop=sp)
                    t.matmul(self.pCD[:, 0:512], xt0, R1b[:, sl + 1024:sl + 1536], start=st, stop=sp)
                    t.matmul(self.pCD[:, 512:1024], xt0, R1b[:, sl + 1536:sl + 2048], start=st, stop=sp)
                    t.matmul(self.pEF[:, 0:512], xt1, R1b[:, sl:sl + 512], start=st, stop=sp)
                    t.matmul(self.pEF[:, 512:1024], xt1, R1b[:, sl + 512:sl + 1024], start=st, stop=sp)
                    t.matmul(self.pGH[:, 0:512], xt1, R1b[:, sl + 1024:sl + 1536], start=st, stop=sp)
                    ins = t.matmul(self.pGH[:, 512:1024], xt1, R1b[:, sl + 1536:sl + 2048], start=st, stop=sp)
                    self.inc(ins, "sPe")
            self._kv_prev_pe = self.val("sPe")
            pe_done = self.val("sPe")

            @blk.vector
            def _(v, s0=s0, s1=s1, pe_done=pe_done):
                v.wait_ge(self.S("sPe"), pe_done)
                v.wait_ge(self.S("cVs0"), self.val("cVs0"))
                v.wait_ge(self.S("cVs1"), self.val("cVs1"))
                v.tensor_copy(self.QB[:, s0 * 1024:(s0 + 1) * 1024], self.pAB[:, :])
                self.inc(v.tensor_copy(self.PT[:, 2048:3072], self.pCD[:, :]), "sDve")
                v.tensor_copy(self.QB[:, s1 * 1024:(s1 + 1) * 1024], self.pEF[:, :])
                self.inc(v.tensor_copy(self.PT[:, 3072:4096], self.pGH[:, :]), "sDve")

            dv_now = self.val("sDve")
            @blk.sync
            def _(sy, s0=s0, s1=s1, dv_now=dv_now):
                sy.wait_ge(self.S("sDve"), dv_now - 1)
                self.inc(sy.dma_start(agin[1024 + s0 * 128:1024 + (s0 + 1) * 128, :],
                                      self.PT[:, 2048:3072]), "cVs0")
                sy.wait_ge(self.S("sDve"), dv_now)
                self.inc(sy.dma_start(agin[1024 + s1 * 128:1024 + (s1 + 1) * 128, :],
                                      self.PT[:, 3072:4096]), "cVs1")

    def emit_k_transpose(self, blk):
        nc = self.nc
        agin = self.agin.ap()
        with nc.psum_tensor("ptrb", [128, 1024], BF16) as ptrb:
            pe0 = self.val("sPe")
            dv0 = self.val("sDve")
            stv = []

            @blk.tensor
            def _(t):
                for s in range(NSLOT):
                    if s == 0:
                        t.wait_ge(self.S("sDve"), dv0)
                    if s >= 1:
                        t.wait_ge(self.S("sDve"), dv0 + s)
                    for h in range(HKV):
                        self.inc(t.transpose(ptrb[:, h * 128:(h + 1) * 128],
                                             self.QB[:, s * 1024 + h * 128:s * 1024 + (h + 1) * 128],
                                             self.identb), "sPe")

            @blk.sync
            def _(sy):
                for s in range(NSLOT):
                    sy.wait_ge(self.S("sDve"), dv0 + s + 1)
                    stv.append(self.inc(sy.dma_start(agin[s * 128:(s + 1) * 128, :],
                                                     self.PT[:, (s % 2) * 1024:(s % 2) * 1024 + 1024]),
                                        f"cKb{s % 2}"))

            @blk.vector
            def _(v):
                for s in range(NSLOT):
                    v.wait_ge(self.S("sPe"), pe0 + 8 * (s + 1))
                    if s >= 2:
                        v.wait_ge(self.S(f"cKb{s % 2}"), stv[s - 2])
                    self.inc(v.tensor_copy(self.PT[:, (s % 2) * 1024:(s % 2) * 1024 + 1024],
                                           ptrb[:, :]), "sDve")

    def emit_qproj(self, blk, xnsrc, qp, xn_wait=None, extra_pe_waits=()):
        pe0 = self.val("sPe")
        dv0 = self.val("sDve")
        wv = []

        @blk.sync
        def _(sy):
            sy.wait_ge(self.S("sCc"), 2)
            i = 0
            for oct_ in range(4):
                for kb in range(8):
                    if i >= 2:
                        sy.wait_ge(self.S("sPe"), pe0 + (i - 1))
                    srcw = self.w_window("wq", oct_, kb)
                    wv.append(self.inc(sy.dma_start(
                        self.WS[:, (i % 2) * 4096:(i % 2) * 4096 + 4096], srcw), f"cQw{i % 2}"))
                    i += 1

        @blk.tensor
        def _(t):
            if xn_wait is not None:
                t.wait_ge(self.S("sDve"), xn_wait)
            for (sm, vvv) in extra_pe_waits:
                t.wait_ge(self.S(sm), vvv)
            i = 0
            for oct_ in range(4):
                if oct_ >= 1:
                    t.wait_ge(self.S("sDve"), dv0 + 8 * oct_)
                for kb in range(8):
                    t.wait_ge(self.S(f"cQw{i % 2}"), wv[i])
                    base = (i % 2) * 4096
                    for m in range(8):
                        for k in range(4):
                            st = (kb == 0 and k == 0)
                            sp = (kb == 7 and k == 3)
                            ins = t.matmul(qp[:, m * 512:(m + 1) * 512],
                                           self.WS[:, base + k * 1024 + m * 128:base + k * 1024 + (m + 1) * 128],
                                           xnsrc[:, (kb * 4 + k) * 512:(kb * 4 + k) * 512 + 512],
                                           start=st, stop=sp)
                    self.inc(ins, "sPe")
                    i += 1

        @blk.vector
        def _(v):
            for oct_ in range(4):
                v.wait_ge(self.S("sPe"), pe0 + 8 * (oct_ + 1))
                for m in range(8):
                    self.inc(v.tensor_copy(
                        self.QB[:, (oct_ * 8 + m) * 512:(oct_ * 8 + m) * 512 + 512],
                        qp[:, m * 512:(m + 1) * 512]), "sDve")

    def emit_prologue(self, blk):
        nc = self.nc
        self.emit_coop_cast(blk)
        self.emit_setup(blk)
        if self.stage in ("s1", "s0a", "s0b", "s0c"):
            return
        self.emit_wprep(blk)
        if self.stage == "s2":
            return

        wprep_pe = self.val("sPe")
        wprep_dve = self.val("sDve")

        with (nc.psum_tensor("pAB", [128, 1024], F32) as pAB,
              nc.psum_tensor("pCD", [128, 1024], F32) as pCD,
              nc.psum_tensor("pEF", [128, 1024], F32) as pEF,
              nc.psum_tensor("pGH", [128, 1024], F32) as pGH):
            self.pAB, self.pCD, self.pEF, self.pGH = pAB, pCD, pEF, pGH
            self.pE, self.pF = pEF[:, 0:512], pEF[:, 512:1024]

            @blk.sync
            def _(sy):
                sy.wait_ge(self.S("sPe"), wprep_pe)
                sy.wait_ge(self.S("sDve"), wprep_dve)
                src = bass.AP(self.xin.tensor, 0, [[1024, 128], [131072, 32], [1, 512]])
                self.inc(sy.dma_start(self.R1[:, :], src), "cX0")
            x0_ld = self.val("cX0")

            @blk.vector
            def _(v):
                v.wait_ge(self.S("cX0"), x0_ld)
            pw = [("cPrS", self.val("cPrS")), ("cPoS", self.val("cPoS"))]
            self.emit_ln(blk, self.R1, self.AO, self.epsln,
                         act_waits=pw, dve_waits=pw)
            xn0b0_done = self.val("sDve")
            if self.stage == "s3":
                return
            self.emit_kv_proj(blk, 0, self.AO, xn_wait=xn0b0_done)
            if self.stage == "s4":
                return
            kvb0_evac = self.val("sDve")
            kv_waits = [("cVs0", self.val("cVs0")), ("cVs1", self.val("cVs1"))]

            kv0_pe = self._kv_prev_pe
            @blk.sync
            def _(sy):
                sy.wait_ge(self.S("sDve"), xn0b0_done)
                sy.wait_ge(self.S("sPe"), kv0_pe)
                src = bass.AP(self.xin.tensor, 512, [[1024, 128], [131072, 32], [1, 512]])
                self.inc(sy.dma_start(self.R1[:, :], src), "cX0")
            x1_ld = self.val("cX0")

            @blk.vector
            def _(v):
                v.wait_ge(self.S("cX0"), x1_ld)
            self.emit_ln(blk, self.R1, self.XN, self.epsln, extra_dst_dram=self.xn0b1,
                         act_waits=kv_waits, dve_waits=kv_waits,
                         pe_waits=[("sDve", kvb0_evac)])

            xn0b1_done = self.val("sDve")
            self.emit_kv_proj(blk, 1, self.XN, xn_wait=xn0b1_done)

        if self.stage == "s5":
            return
        self.emit_k_transpose(blk)
        ktr_dve = self.val("sDve")
        if self.stage == "s6":
            return

        if self.stage == "s6b":
            return
        ag_waits = [(s, self.val(s)) for s in ["cPrS", "cPoS", "cVs0", "cVs1", "cKb0", "cKb1"]]
        @blk.gpsimd
        def _(g):
            for (sm, vv) in ag_waits:
                g.wait_ge(self.S(sm), vv)
            self.inc(g.collective_compute(
                "AllGather", ALU.bypass,
                replica_groups=[list(range(NCORE))],
                ins=[self.agin.ap().opt()],
                outs=[self.agout.ap().opt()]), "sCc")
        if self.stage == "s7":
            return
        with nc.psum_tensor("qp", [128, 4096], F32) as qp:
            self.emit_qproj(blk, self.AO, qp, xn_wait=ktr_dve)

    def emit_debug_stores(self, stores):
        nc = self.nc
        self.dbg_outputs_names = [s[0] for s in stores]
        with nc.Block() as blk:
            @blk.sync
            def _(sy):
                for (name, region, shape, dt) in stores:
                    ap = self.nc.dram_tensor(name, shape, dt, kind="ExternalOutput").ap()
                    self.inc(sy.dma_start(ap, region), "cDbg")
                sy.wait_ge(self.S("cDbg"), self.val("cDbg"))

    def build(self):
        nc = self.nc
        if self.stage == "full":
            nc2 = self.build_full()
            self.dbg_outputs_names = ["yout"]
            self.es.close()
            return nc2
        with nc.Block() as blk:
            self.emit_prologue(blk)
        if self.stage.startswith("s"):
            self.emit_debug_stores([
                ("d_mark", self.consts[:, 0:128], [128, 128], F32),
            ])
            self.es.close()
            return nc
        if self.stage == "xn0":
            self.emit_debug_stores([
                ("d_xn0b0", self.AO[:, :], [128, 16384], BF16),
                ("d_xn0b1", self.XN[:, :], [128, 16384], BF16),
            ])
            self.es.close()
            return nc
        if self.stage == "att":
            with nc.Block() as blk2:
                self.emit_attention(blk2, 0, q_ready=self.val("sDve"), wait_cc=True)
            self.emit_debug_stores([
                ("d_ao", self.AO[:, :], [128, 16384], BF16),
            ])
            self.es.close()
            return nc
        if self.stage == "kv":
            with nc.Block() as blk2:
                @blk2.gpsimd
                def _(g):
                    g.wait_ge(self.S("sCc"), self.val("sCc"))
            self.emit_debug_stores([
                ("d_agin", self.agin.ap(), [3072, 1024], BF16),
                ("d_q", self.QB[:, :], [128, 16384], BF16),
            ])
            self.es.close()
            return nc
        if self.stage == "full":
            nc2 = self.build_full()
            self.dbg_outputs_names = ["yout"]
            self.es.close()
            return nc2
        if self.stage in ("x1", "x2", "x3", "x4"):
            with nc.Block() as blk:
                pass
            self.es.close()
            return nc
        raise NotImplementedError(self.stage)

    def emit_attention(self, blk, half, q_ready, wait_cc=False, r1_wait=None):
        _sfx = str(half)
        nc = self.nc
        R1b = self.R1[:, :].bitcast(BF16)
        XNf = self.XN[:, :].bitcast(F32)
        AVACC = XNf[:, 0:4096]
        DEN = XNf[0:1, 4096:8192]
        PTf = self.PT[:, :].bitcast(F32)

        xnst = self.val("cXnS")
        kv_ctr = 0
        kv_gate = [None, None]
        ebank_prev = None
        ev_last = None
        last_bc_pe = [None, None]
        last_bc_tt = [None, None]
        first_pe_wait = q_ready
        RSTRIDE = 3072 * 1024

        with (nc.psum_tensor("pSCa" + _sfx, [128, 1024], F32) as pSCa,
              nc.psum_tensor("pSCb" + _sfx, [128, 1024], F32) as pSCb,
              nc.psum_tensor("pav" + _sfx, [128, 512], F32) as pav,
              nc.psum_tensor("pden" + _sfx, [128, 512], F32) as pden,
              nc.psum_tensor("pbc" + _sfx, [128, 512], F32) as pbc,
              nc.psum_tensor("pbcB" + _sfx, [128, 512], F32) as pbcB):
            pSC = [pSCa, pSCb]
            pbc2 = [pbc, pbcB]

            def emit_av(t, prev, ev2, jlist):
                for j in jlist:
                    jp = j // 2
                    if prev["lastg"]:
                        t.wait_ge(self.S("sDve"),
                                  max(prev["msk_base"] + j + 1,
                                      ev2 if (j == 0 and ev2 is not None) else 0))
                    else:
                        if j % 2 == 0:
                            t.wait_ge(self.S("sAct"), prev["exp_base"] + jp + 1)
                        if j == 0 and ev2 is not None:
                            t.wait_ge(self.S("sDve"), ev2)
                    ro = j if prev["b"] == 0 else 7 - j
                    lhsT = R1b[:, prev["vbase"] + ro * 1024 + prev["h"] * 128:
                               prev["vbase"] + ro * 1024 + (prev["h"] + 1) * 128]
                    pt_t = self.PT[:, j * 512:(j + 1) * 512]
                    t.matmul(pav[:, :], lhsT, pt_t, start=(j == 0), stop=(j == 7))
                    self.inc(t.matmul(pden[:, :], self.onesb[:, 0:128], pt_t,
                                      start=(j == 0), stop=(j == 7)), "sPe")

            def emit_evac(prev, av_end):
                @blk.vector
                def _(v, prev=prev, av_end=av_end):
                    if prev["first_evac"]:
                        v.wait_ge(self.S("cXnS"), xnst)
                    v.wait_ge(self.S("sPe"), av_end)
                    dst = AVACC[:, prev["h"] * 512:(prev["h"] + 1) * 512]
                    dden = DEN[0:1, prev["h"] * 512:(prev["h"] + 1) * 512]
                    self.wait(v, "sDve")
                    if prev["g"] == 0:
                        self.inc(v.tensor_copy(dst, pav[:, :]), "sDve")
                        self.inc(v.tensor_copy(dden, pden[0:1, :]), "sDve")
                    else:
                        self.inc(v.tensor_add(dst, dst, pav[:, :]), "sDve")
                        self.inc(v.tensor_add(dden, dden, pden[0:1, :]), "sDve")
                return self.val("sDve")

            for si in range(4):
                s = half * 4 + si
                b = half
                ngrp = NGRP[s]

                @blk.sync
                def _(sy, s=s, si=si):
                    if wait_cc and si == 0:
                        sy.wait_ge(self.S("sCc"), 3)
                    sy.wait_ge(self.S("sDve"), self.val("sDve"))
                    self.inc(sy.dma_start(
                        self.mask8[:, :],
                        bass.AP(self.maskt.tensor, s * 131072, [[1024, 128], [1, 1024]])),
                        "cMsk")
                mask_ld = self.val("cMsk")

                prev = None
                for g in range(ngrp):
                    p = kv_ctr % 2
                    kv_ctr += 1
                    kbase = p * 16384
                    vbase = p * 16384 + 8192
                    lastg = (g == ngrp - 1)
                    slotp = g if b == 0 else 4 + g

                    gate = kv_gate[p]
                    first_kv = (si == 0 and g == 0)
                    @blk.sync
                    def _(sy, kbase=kbase, vbase=vbase, slotp=slotp, gate=gate,
                          first_kv=first_kv):
                        if first_kv and r1_wait is not None:
                            sy.wait_ge(self.S(r1_wait[0]), r1_wait[1])
                        if gate is not None:
                            sy.wait_ge(self.S("sPe"), gate)
                        kdst = R1b[:, kbase:kbase + 8192].rearrange("p (r c) -> p r c", r=8)
                        vdst = R1b[:, vbase:vbase + 8192].rearrange("p (r c) -> p r c", r=8)
                        ksrc = bass.AP(self.agout, slotp * 128 * 1024,
                                       [[1024, 128], [RSTRIDE, 8], [1, 1024]])
                        vsrc = bass.AP(self.agout, 1024 * 1024 + slotp * 128 * 1024,
                                       [[1024, 128], [RSTRIDE, 8], [1, 1024]])
                        self.inc(sy.dma_start(kdst, ksrc), f"cKV{p}")
                        self.inc(sy.dma_start(vdst, vsrc), f"cKV{p}")
                    kv_ld = self.val(f"cKV{p}")

                    for h in range(HKV):
                        pe0 = self.val("sPe")
                        ebu = self.val("sAct")

                        @blk.tensor
                        def _(t, si=si, b=b, h=h, kbase=kbase, kv_ld=kv_ld, p=p,
                              prev=prev, ev2=ev_last, ebu=ebu, ebp=ebank_prev,
                              fpw=first_pe_wait, h0=(h == 0)):
                            if fpw is not None:
                                t.wait_ge(self.S("sDve"), fpw)
                            if h0:
                                t.wait_ge(self.S(f"cKV{p}"), kv_ld)

                            def sc_pair(jp):
                                bank = jp % 2
                                if jp >= 2:
                                    t.wait_ge(self.S("sAct"), ebu + (jp - 2) + 1)
                                elif ebp is not None:
                                    t.wait_ge(self.S("sAct"), ebp + 3 + jp)
                                for jj in range(2):
                                    j = 2 * jp + jj
                                    ro = j if b == 0 else 7 - j
                                    lhsT = R1b[:, kbase + ro * 1024 + h * 128:
                                               kbase + ro * 1024 + (h + 1) * 128]
                                    rhs = bass.AP(self.QB, (4 * h) * 512 + si * 128,
                                                  [[self.QB[0:1, 0:1].ap[0][0], 128], [512, 4], [1, 128]])
                                    self.inc(t.matmul(pSC[bank][:, jj * 512:(jj + 1) * 512],
                                                      lhsT, rhs, start=True, stop=True), "sPe")

                            if prev is None:
                                for jp in range(4):
                                    sc_pair(jp)
                            else:
                                emit_av(t, prev, ev2, [0, 1, 2, 3, 4, 5])
                                sc_pair(0)
                                emit_av(t, prev, ev2, [6, 7])
                                sc_pair(1)
                                sc_pair(2)
                                sc_pair(3)
                        first_pe_wait = None
                        if prev is not None and prev["h"] == HKV - 1:
                            kv_gate[prev["p"]] = pe0 + 10

                        @blk.scalar
                        def _(s_, pe0=pe0, gates=((8, 12, 14, 16) if prev is not None
                                                  else (2, 4, 6, 8))):
                            for jp in range(4):
                                s_.wait_ge(self.S("sPe"), pe0 + gates[jp])
                                self.inc(s_.activation(self.PT[:, jp * 1024:(jp + 1) * 1024],
                                                       pSC[jp % 2][:, :], AF.Exp,
                                                       scale=ATT_SCALE), "sAct")
                        ebank_prev = ebu

                        msk_base = self.val("sDve")
                        if lastg:
                            @blk.vector
                            def _(v, ebu=ebu, mask_ld=mask_ld):
                                v.wait_ge(self.S("cMsk"), mask_ld)
                                for j in range(8):
                                    jp = j // 2
                                    v.wait_ge(self.S("sAct"), ebu + jp + 1)
                                    pt_t = self.PT[:, j * 512:(j + 1) * 512]
                                    mview = bass.AP(self.mask8,
                                                    self.mask8[0:1, 0:1].offset + j * 128,
                                                    [[self.mask8[0:1, 0:1].ap[0][0], 128], [0, 4], [1, 128]])
                                    self.inc(v.tensor_mul(pt_t, pt_t, mview), "sDve")

                        if prev is not None:
                            ev_last = emit_evac(prev, pe0 + 10)

                        prev = {"h": h, "g": g, "p": p, "b": b, "vbase": vbase,
                                "lastg": lastg, "msk_base": msk_base, "exp_base": ebu,
                                "first_evac": (si == 0 and g == 0 and h == 0)}

                fl0 = self.val("sPe")
                @blk.tensor
                def _(t, prev=prev, ev2=ev_last):
                    emit_av(t, prev, ev2, [0, 1, 2, 3, 4, 5, 6, 7])
                kv_gate[prev["p"]] = fl0 + 8
                ev_last = emit_evac(prev, fl0 + 8)

                for h in range(HKV):
                    q = h % 2
                    @blk.vector
                    def _(v, h=h, q=q, lbp=last_bc_pe[q]):
                        if lbp is not None:
                            v.wait_ge(self.S("sPe"), lbp)
                        self.wait(v, "sDve")
                        self.inc(v.reciprocal(PTf[0:1, q * 512:q * 512 + 512],
                                              DEN[0:1, h * 512:(h + 1) * 512]), "sDve")
                    rc_done = self.val("sDve")

                    @blk.tensor
                    def _(t, q=q, rc_done=rc_done, lbt=last_bc_tt[q]):
                        t.wait_ge(self.S("sDve"), rc_done)
                        if lbt is not None:
                            t.wait_ge(self.S("sDve"), lbt)
                        self.inc(t.matmul(pbc2[q][:, :], self.consts[0:1, 512:640],
                                          PTf[0:1, q * 512:q * 512 + 512],
                                          start=True, stop=True), "sPe")
                    last_bc_pe[q] = self.val("sPe")

                    @blk.vector
                    def _(v, h=h, si=si, q=q, lbp=last_bc_pe[q]):
                        v.wait_ge(self.S("sPe"), lbp)
                        out = bass.AP(self.AO, (4 * h) * 512 + si * 128,
                                      [[self.AO[0:1, 0:1].ap[0][0], 128], [512, 4], [1, 128]])
                        self.inc(v.tensor_mul(out, AVACC[:, h * 512:(h + 1) * 512],
                                              pbc2[q][:, :]), "sDve")
                    last_bc_tt[q] = self.val("sDve")
        return self.val("sDve")

    def emit_oproj(self, blk, half, pp, ao_ready, r1_free_pe):
        wv = []
        pe0 = self.val("sPe")
        dv0 = self.val("sDve")

        @blk.sync
        def _(sy):
            sy.wait_ge(self.S("sPe"), r1_free_pe)
            src = bass.AP(self.xin.tensor, half * 512,
                          [[1024, 128], [131072, 32], [1, 512]])
            self.inc(sy.dma_start(self.R1[:, :], src), "cX0")
            i = 0
            for oct_ in range(4):
                for kb in range(8):
                    if i >= 2:
                        sy.wait_ge(self.S("sPe"), pe0 + (i - 1))
                    srcw = self.w_window("wo", oct_, kb)
                    wv.append(self.inc(sy.dma_start(
                        self.WS[:, (i % 2) * 4096:(i % 2) * 4096 + 4096], srcw), f"cQw{i % 2}"))
                    i += 1
        x0_ld = self.val("cX0")

        @blk.tensor
        def _(t):
            t.wait_ge(self.S("sDve"), ao_ready)
            i = 0
            for oct_ in range(4):
                if oct_ >= 1:
                    t.wait_ge(self.S("sDve"), dv0 + 8 * oct_)
                for kb in range(8):
                    t.wait_ge(self.S(f"cQw{i % 2}"), wv[i])
                    base = (i % 2) * 4096
                    for m in range(8):
                        for k in range(4):
                            st = (kb == 0 and k == 0)
                            sp = (kb == 7 and k == 3)
                            ins = t.matmul(pp[:, m * 512:(m + 1) * 512],
                                           self.WS[:, base + k * 1024 + m * 128:base + k * 1024 + (m + 1) * 128],
                                           self.AO[:, (kb * 4 + k) * 512:(kb * 4 + k) * 512 + 512],
                                           start=st, stop=sp)
                    self.inc(ins, "sPe")
                    i += 1

        @blk.vector
        def _(v):
            v.wait_ge(self.S("cX0"), x0_ld)
            for oct_ in range(4):
                v.wait_ge(self.S("sPe"), pe0 + 8 * (oct_ + 1))
                for m in range(8):
                    mg = oct_ * 8 + m
                    r1c = self.R1[:, mg * 512:(mg + 1) * 512]
                    self.inc(v.tensor_add(r1c, r1c, pp[:, m * 512:(m + 1) * 512]), "sDve")
        return self.val("sDve")

    def emit_ffn(self, blk, half, pp, xn_ready, pp_free_dve, ws_free_pe=0):
        self.ws_free_pe = ws_free_pe
        self.y_ev = []
        self.bb_free = 0
        C = self.consts
        small = self.small
        RT, RMS = pp[:, 0:512], pp[:, 512:1024]
        TRa, TRb = pp[:, 1024:1152], pp[:, 1536:1664]
        BC, Y = pp[:, 2048:2560], pp[:, 2560:3584]
        T_in, T_r = small[:, 0:512], small[:, 512:1024]
        SC = lambda a, b: small[:, a:b]
        Z = SC(1024, 1088)
        UP, VP = SC(1088, 1104), SC(1104, 1120)
        AT = SC(1152, 1216)
        ET = SC(1216, 1280)
        RC = SC(1280, 1284)
        SS = SC(1288, 1304)
        HH = SC(1312, 1376)
        B1 = SC(1376, 1440)
        PRER, POSTR = SC(1440, 1456), SC(1456, 1472)
        PREE = SC(1472, 1488)
        PRES, PRERS = SC(1488, 1492), SC(1492, 1496)
        HPRE, HPOST = SC(1496, 1512), SC(1512, 1528)
        POSTE = SC(1528, 1544)
        POSTS, POSTRS = SC(1544, 1548), SC(1548, 1552)
        Bt_in = small[:, 1536 + 512:1536 + 1024] if False else small[:, 2048:2560]
        XPRE = self.AO[:, 0:4096]
        YB = self.AO[:, 4096:8192]
        BB = lambda r: self.AO[:, 8192 + r * 512:8192 + (r + 1) * 512]
        BROWS = self.PT[:, 3584:4096]

        pe0 = self.val("sPe")
        act0 = self.val("sAct")

        @blk.scalar
        def _(s_):
            s_.wait_ge(self.S("sDve"), xn_ready)
            for k in range(DC):
                x2t = self.PT[:, 2048 + (k % 2) * 512:2048 + (k % 2) * 512 + 512]
                if k >= 2:
                    s_.wait_ge(self.S("sPe"), pe0 + (k - 2) + 1)
                self.inc(s_.activation(x2t, self.XN[:, k * 512:(k + 1) * 512], AF.Square),
                         "sAct")

        @blk.tensor
        def _(t):
            t.wait_ge(self.S("sDve"), pp_free_dve)
            for k in range(DC):
                x2t = self.PT[:, 2048 + (k % 2) * 512:2048 + (k % 2) * 512 + 512]
                t.wait_ge(self.S("sAct"), act0 + k + 1)
                t.matmul(RT, self.wrouter[:, k * 128:(k + 1) * 128],
                         self.XN[:, k * 512:(k + 1) * 512],
                         start=(k == 0), stop=(k == DC - 1))
                self.inc(t.matmul(RMS, self.onesb[:, 0:128], x2t,
                                  start=(k == 0), stop=(k == DC - 1)), "sPe")
        rt_done = self.val("sPe")

        @blk.vector
        def _(v):
            v.wait_ge(self.S("sPe"), rt_done)
            self.wait(v, "sDve")
            v.tensor_copy(T_in[0:16, :], RT[0:16, :])
            v.tensor_copy(T_in[32:36, :], RT[32:36, :])
            v.tensor_copy(T_in[64:68, :], RT[64:68, :])
            self.inc(v.tensor_copy(T_r[0:1, :], RMS[0:1, :]), "sDve")
        tin_done = self.val("sDve")

        tr_dve = []
        for a in range(4):
            pe_a = self.val("sPe")
            @blk.tensor
            def _(t, a=a, tin_done=tin_done):
                t.wait_ge(self.S("sDve"), tin_done)
                if a >= 1:
                    t.wait_ge(self.S("sDve"), tr_dve[a - 1])
                t.transpose(TRa, T_in[:, a * 128:(a + 1) * 128], self.ident_f)
                self.inc(t.transpose(TRb, T_r[:, a * 128:(a + 1) * 128], self.ident_f), "sPe")
            pe_a1 = self.val("sPe")

            @blk.scalar
            def _(s_, a=a, pe_a1=pe_a1):
                s_.wait_ge(self.S("sPe"), pe_a1)
                self.inc(s_.activation(RC[:, a:a + 1], bass.AP(TRb.tensor, TRb.offset, [[TRb.ap[0][0], 128], [1, 1]]),
                                       AF.Ln, bias=C[:, 149:150], scale=1.0 / D), "sAct")
                self.wait(s_, "sAct")
                self.inc(s_.activation(RC[:, a:a + 1], RC[:, a:a + 1], AF.Exp, scale=-0.5), "sAct")
            rc_done = self.val("sAct")

            @blk.vector
            def _(v, a=a, rc_done=rc_done):
                v.wait_ge(self.S("sAct"), rc_done)
                self.inc(v.tensor_scalar(Z[:, a * 16:(a + 1) * 16],
                                         bass.AP(TRa.tensor, TRa.offset, [[TRa.ap[0][0], 128], [1, 16]]),
                                         RC[:, a:a + 1], 10.0, ALU.mult, ALU.mult), "sDve")
                self.wait(v, "sDve")
                self.inc(v.tensor_add(Z[:, a * 16:(a + 1) * 16], Z[:, a * 16:(a + 1) * 16],
                                      C[:, 440:456]), "sDve")
                self.inc(v.tensor_scalar(PRER[:, a * 4:(a + 1) * 4],
                                         bass.AP(TRa.tensor, TRa.offset + 32, [[TRa.ap[0][0], 128], [1, 4]]),
                                         RC[:, a:a + 1], None, ALU.mult), "sDve")
                self.wait(v, "sDve")
                self.inc(v.tensor_add(PRER[:, a * 4:(a + 1) * 4], PRER[:, a * 4:(a + 1) * 4],
                                      C[:, 432:436]), "sDve")
                self.inc(v.tensor_scalar(POSTR[:, a * 4:(a + 1) * 4],
                                         bass.AP(TRa.tensor, TRa.offset + 64, [[TRa.ap[0][0], 128], [1, 4]]),
                                         RC[:, a:a + 1], None, ALU.mult), "sDve")
                self.wait(v, "sDve")
                self.inc(v.tensor_add(POSTR[:, a * 4:(a + 1) * 4], POSTR[:, a * 4:(a + 1) * 4],
                                      C[:, 436:440]), "sDve")
            tr_dve.append(self.val("sDve"))
        z_ready0 = self.val("sDve")

        zs = Z[0:1, 0:1]
        zst = zs.ap[0][0]
        MU = SC(1552, 1568)
        mu_aij = bass.AP(Z.tensor, MU.offset, [[zst, 128], [4, 4], [1, 4], [0, 4]])
        z_aij = bass.AP(Z.tensor, Z.offset, [[zst, 128], [16, 4], [4, 4], [1, 4]])
        z_aji = bass.AP(Z.tensor, Z.offset, [[zst, 128], [16, 4], [1, 4], [4, 4]])
        et_red = bass.AP(Z.tensor, ET.offset, [[zst, 128], [16, 4], [4, 4], [1, 4]])
        up_aij = bass.AP(Z.tensor, UP.offset, [[zst, 128], [4, 4], [1, 4], [0, 4]])
        up_aji = bass.AP(Z.tensor, UP.offset, [[zst, 128], [4, 4], [0, 4], [1, 4]])
        vp_aij = bass.AP(Z.tensor, VP.offset, [[zst, 128], [4, 4], [0, 4], [1, 4]])

        @blk.vector
        def _(v):
            v.wait_ge(self.S("sDve"), z_ready0)
            self.inc(v.reduce_max(MU[:, :], z_aij, axis=AX.X), "sDve")
            self.wait(v, "sDve")
            self.inc(v.tensor_tensor(Z[:, :], Z[:, :], mu_aij, ALU.subtract), "sDve")
        z_ready = self.val("sDve")

        for it in range(50):
            if it > 0:
                @blk.vector
                def _(v, w=lnv_done):
                    v.wait_ge(self.S("sAct"), w)
                    self.inc(v.tensor_tensor(ET[:, :], z_aij, vp_aij, ALU.subtract), "sDve")
            as_done = self.val("sDve")

            @blk.scalar
            def _(s_, it=it, as_done=as_done):
                s_.wait_ge(self.S("sDve"), z_ready if it == 0 else as_done)
                src = Z[:, :] if it == 0 else ET[:, :]
                self.inc(s_.activation(ET[:, :], src, AF.Exp), "sAct")
            e_done = self.val("sAct")

            @blk.vector
            def _(v, e_done=e_done):
                v.wait_ge(self.S("sAct"), e_done)
                self.inc(v.reduce_sum(SS[:, :], et_red, axis=AX.X), "sDve")
            s_done = self.val("sDve")

            @blk.scalar
            def _(s_, s_done=s_done):
                s_.wait_ge(self.S("sDve"), s_done)
                self.inc(s_.activation(UP[:, :], SS[:, :], AF.Ln), "sAct")
            lns_done = self.val("sAct")

            @blk.vector
            def _(v, lns_done=lns_done):
                v.wait_ge(self.S("sAct"), lns_done)
                self.inc(v.tensor_tensor(ET[:, :], z_aji, up_aji, ALU.subtract), "sDve")
            as2_done = self.val("sDve")

            @blk.scalar
            def _(s_, as2_done=as2_done):
                s_.wait_ge(self.S("sDve"), as2_done)
                self.inc(s_.activation(ET[:, :], ET[:, :], AF.Exp), "sAct")
            e2_done = self.val("sAct")

            @blk.vector
            def _(v, e2_done=e2_done):
                v.wait_ge(self.S("sAct"), e2_done)
                self.inc(v.reduce_sum(SS[:, :], et_red, axis=AX.X), "sDve")
            s2_done = self.val("sDve")

            @blk.scalar
            def _(s_, s2_done=s2_done):
                s_.wait_ge(self.S("sDve"), s2_done)
                self.inc(s_.activation(VP[:, :], SS[:, :], AF.Ln), "sAct")
            lnv_done = self.val("sAct")
        sk_av = self.val("sDve")

        @blk.vector
        def _(v, w=lnv_done):
            v.wait_ge(self.S("sAct"), w)
            self.wait(v, "sDve")
            self.inc(v.tensor_tensor(B1[:, :], z_aij, vp_aij, ALU.subtract), "sDve")
            self.wait(v, "sDve")
            self.inc(v.tensor_tensor(B1[:, :], B1[:, :], up_aij, ALU.subtract), "sDve")
        hfin = self.val("sDve")

        @blk.scalar
        def _(s_):
            s_.wait_ge(self.S("sDve"), hfin)
            self.inc(s_.activation(HH[:, :], B1[:, :], AF.Exp), "sAct")
        h_done = self.val("sAct")

        @blk.vector
        def _(v):
            v.wait_ge(self.S("sAct"), h_done)
            self.wait(v, "sDve")
            self.inc(v.tensor_scalar(B1[:, :], HH[:, :], C[:, 144:145], None, ALU.mult), "sDve")
            self.wait(v, "sDve")
            ipv = bass.AP(C, C[0:1, 128:129].offset, [[C[0:1, 0:1].ap[0][0], 128], [0, 4], [1, 16]])
            self.inc(v.tensor_scalar(HH[:, :], ipv, C[:, 145:146], None, ALU.mult), "sDve")
            self.wait(v, "sDve")
            self.inc(v.tensor_add(B1[:, :], B1[:, :], HH[:, :]), "sDve")
        bres_done = self.val("sDve")

        PM = SC(1304, 1312)
        @blk.vector
        def _(v):
            v.wait_ge(self.S("sDve"), z_ready)
            self.inc(v.reduce_max(PM[:, 0:4],
                                  bass.AP(PRER.tensor, PRER.offset, [[PRER.ap[0][0], 128], [4, 4], [1, 4]]),
                                  axis=AX.X), "sDve")
            self.inc(v.reduce_max(PM[:, 4:8],
                                  bass.AP(POSTR.tensor, POSTR.offset, [[POSTR.ap[0][0], 128], [4, 4], [1, 4]]),
                                  axis=AX.X), "sDve")
            self.wait(v, "sDve")
            pmv = bass.AP(PM.tensor, PM.offset, [[PM.ap[0][0], 128], [1, 4], [0, 4]])
            pmv2 = bass.AP(PM.tensor, PM.offset + 4, [[PM.ap[0][0], 128], [1, 4], [0, 4]])
            self.inc(v.tensor_tensor(PRER[:, :], PRER[:, :], pmv, ALU.subtract), "sDve")
            self.inc(v.tensor_tensor(POSTR[:, :], POSTR[:, :], pmv2, ALU.subtract), "sDve")
        sm_stab = self.val("sDve")

        @blk.scalar
        def _(s_):
            s_.wait_ge(self.S("sDve"), sm_stab)
            self.inc(s_.activation(PREE[:, :], PRER[:, :], AF.Exp), "sAct")
            self.inc(s_.activation(POSTE[:, :], POSTR[:, :], AF.Exp), "sAct")
        pp_e = self.val("sAct")

        @blk.vector
        def _(v):
            v.wait_ge(self.S("sAct"), pp_e)
            self.inc(v.reduce_sum(PRES[:, :],
                                  bass.AP(PREE.tensor, PREE.offset, [[PREE.ap[0][0], 128], [4, 4], [1, 4]]),
                                  axis=AX.X), "sDve")
            self.inc(v.reduce_sum(POSTS[:, :],
                                  bass.AP(POSTE.tensor, POSTE.offset, [[POSTE.ap[0][0], 128], [4, 4], [1, 4]]),
                                  axis=AX.X), "sDve")
            self.wait(v, "sDve")
            self.inc(v.reciprocal(PRERS[:, :], PRES[:, :]), "sDve")
            self.inc(v.reciprocal(POSTRS[:, :], POSTS[:, :]), "sDve")
            self.wait(v, "sDve")
            pre_rs_v = bass.AP(PRERS.tensor, PRERS.offset, [[PRERS.ap[0][0], 128], [1, 4], [0, 4]])
            post_rs_v = bass.AP(POSTRS.tensor, POSTRS.offset, [[POSTRS.ap[0][0], 128], [1, 4], [0, 4]])
            self.inc(v.tensor_tensor(HPRE[:, :], PREE[:, :], pre_rs_v, ALU.mult), "sDve")
            self.inc(v.tensor_tensor(HPOST[:, :], POSTE[:, :], post_rs_v, ALU.mult), "sDve")
        hp_done = self.val("sDve")

        @blk.vector
        def _(v):
            v.wait_ge(self.S("sDve"), max(bres_done, hp_done))
            for a in range(4):
                bt = Bt_in[:, a * 128:(a + 1) * 128]
                v.tensor_copy(bt[:, 0:16], B1[:, a * 16:(a + 1) * 16])
                v.tensor_copy(bt[:, 16:20], HPRE[:, a * 4:(a + 1) * 4])
                v.tensor_copy(bt[:, 20:24], HPOST[:, a * 4:(a + 1) * 4])
            self.inc(v.tensor_copy(Bt_in[:, 24:25], B1[:, 0:1]), "sDve")
        bt_done = self.val("sDve")

        br_pe0 = self.val("sPe")
        br_dv0 = self.val("sDve")
        @blk.tensor
        def _(t):
            t.wait_ge(self.S("sDve"), bt_done)
            for a in range(4):
                if a >= 1:
                    t.wait_ge(self.S("sDve"), br_dv0 + a)
                self.inc(t.transpose(TRa, Bt_in[:, a * 128:(a + 1) * 128], self.ident_f), "sPe")

        @blk.vector
        def _(v):
            for a in range(4):
                v.wait_ge(self.S("sPe"), br_pe0 + a + 1)
                self.inc(v.tensor_copy(BROWS[0:24, a * 128:(a + 1) * 128],
                                       bass.AP(TRa.tensor, TRa.offset, [[TRa.ap[0][0], 24], [1, 128]])), "sDve")
        brows_done = self.val("sDve")

        bb_pe0 = self.val("sPe")
        bb_dv0 = self.val("sDve")
        @blk.tensor
        def _(t):
            t.wait_ge(self.S("sDve"), brows_done)
            for n in range(4):
                if n >= 1:
                    t.wait_ge(self.S("sDve"), bb_dv0 + n)
                self.inc(t.matmul(BC, self.selt_s[:, (16 + n) * 128:(17 + n) * 128],
                                  self.PT[:, 3584:4096], start=True, stop=True), "sPe")

        @blk.vector
        def _(v):
            self.wait(v, "sDve")
            for n in range(4):
                v.wait_ge(self.S("sPe"), bb_pe0 + n + 1)
                self.inc(v.tensor_copy(BB(n), BC[:, :]), "sDve")
        bbpre_done = self.val("sDve")

        @blk.vector
        def _(v):
            v.wait_ge(self.S("sDve"), bbpre_done)
            for c in range(8):
                t0 = self.PT[:, 0:512]
                t1 = self.PT[:, 512:1024]
                t2 = self.PT[:, 1024:1536]
                t3 = self.PT[:, 1536:2048]
                self.wait(v, "sDve")
                v.tensor_mul(t0, self.XN[:, (0 * 8 + c) * 512:(0 * 8 + c) * 512 + 512], BB(0))
                v.tensor_mul(t1, self.XN[:, (1 * 8 + c) * 512:(1 * 8 + c) * 512 + 512], BB(1))
                v.tensor_mul(t2, self.XN[:, (2 * 8 + c) * 512:(2 * 8 + c) * 512 + 512], BB(2))
                self.inc(v.tensor_mul(t3, self.XN[:, (3 * 8 + c) * 512:(3 * 8 + c) * 512 + 512], BB(3)), "sDve")
                self.wait(v, "sDve")
                v.tensor_add(t0, t0, t1)
                self.inc(v.tensor_add(t2, t2, t3), "sDve")
                self.wait(v, "sDve")
                self.inc(v.tensor_add(XPRE[:, c * 512:(c + 1) * 512], t0, t2), "sDve")
        xpre_done = self.val("sDve")

        @blk.sync
        def _(sy):
            sy.wait_ge(self.S("sPe"), self.ws_free_pe)
            src = bass.AP(self.ag2aout, 1024 * 1024,
                          [[1024, 128], [self.AG2AR, 8], [1, 1024]])
            self.inc(sy.dma_start(self.WS[:, 0:8192], src), "cWbr")
        wbr_ld = self.val("cWbr")

        y_pe0 = self.val("sPe")
        y_dv0 = self.val("sDve")
        @blk.tensor
        def _(t):
            t.wait_ge(self.S("cWbr"), wbr_ld)
            t.wait_ge(self.S("sDve"), xpre_done)
            for co in range(8):
                yb = Y[:, (co % 2) * 512:(co % 2) * 512 + 512]
                if co >= 2:
                    t.wait_ge(self.S("sDve"), y_dv0 + co - 1)
                for kc in range(8):
                    ins = t.matmul(yb, self.WS[:, kc * 1024 + co * 128:kc * 1024 + (co + 1) * 128],
                                   XPRE[:, kc * 512:(kc + 1) * 512],
                                   start=(kc == 0), stop=(kc == 7))
                self.inc(ins, "sPe")

        @blk.vector
        def _(v):
            for co in range(8):
                v.wait_ge(self.S("sPe"), y_pe0 + co + 1)
                yb = Y[:, (co % 2) * 512:(co % 2) * 512 + 512]
                self.inc(v.tensor_copy(YB[:, co * 512:(co + 1) * 512], yb), "sDve")
                self.y_ev.append(self.val("sDve"))
        y_done = self.val("sDve")

        for i in range(4):
            bm_pe0 = self.val("sPe")
            bbf = self.bb_free
            bm_dv0 = self.val("sDve")
            @blk.tensor
            def _(t, i=i, bbf=bbf, bm_dv0=bm_dv0):
                if bbf:
                    t.wait_ge(self.S("sDve"), bbf)
                rows = [i * 4 + 0, i * 4 + 1, i * 4 + 2, i * 4 + 3, 20 + i]
                for ri, r in enumerate(rows):
                    if ri >= 1:
                        t.wait_ge(self.S("sDve"), bm_dv0 + ri)
                    self.inc(t.matmul(BC, self.selt_s[:, r * 128:(r + 1) * 128],
                                      self.PT[:, 3584:4096], start=True, stop=True), "sPe")

            @blk.vector
            def _(v, i=i):
                self.wait(v, "sDve")
                for ri in range(5):
                    v.wait_ge(self.S("sPe"), bm_pe0 + ri + 1)
                    self.inc(v.tensor_copy(BB(ri), BC[:, :]), "sDve")
            bb_done = self.val("sDve")

            @blk.vector
            def _(v, i=i, bb_done=bb_done, y_done=y_done):
                v.wait_ge(self.S("sDve"), max(bb_done, y_done))
                for c in range(8):
                    t0 = self.PT[:, 0:512]
                    t1 = self.PT[:, 512:1024]
                    t2 = self.PT[:, 1024:1536]
                    t3 = self.PT[:, 1536:2048]
                    t4 = self.PT[:, 2048:2560]
                    self.wait(v, "sDve")
                    v.tensor_mul(t0, self.XN[:, (0 * 8 + c) * 512:(0 * 8 + c) * 512 + 512], BB(0))
                    v.tensor_mul(t1, self.XN[:, (1 * 8 + c) * 512:(1 * 8 + c) * 512 + 512], BB(1))
                    v.tensor_mul(t2, self.XN[:, (2 * 8 + c) * 512:(2 * 8 + c) * 512 + 512], BB(2))
                    v.tensor_mul(t3, self.XN[:, (3 * 8 + c) * 512:(3 * 8 + c) * 512 + 512], BB(3))
                    self.inc(v.tensor_mul(t4, YB[:, c * 512:(c + 1) * 512], BB(4)), "sDve")
                    self.wait(v, "sDve")
                    mg = i * 8 + c
                    r1c = self.R1[:, mg * 512:(mg + 1) * 512]
                    self.inc(v.tensor_add(r1c, r1c, t0), "sDve")
                    self.wait(v, "sDve")
                    v.tensor_add(t1, t1, t2)
                    self.inc(v.tensor_add(t3, t3, t4), "sDve")
                    self.wait(v, "sDve")
                    self.inc(v.tensor_add(t1, t1, t3), "sDve")
                    self.wait(v, "sDve")
                    self.inc(v.tensor_add(r1c, r1c, t1), "sDve")
            self.bb_free = self.val("sDve")
        return self.val("sDve")

    def emit_attnb(self, blk, half, pp, xn_ready, pp_free_dve, ws_free_pe2=0):
        self.ws_free_pe2 = ws_free_pe2
        self.elu_act = []
        _sfx = str(half)
        agout = self.agout.ap()
        PTf = self.PT[:, :].bitcast(F32)
        SIGQ = self.AO[:, 0:4096]
        RETB = self.AO[:, 4096:8192]
        RSTRIDE = 3072 * 1024

        pe0 = self.val("sPe")
        slabv = []
        @blk.sync
        def _(sy):
            sy.wait_ge(self.S("sPe"), self.ws_free_pe2)
            for kc in range(DC):
                if kc >= 3:
                    sy.wait_ge(self.S("sPe"), pe0 + (kc - 3) + 1)
                rank, kl = kc // 4, kc % 4
                src = bass.AP(self.agout, rank * RSTRIDE + (2048 + kl * 128) * 1024,
                              [[1024, 128], [1, 1024]])
                slabv.append(self.inc(sy.dma_start(
                    self.WS[:, (kc % 3) * 1024:(kc % 3) * 1024 + 1024], src), f"cSb{kc % 3}"))

        @blk.tensor
        def _(t):
            t.wait_ge(self.S("sDve"), max(xn_ready, pp_free_dve))
            for kc in range(DC):
                t.wait_ge(self.S(f"cSb{kc % 3}"), slabv[kc])
                for m in range(8):
                    ins = t.matmul(pp[:, m * 512:(m + 1) * 512],
                                   self.WS[:, (kc % 3) * 1024 + m * 128:(kc % 3) * 1024 + (m + 1) * 128],
                                   self.XN[:, kc * 512:(kc + 1) * 512],
                                   start=(kc == 0), stop=(kc == DC - 1))
                self.inc(ins, "sPe")
        qs_done = self.val("sPe")

        elu_dv = []
        for m in range(8):
            @blk.vector
            def _(v, m=m, qs_done=qs_done):
                v.wait_ge(self.S("sPe"), qs_done)
                self.wait(v, "sDve")
                if m >= 2:
                    v.wait_ge(self.S("sAct"), self.elu_act[m - 2])
                u = PTf[:, (m % 2) * 1024:(m % 2) * 1024 + 512]
                r = PTf[:, (m % 2) * 1024 + 512:(m % 2) * 1024 + 1024]
                self.inc(v.tensor_scalar(u, pp[:, m * 512:(m + 1) * 512],
                                         self.bqs[:, m:m + 1], None, ALU.add), "sDve")
                self.wait(v, "sDve")
                self.inc(v.tensor_scalar(r, u, 0.0, None, ALU.max), "sDve")
                self.wait(v, "sDve")
                self.inc(v.tensor_sub(u, u, r), "sDve")
            t_dv = self.val("sDve")

            @blk.scalar
            def _(s_, m=m, t_dv=t_dv):
                s_.wait_ge(self.S("sDve"), t_dv)
                u = PTf[:, (m % 2) * 1024:(m % 2) * 1024 + 512]
                self.inc(s_.activation(u, u, AF.Exp), "sAct")
            self.elu_act.append(self.val("sAct"))
            e_act = self.val("sAct")

            @blk.vector
            def _(v, m=m, e_act=e_act):
                v.wait_ge(self.S("sAct"), e_act)
                u = PTf[:, (m % 2) * 1024:(m % 2) * 1024 + 512]
                r = PTf[:, (m % 2) * 1024 + 512:(m % 2) * 1024 + 1024]
                self.inc(v.tensor_add(SIGQ[:, m * 512:(m + 1) * 512], u, r), "sDve")
            elu_dv.append(self.val("sDve"))
        sig_done = self.val("sDve")

        NUM = [pp[:, 0:512], pp[:, 1024:1536]]
        DEN = [pp[:, 512:1024], pp[:, 1536:2048]]
        ret_dv = []
        for h in range(8):
            par = h % 2
            pe_h0 = self.val("sPe")
            @blk.tensor
            def _(t, h=h, par=par):
                if h == 0:
                    t.wait_ge(self.S("cMZ"), self.m_done)
                t.wait_ge(self.S("sDve"), elu_dv[min(2 * h + 1, 7)])
                if h >= 2:
                    t.wait_ge(self.S("sDve"), ret_dv[h - 2])

                t.matmul(NUM[par], self.mz[:, h * 128:(h + 1) * 128],
                         SIGQ[:, h * 512:(h + 1) * 512], start=True, stop=True)
                self.inc(t.matmul(DEN[par], self.mz[:, 1024 + h * 128:1024 + (h + 1) * 128],
                                  SIGQ[:, h * 512:(h + 1) * 512], start=True, stop=True), "sPe")
            nd_pe = self.val("sPe")

            @blk.vector
            def _(v, h=h, par=par, nd_pe=nd_pe):
                v.wait_ge(self.S("sPe"), nd_pe)
                dn = PTf[:, 1536:2048] if par else PTf[:, 512:1024]
                self.wait(v, "sDve")
                self.inc(v.tensor_scalar(dn, DEN[par], 1e-6, None, ALU.add), "sDve")
                self.wait(v, "sDve")
                self.inc(v.reciprocal(dn, dn), "sDve")
                self.wait(v, "sDve")
                self.inc(v.tensor_mul(RETB[:, h * 512:(h + 1) * 512], NUM[par], dn), "sDve")
            ret_dv.append(self.val("sDve"))
        ret_done = self.val("sDve")

        opj_pe0 = self.val("sPe")
        winv = []
        @blk.sync
        def _(sy):
            i = 0
            for oct_ in range(4):
                for jc in range(8):
                    if i >= 2:
                        sy.wait_ge(self.S("sPe"), opj_pe0 + (i - 1))
                    src = bass.AP(self.agout, (oct_ * 2) * RSTRIDE + (2560 + jc * 64) * 1024,
                                  [[512, 128], [RSTRIDE, 2], [1, 512]])
                    winv.append(self.inc(sy.dma_start(
                        self.WS[:, 4096 + (i % 2) * 1024:4096 + (i % 2) * 1024 + 1024], src),
                        f"cWf{i % 2}"))
                    i += 1

        dv0 = self.val("sDve")
        @blk.tensor
        def _(t):
            t.wait_ge(self.S("sDve"), ret_done)
            i = 0
            for oct_ in range(4):
                if oct_ >= 1:
                    t.wait_ge(self.S("sDve"), dv0 + 8 * oct_)
                for jc in range(8):
                    t.wait_ge(self.S(f"cWf{i % 2}"), winv[i])
                    base = 4096 + (i % 2) * 1024
                    for m in range(8):
                        mc = oct_ * 8 + m
                        col = ((mc // 4) - oct_ * 2) * 512 + (mc % 4) * 128
                        ins = t.matmul(pp[:, m * 512:(m + 1) * 512],
                                       self.WS[:, base + col:base + col + 128],
                                       RETB[:, jc * 512:(jc + 1) * 512],
                                       start=(jc == 0), stop=(jc == 7))
                    self.inc(ins, "sPe")
                    i += 1
        ob_pe0 = opj_pe0

        @blk.vector
        def _(v):
            for oct_ in range(4):
                v.wait_ge(self.S("sPe"), opj_pe0 + 8 * (oct_ + 1))
                for m in range(8):
                    mg = oct_ * 8 + m
                    r1c = self.R1[:, mg * 512:(mg + 1) * 512]
                    self.inc(v.tensor_add(r1c, r1c, pp[:, m * 512:(m + 1) * 512]), "sDve")
        return self.val("sDve")

    def emit_pipe(self, blk, half, ao_ready, r1_free_pe):
        nc = self.nc
        with nc.psum_tensor("pp%d" % half, [128, 4096], F32) as pp:
            self.pE, self.pF = pp[:, 0:512], pp[:, 512:1024]
            x1_done = self.emit_oproj(blk, half, pp, ao_ready, r1_free_pe)
            op_evac = self.val("sDve")
            xn1 = self.emit_ln(blk, self.R1, self.XN, self.epsln,
                               pe_waits=[("sDve", op_evac)])
            x2_done = self.emit_ffn(blk, half, pp, xn1, self.val("sDve"),
                                    ws_free_pe=self.val("sPe"))
            xn2 = self.emit_ln(blk, self.R1, self.XN, self.epsln,
                               pe_waits=[("sDve", x2_done)],
                               dve_waits=[("sDve", x2_done)])
            x3_done = self.emit_attnb(blk, half, pp, xn2, self.val("sDve"),
                                      ws_free_pe2=self.val("sPe"))
            xn3 = self.emit_ln(blk, self.R1, self.XN, self.epsln,
                               pe_waits=[("sDve", x3_done)],
                               dve_waits=[("sDve", x3_done)])
            x4_done = self.emit_ffn(blk, half, pp, xn3, self.val("sDve"),
                                    ws_free_pe=self.val("sPe"))

        @blk.sync
        def _(sy, half=half):
            sy.wait_ge(self.S("sDve"), x4_done)
            dst = bass.AP(self.yout.tensor, half * 512,
                          [[1024, 128], [131072, 32], [1, 512]])
            src = bass.AP(self.R1, self.R1[0:1, 0:1].offset,
                          [[self.R1[0:1, 0:1].ap[0][0], 128], [512, 32], [1, 512]])
            self.inc(sy.dma_start(dst, src), "cOut")
        return self.val("cOut")

    def build_full(self):
        nc = self.nc
        with nc.Block() as blk:
            self.emit_prologue(blk)
            q0_ready = self.val("sDve")
            self.emit_attention(blk, 0, q_ready=q0_ready, wait_cc=True)
            att0_pe = self.val("sPe")
            ao0 = self.val("sDve")
            self.emit_pipe(blk, 0, ao_ready=ao0, r1_free_pe=att0_pe)

            xnld0 = self.val("cXn1")
            @blk.sync
            def _(sy):
                sy.wait_ge(self.S("sDve"), self.val("sDve"))
                src = bass.AP(self.xn0b1.tensor, 0, [[512, 128], [65536, 32], [1, 512]])
                dst = bass.AP(self.XN, self.XN[0:1, 0:1].offset,
                              [[self.XN[0:1, 0:1].ap[0][0], 128], [512, 32], [1, 512]])
                self.inc(sy.dma_start(dst, src), "cXn1")
            xn1_ld = self.val("cXn1")

            @blk.vector
            def _(v):
                v.wait_ge(self.S("cXn1"), xn1_ld)
                self.inc(v.engine_nop(), "sDve")
            xnb1_ready = self.val("sDve")

            with nc.psum_tensor("qp1", [128, 4096], F32) as qp1:
                self.emit_qproj(blk, self.XN, qp1, xn_wait=xnb1_ready)
            q1_ready = self.val("sDve")
            self.emit_attention(blk, 1, q_ready=q1_ready, wait_cc=False,
                                r1_wait=("cOut", self.val("cOut")))
            att1_pe = self.val("sPe")
            ao1 = self.val("sDve")
            self.emit_pipe(blk, 1, ao_ready=ao1, r1_free_pe=att1_pe)

        with nc.Block() as blk2:
            @blk2.sync
            def _(sy):
                sy.wait_ge(self.S("cOut"), self.val("cOut"))
        return nc



def kernel(**inputs):
    m = MK("full")
    nc = m.build()
    in_maps = host_build_inputs(inputs)
    res = run_bass_kernel_spmd(nc, in_maps, core_ids=list(range(NCORE)))
    return host_gather_output(res.results)


# version 07 (final)
# speedup vs baseline: 1.2076x; 1.1526x over previous
import numpy as np
from contextlib import ExitStack

import concourse.bass as bass
import concourse.mybir as mybir
from concourse.bass_utils import run_bass_kernel_spmd

F32 = mybir.dt.float32
BF16 = mybir.dt.bfloat16
AF = mybir.ActivationFunctionType
ALU = mybir.AluOpType
AX = mybir.AxisListType

D = 4096
DC = 32
SEQ = 4096
BATCH = 2
NCORE = 8
TOK = 1024
HALF = 512
NSLOT = 8
TB = 128
NKT = 32
HKV = 8
G = 4
HD = 128
LN_EPS = 1e-5
RMS_EPS = 1.1920929e-07
ATT_SCALE = HD ** -0.5

PFIX = [8 * (s % 4 + 1) for s in range(NSLOT)]
NGRP = [p // 8 for p in PFIX]

RANKS = list(range(NCORE))


def slot_to_block(core, s):
    idx = s % 4
    if s < 4:
        return core + 8 * idx
    return (7 - core) + 8 * idx


def block_owner(b, kt):
    if b == 0:
        return kt % 8, kt // 8
    return 7 - (kt % 8), 4 + kt // 8



def host_build_inputs(inputs):
    x = np.asarray(inputs["L0_attn_a_hidden_states"], np.float32)
    wq_a = np.ascontiguousarray(np.asarray(inputs["L0_attn_a_w_q"], np.float32))
    wk_a = np.ascontiguousarray(np.asarray(inputs["L0_attn_a_w_k"], np.float32))
    wv_a = np.ascontiguousarray(np.asarray(inputs["L0_attn_a_w_v"], np.float32))
    wo_a = np.ascontiguousarray(np.asarray(inputs["L0_attn_a_w_o"], np.float32))
    Wq_b = np.asarray(inputs["L0_attn_b_Wq"], np.float32)
    bq_b = np.asarray(inputs["L0_attn_b_bq"], np.float32)
    Wo_b = np.asarray(inputs["L0_attn_b_Wo"], np.float32)
    M_b = np.ascontiguousarray(np.asarray(inputs["L0_attn_b_M"], np.float32))
    z_b = np.ascontiguousarray(np.asarray(inputs["L0_attn_b_z"], np.float32))
    wres = np.ascontiguousarray(np.asarray(inputs["L0_ffn_W_res"], np.float32))
    wpre = np.ascontiguousarray(np.asarray(inputs["L0_ffn_W_pre"], np.float32))
    wpost = np.ascontiguousarray(np.asarray(inputs["L0_ffn_W_post"], np.float32))
    hres_b = np.ascontiguousarray(np.asarray(inputs["L0_ffn_H_res_base"], np.float32))
    hpre_b = np.ascontiguousarray(np.asarray(inputs["L0_ffn_H_pre_base"], np.float32))
    hpost_b = np.ascontiguousarray(np.asarray(inputs["L0_ffn_H_post_base"], np.float32))
    gate = np.ascontiguousarray(np.asarray(inputs["L0_ffn_gate"], np.float32))
    wbr = np.ascontiguousarray(np.asarray(inputs["L0_ffn_W_branch"], np.float32))

    sel = np.concatenate([np.arange(4 * h * 128, 4 * h * 128 + 128) for h in range(HKV)])
    wqsel = np.ascontiguousarray(Wq_b[sel, :])
    bqsel = np.ascontiguousarray(bq_b[sel])

    import ml_dtypes
    bf = ml_dtypes.bfloat16

    ident = np.eye(128, dtype=np.float32)
    ipat = np.tile(np.eye(4, dtype=np.float32).reshape(1, 16), (128, 1)).copy()

    in_maps = []
    for c in range(NCORE):
        cols = []
        for s in range(NSLOT):
            b = s // 4
            blk = slot_to_block(c, s)
            cols.append(x[b, blk * TB:(blk + 1) * TB, :].T)
        x_fm = np.ascontiguousarray(np.concatenate(cols, axis=1), dtype=np.float32)

        maskt = np.zeros((NSLOT, 8, 128, 128), np.float32)
        triu = np.triu(np.ones((128, 128), np.float32))
        for s in range(NSLOT):
            blk = slot_to_block(c, s)
            prefix = blk + 1
            base = PFIX[s] - 8
            for j in range(8):
                p = base + j
                if p < prefix - 1:
                    maskt[s, j] = 1.0
                elif p == prefix - 1:
                    maskt[s, j] = triu
        maskt = np.ascontiguousarray(maskt.transpose(0, 2, 1, 3)).astype(bf)

        wqsel_cols = np.ascontiguousarray(wqsel[:, 512 * c:512 * (c + 1)])
        wob_rows = np.ascontiguousarray(Wo_b[512 * c:512 * (c + 1), :])

        wsl = np.concatenate([
            wk_a[512 * c:512 * (c + 1), :].reshape(512, 1024),
            wv_a[512 * c:512 * (c + 1), :].reshape(512, 1024),
            wbr[128 * c:128 * (c + 1), :].reshape(128, 1024),
            wq_a[512 * c:512 * (c + 1), :].reshape(2048, 1024),
            wo_a[512 * c:512 * (c + 1), :].reshape(2048, 1024),
        ], axis=0)
        wslabs = np.ascontiguousarray(wsl, dtype=np.float32)
        _ = (inputs["L0_attn_b_Wk"], inputs["L0_attn_b_bk"],
             inputs["L0_attn_b_Wv"], inputs["L0_attn_b_bv"])

        selt = np.zeros((128, 24 * 128), np.float32)
        for r in range(24):
            selt[r, r * 128:(r + 1) * 128] = 1.0
        selt = selt.astype(bf)
        baser = np.concatenate([hres_b.reshape(16), hpre_b, hpost_b]).reshape(1, 24).astype(np.float32)

        in_maps.append({
            "xin": x_fm, "selt": selt, "baser": baser, "wslabs": wslabs,
            "wqsel_cols": wqsel_cols, "bqsel": bqsel, "wob_rows": wob_rows,
            "m_in": M_b, "z_in": z_b,
            "wres": wres, "wpre": wpre, "wpost": wpost,
            "hres_b": hres_b.reshape(16).copy(), "hpre_b": hpre_b, "hpost_b": hpost_b,
            "gate": gate, "wbr": wbr,
            "ident": ident, "ipat": ipat, "maskt": maskt,
        })
    return in_maps


def host_gather_output(results):
    out = np.empty((BATCH, SEQ, D), np.float32)
    for c in range(NCORE):
        y = results[c]["yout"]
        for s in range(NSLOT):
            b = s // 4
            blk = slot_to_block(c, s)
            out[b, blk * TB:(blk + 1) * TB, :] = y[:, s * TB:(s + 1) * TB].T
    return out



class MK:
    def __init__(self, stage="full"):
        self.stage = stage
        self._kv_prev_pe = None
        nc = bass.Bass("TRN2", target_bir_lowering=False)
        self.nc = nc
        self.es = ExitStack()
        self.cnt = {}
        self.sems = {}
        self.dbg_outputs = {}
        self.wdone = {}

        T = lambda n, s, d=F32: nc.dram_tensor(n, s, d, kind="ExternalInput").ap()
        self.xin = T("xin", [D, TOK])
        self.wslabs = T("wslabs", [5248, 1024])
        self.wqsel_cols = T("wqsel_cols", [1024, 512])
        self.bqsel = T("bqsel", [1024])
        self.wob_rows = T("wob_rows", [512, D])
        self.m_in = T("m_in", [HKV, 128, 128])
        self.z_in = T("z_in", [HKV, 128])
        self.wres = T("wres", [16, D])
        self.wpre = T("wpre", [4, D])
        self.wpost = T("wpost", [4, D])
        self.hres_b = T("hres_b", [16])
        self.hpre_b = T("hpre_b", [4])
        self.hpost_b = T("hpost_b", [4])
        self.gate = T("gate", [1])
        self.wbr = T("wbr", [1024, 1024])
        self.ident = T("ident", [128, 128])
        self.ipat = T("ipat", [128, 16])
        self.maskt = T("maskt", [NSLOT, 128, 8, 128], BF16)
        self.selt = T("selt", [128, 24 * 128], BF16)
        self.baser = T("baser", [1, 24])

        self.yout = nc.dram_tensor("yout", [D, TOK], F32, kind="ExternalOutput").ap()
        I = lambda n, s, d=BF16: nc.dram_tensor(n, s, d, kind="Internal").ap()
        self.xn0b1 = I("xn0b1", [DC, 128, HALF])
        self.ag2ain = nc.dram_tensor("ag2ain", [1152, 1024], BF16, kind="Internal")
        self.ag2aout = nc.dram_tensor("ag2aout", [1152 * NCORE, 1024], BF16, kind="Internal")
        self.ag2bin = nc.dram_tensor("ag2bin", [4096, 1024], BF16, kind="Internal")
        self.ag2bout = nc.dram_tensor("ag2bout", [4096 * NCORE, 1024], BF16, kind="Internal")
        self.agin = nc.dram_tensor("agin", [3072, 1024], BF16, kind="Internal")
        self.agout = nc.dram_tensor("agout", [3072 * NCORE, 1024], BF16, kind="Internal")

        sb = lambda n, s, d: self.es.enter_context(nc.sbuf_tensor(n, s, d))
        self.R1 = sb("R1", [128, 16384], F32)
        self.XN = sb("XN", [128, 16384], BF16)
        self.QB = sb("QB", [128, 16384], BF16)
        self.AO = sb("AO", [128, 16384], BF16)
        self.WS = sb("WS", [128, 8192], BF16)
        self.PT = sb("PTS", [128, 4096], BF16)
        self.wrouter = sb("wrouter", [128, DC * 128], BF16)
        self.mz = sb("mz", [128, 2048], BF16)
        self.consts = sb("consts", [128, 640], F32)
        self.cbf = sb("cbf", [128, 256], BF16)
        self.selt_s = sb("selt_s", [128, 3072], BF16)
        self.bqs = sb("bqs", [128, 8], F32)
        self.mask8 = sb("mask8", [128, 8 * 128], BF16)
        self.kvA = None

        self.small = self.QB[:, 8192:16384].bitcast(F32)
        for name in ["sPe", "sAct", "sDve", "sPool", "sCc"]:
            self.sems[name] = self.es.enter_context(nc.semaphore(name))
            self.cnt[name] = 0

    def S(self, name):
        if name not in self.sems:
            self.sems[name] = self.es.enter_context(self.nc.semaphore(name))
            self.cnt[name] = 0
        return self.sems[name]

    ENG = {"sPe", "sAct", "sDve", "sPool", "sCc"}

    def inc(self, inst, name, k=None):
        self.S(name)
        if k is None:
            k = 1 if name in self.ENG else 16
        self.cnt[name] += k
        inst.then_inc(self.sems[name], k)
        return self.cnt[name]

    def val(self, name):
        self.S(name)
        return self.cnt[name]

    def wait(self, eng, name, v=None):
        self.S(name)
        eng.wait_ge(self.sems[name], self.cnt[name] if v is None else v)

    def W(self, eng, name, v=None):
        self.S(name)
        eng.wait_ge(self.sems[name], self.cnt[name] if v is None else v)

    def add_dbg(self, name, shape, dtype=F32):
        ap = self.nc.dram_tensor(name, shape, dtype, kind="ExternalOutput").ap()
        self.dbg_outputs[name] = ap
        return ap

    AG2AR = 1152 * 1024
    AG2BR = 4096 * 1024

    def w_slab(self, which, k, w):
        base = {"wk": 0, "wv": 512, "wbr": 1024}[which]
        rank, j = (k, 0) if which == "wbr" else (k // 4, k % 4)
        off = rank * self.AG2AR + (base + j * 128) * 1024
        return bass.AP(self.ag2aout, off, [[w, 128], [1, w]])

    def w_window(self, which, oct_, kb):
        base = {"wq": 0, "wo": 2048}[which]
        off = kb * self.AG2BR + (base * 1024) + oct_ * 1024
        return bass.AP(self.ag2bout, off, [[4096, 128], [512 * 1024, 4], [1, 1024]])


    def emit_ln(self, blk, xsrc, dst, eps, extra_dst_dram=None, act_waits=(),
                dve_waits=(), pe_waits=()):
        small = self.small
        XB = lambda k: self.PT[:, (k % 4) * 1024 + 512:(k % 4) * 1024 + 1024]
        pSx, pSx2 = self.pE, self.pF

        cast_base = self.val("sDve")
        sq_base = self.val("sAct")
        pe_base = self.val("sPe")

        @blk.vector
        def _(v):
            for (s, val) in dve_waits:
                v.wait_ge(self.S(s), val)
            for k in range(DC):
                if k >= 4:
                    v.wait_ge(self.S("sPe"), pe_base + (k - 4) + 1)
                    v.wait_ge(self.S("sAct"), sq_base + (k - 4) + 1)
                self.inc(v.tensor_copy(XB(k), xsrc[:, k * 512:(k + 1) * 512]), "sDve")

        @blk.scalar
        def _(s):
            for (sm, val) in act_waits:
                s.wait_ge(self.S(sm), val)
            for k in range(DC):
                x2 = self.PT[:, (k % 4) * 1024:(k % 4) * 1024 + 512]
                s.wait_ge(self.S("sDve"), cast_base + k + 1)
                if k >= 4:
                    s.wait_ge(self.S("sPe"), pe_base + (k - 4) + 1)
                self.inc(s.activation(x2, XB(k), AF.Square), "sAct")

        @blk.tensor
        def _(t):
            for (sm, val) in pe_waits:
                t.wait_ge(self.S(sm), val)
            for k in range(DC):
                x2 = self.PT[:, (k % 4) * 1024:(k % 4) * 1024 + 512]
                t.wait_ge(self.S("sDve"), cast_base + k + 1)
                t.matmul(pSx[:, 0:512], self.onesb[:, 0:128], XB(k),
                         start=(k == 0), stop=(k == DC - 1))
                t.wait_ge(self.S("sAct"), sq_base + k + 1)
                self.inc(t.matmul(pSx2[:, 0:512], self.onesb[:, 0:128], x2,
                                  start=(k == 0), stop=(k == DC - 1)), "sPe")
        pe_done = self.val("sPe")

        SL = lambda i: small[:, i * 512:(i + 1) * 512]
        m_rep, var, sv, r_rep, A_rep = SL(0), SL(1), SL(2), SL(3), SL(4)

        @blk.vector
        def _(v):
            v.wait_ge(self.S("sPe"), pe_done)
            self.inc(v.tensor_scalar_mul(m_rep, pSx[:, 0:512], 1.0 / D), "sDve")
            self.wait(v, "sDve")
            self.inc(v.tensor_mul(SL(5), m_rep, m_rep), "sDve")
            self.wait(v, "sDve")
            self.inc(v.scalar_tensor_tensor(var, pSx2[:, 0:512], 1.0 / D, SL(5),
                                            ALU.mult, ALU.subtract), "sDve")
        var_done = self.val("sDve")

        @blk.scalar
        def _(s):
            s.wait_ge(self.S("sDve"), var_done)
            self.inc(s.activation(sv, var, AF.Ln, bias=eps), "sAct")
            self.wait(s, "sAct")
            self.inc(s.activation(r_rep, sv, AF.Exp, scale=-0.5), "sAct")
        r_done = self.val("sAct")

        @blk.vector
        def _(v):
            v.wait_ge(self.S("sAct"), r_done)
            v.wait_ge(self.S("sDve"), var_done)
            self.inc(v.scalar_tensor_tensor(A_rep, m_rep, -1.0, r_rep, ALU.mult, ALU.mult), "sDve")
            for k in range(DC):
                tmp = SL(5) if k % 2 == 0 else SL(6)
                self.wait(v, "sDve")
                self.inc(v.tensor_mul(tmp, xsrc[:, k * 512:(k + 1) * 512], r_rep), "sDve")
                self.wait(v, "sDve")
                self.inc(v.tensor_add(dst[:, k * 512:(k + 1) * 512], tmp, A_rep), "sDve")
        xn_done = self.val("sDve")

        if extra_dst_dram is not None:
            @blk.sync
            def _(sy):
                sy.wait_ge(self.S("sDve"), xn_done)
                dram_ap = bass.AP(extra_dst_dram.tensor, 0,
                                  [[512, 128], [65536, 32], [1, 512]])
                src_ap = bass.AP(dst, dst[0:1, 0:1].offset,
                                 [[dst[0:1, 0:1].ap[0][0], 128], [512, 32], [1, 512]])
                self.inc(sy.dma_start(dram_ap, src_ap), "cXnS")
        return xn_done

    def emit_coop_cast(self, blk):
        slabs = []
        for j in range(4):
            slabs.append((j * 128, 128, 1024, 0))
        for j in range(4):
            slabs.append((512 + j * 128, 128, 1024, 0))
        slabs.append((1024, 128, 1024, 0))
        for j in range(4):
            slabs.append((1152 + j * 512, 512, 4096, 1))
        for j in range(4):
            slabs.append((3200 + j * 512, 512, 4096, 1))
        inv = []

        w0_base = self.val("cAgW0")
        w1_base = self.val("cAgW1")

        na = sum(1 for s in slabs if s[3] == 0)

        @blk.gpsimd
        def _(g):
            for i, (r0, nr, w, grp) in enumerate(slabs):
                p = i % 2
                if i >= 2:
                    base = w0_base if p == 0 else w1_base
                    g.wait_ge(self.S(f"cAgW{p}"), base + 16 * (i // 2))
                srcw = bass.AP(self.wslabs.tensor, r0 * 1024, [[w, 128], [1, w]])
                inv.append(self.inc(g.dma_start(self.WS[:, p * 4096:p * 4096 + w], srcw),
                                    f"cAgC{p}"))
                if i == na - 1:
                    pass

        agaw = []
        @blk.scalar
        def _(s_):
            for i, (r0, nr, w, grp) in enumerate(slabs):
                p = i % 2
                s_.wait_ge(self.S(f"cAgC{p}"), inv[i])
                if grp == 0:
                    dst = bass.AP(self.ag2ain, r0 * 1024, [[w, 128], [1, w]])
                else:
                    dst = bass.AP(self.ag2bin, (r0 - 1152) * 1024, [[w, 128], [1, w]])
                self.inc(s_.dma_start(dst, self.WS[:, p * 4096:p * 4096 + w]), f"cAgW{p}")
                if i == na - 1:
                    agaw.append([("cAgW0", self.val("cAgW0")), ("cAgW1", self.val("cAgW1"))])
        ag2b_waits = [("cAgW0", self.val("cAgW0")), ("cAgW1", self.val("cAgW1"))]

        @blk.gpsimd
        def _(g):
            for (sm, vv) in agaw[0]:
                g.wait_ge(self.S(sm), vv)
            self.inc(g.collective_compute(
                "AllGather", ALU.bypass,
                replica_groups=[list(range(NCORE))],
                ins=[self.ag2ain.ap().opt()],
                outs=[self.ag2aout.ap().opt()]), "sCc")
            for (sm, vv) in ag2b_waits:
                g.wait_ge(self.S(sm), vv)
            self.inc(g.collective_compute(
                "AllGather", ALU.bypass,
                replica_groups=[list(range(NCORE))],
                ins=[self.ag2bin.ap().opt()],
                outs=[self.ag2bout.ap().opt()]), "sCc")

    def emit_weight_casts(self, blk, group, gate_pe=None):
        slabs = []
        for (src, dst, n, w) in group:
            for k in range(n):
                slabs.append((src, dst, k, w, src.tensor.name))

        @blk.gpsimd
        def _(g):
            if gate_pe is not None:
                g.wait_ge(self.S("sPe"), gate_pe)
            for i, (src, dst, k, w, nm) in enumerate(slabs):
                p = i % 2
                buf = self.WS[:, p * 4096:p * 4096 + w]
                if self.val(f"cWw{p}") > 0:
                    g.wait_ge(self.S(f"cWw{p}"), self.val(f"cWw{p}"))
                cv = self.inc(g.dma_start(buf, src[k * 128:(k + 1) * 128, 0:w]), f"cWc{p}")
                g.wait_ge(self.S(f"cWc{p}"), cv)
                self.inc(g.dma_start(dst[k, :, :], buf), f"cWw{p}")
                self.wdone[nm] = [("cWw0", self.val("cWw0")), ("cWw1", self.val("cWw1"))]

    def emit_setup(self, blk):
        nc = self.nc
        C = self.consts

        @blk.sync
        def _(sy):
            self.inc(sy.dma_start(C[:, 0:128], self.ident), "cLdC")
            self.inc(sy.dma_start(C[:, 128:144], self.ipat), "cLdC")
            self.inc(sy.dma_start(C[0:1, 150:151], bass.AP(self.gate.tensor, 0, [[1, 1], [1, 1]])), "cLdC")
            self.inc(sy.dma_start(C[0:16, 146:147], bass.AP(self.hres_b.tensor, 0, [[1, 16], [1, 1]])), "cLdC")
            self.inc(sy.dma_start(C[32:36, 146:147], bass.AP(self.hpre_b.tensor, 0, [[1, 4], [1, 1]])), "cLdC")
            self.inc(sy.dma_start(C[64:68, 146:147], bass.AP(self.hpost_b.tensor, 0, [[1, 4], [1, 1]])), "cLdC")
            for m in range(8):
                self.inc(sy.dma_start(self.bqs[:, m:m + 1],
                                      bass.AP(self.bqsel.tensor, m * 128, [[1, 128], [1, 1]])), "cLdC")
            for h in range(HKV):
                self.inc(sy.dma_start(C[:, 256 + h:257 + h],
                                      bass.AP(self.z_in.tensor, h * 128, [[1, 128], [1, 1]])), "cLdC")
            self.inc(sy.dma_start(self.selt_s[:, :], self.selt), "cLdC")
            self.inc(sy.dma_start(C[0:1, 384:408], self.baser), "cLdC")
            self.inc(sy.dma_start(self.R1[0:16, 0:4096], self.wres), "cLdC")
            self.inc(sy.dma_start(self.R1[32:36, 0:4096], self.wpre), "cLdC")
            self.inc(sy.dma_start(self.R1[64:68, 0:4096], self.wpost), "cLdC")
        ld_all = self.val("cLdC")

        if self.stage == "s0a":
            return
        @blk.gpsimd
        def _(g):
            for h in range(HKV):
                self.inc(g.dma_start(self.mz[:, h * 128:(h + 1) * 128], self.m_in[h]), "cMZ")
        self.m_done = self.val("cMZ")
        if self.stage == "s0b":
            return

        @blk.vector
        def _(v):
            v.memset(self.cbf[:, 0:128], 1.0)
            v.memset(C[0:1, 512:640], 1.0)
            v.memset(C[:, 148:149], LN_EPS)
            v.memset(C[:, 149:150], RMS_EPS)
            v.wait_ge(self.S("cLdC"), ld_all)
            v.tensor_copy(self.cbf[:, 128:256], C[:, 0:128])
            v.memzero(self.wrouter[:, :])
            for h in range(HKV):
                col = C[:, 256 + h:257 + h]
                colb = bass.AP(col.tensor, col.offset, [[col.ap[0][0], 128], [0, 128]])
                ins = v.tensor_copy(self.mz[:, 1024 + h * 128:1024 + (h + 1) * 128], colb)
            self.inc(ins, "sDve")
        setup_dve = self.val("sDve")
        self.onesb = self.cbf[:, 0:128]
        if self.stage == "s0c":
            self.identb = self.cbf[:, 128:256]
            self.ident_f = C[:, 0:128]
            return
        self.identb = self.cbf[:, 128:256]
        self.ident_f = C[:, 0:128]
        self.epsln = C[:, 148:149]
        self.epsrms = C[:, 149:150]

        pe_base = self.val("sPe")
        rt_evac0 = self.val("sDve")
        with (nc.psum_tensor("prtA", [128, 128], F32) as prtA,
              nc.psum_tensor("prtB", [128, 128], F32) as prtB):
            prts = [prtA, prtB]

            @blk.tensor
            def _(t):
                t.wait_ge(self.S("sDve"), setup_dve)
                for k in range(DC):
                    if k >= 2:
                        t.wait_ge(self.S("sDve"), rt_evac0 + 3 * (k - 2) + 3)
                    self.inc(t.transpose(prts[k % 2][:, :],
                                         self.R1[:, k * 128:(k + 1) * 128],
                                         self.ident_f), "sPe")

            @blk.vector
            def _(v):
                for k in range(DC):
                    v.wait_ge(self.S("sPe"), pe_base + k + 1)
                    pt = prts[k % 2]
                    v.tensor_copy(self.wrouter[:, k * 128:k * 128 + 16], pt[:, 0:16])
                    v.tensor_copy(self.wrouter[:, k * 128 + 32:k * 128 + 36], pt[:, 32:36])
                    self.inc(v.tensor_copy(self.wrouter[:, k * 128 + 64:k * 128 + 68], pt[:, 64:68]), "sDve", 3)
        self.router_done = self.val("sDve")

        with nc.psum_tensor("pst", [128, 512], F32) as pst:
            g0 = self.val("sDve")
            @blk.scalar
            def _(s):
                s.wait_ge(self.S("cLdC"), ld_all)
                self.inc(s.activation(C[0:1, 151:152], C[0:1, 150:151], AF.Exp, scale=-1.0), "sAct")
            ge = self.val("sAct")
            @blk.vector
            def _(v):
                v.wait_ge(self.S("sAct"), ge)
                self.inc(v.tensor_scalar_add(C[0:1, 152:153], C[0:1, 151:152], 1.0), "sDve")
                self.wait(v, "sDve")
                self.inc(v.reciprocal(C[0:1, 153:154], C[0:1, 152:153]), "sDve")
                self.wait(v, "sDve")
                self.inc(v.tensor_scalar(C[0:1, 154:155], C[0:1, 153:154], -1.0, 1.0,
                                         ALU.mult, ALU.add), "sDve")
            gdve = self.val("sDve")
            @blk.tensor
            def _(t):
                t.wait_ge(self.S("sDve"), gdve)
                t.matmul(pst[:, 0:1], self.onesb[0:1, 0:128].bitcast(BF16) if False else C[0:1, 512:640], C[0:1, 153:154], start=True, stop=True)
                t.matmul(pst[:, 4:5], C[0:1, 512:640], C[0:1, 154:155], start=True, stop=True)
                self.inc(t.matmul(pst[:, 8:32], C[0:1, 512:640], C[0:1, 384:408], start=True, stop=True), "sPe")
            gpe = self.val("sPe")
            @blk.vector
            def _(v):
                v.wait_ge(self.S("sPe"), gpe)
                v.tensor_copy(C[:, 144:145], pst[:, 0:1])
                v.tensor_copy(C[:, 145:146], pst[:, 4:5])
                v.tensor_copy(C[:, 416:440], pst[:, 8:32])
                self.inc(v.tensor_scalar_mul(C[:, 440:456], pst[:, 8:24], 10.0), "sDve")
            self.setup2_done = self.val("sDve")

    def emit_wprep(self, blk):
        nc = self.nc
        agin = self.agin.ap()
        with (nc.psum_tensor("pwpA", [128, 128], F32) as pwpA,
              nc.psum_tensor("pwpB", [128, 128], F32) as pwpB):
            pwps = [pwpA, pwpB]
            for kk in range(4):
                pe0 = self.val("sPe")
                dv0 = self.val("sDve")
                ldv = []

                @blk.sync
                def _(sy, kk=kk, pe0=pe0, ldv=ldv):
                    for mm in range(8):
                        if mm >= 4:
                            sy.wait_ge(self.S("sPe"), pe0 + mm - 4 + 1)
                        ldv.append(self.inc(sy.dma_start(
                            self.R1[:, 8192 + (mm % 4) * 128:8192 + (mm % 4) * 128 + 128],
                            self.wqsel_cols[mm * 128:(mm + 1) * 128, kk * 128:(kk + 1) * 128]),
                            f"cWq{mm % 4}"))

                @blk.tensor
                def _(t, kk=kk, dv0=dv0, ldv=ldv):
                    if kk == 0:
                        t.wait_ge(self.S("sDve"), self.setup2_done)
                    for mm in range(8):
                        t.wait_ge(self.S(f"cWq{mm % 4}"), ldv[mm])
                        if kk > 0 and mm == 0:
                            t.wait_ge(self.S("sDve"), dv0)
                        if mm >= 2:
                            t.wait_ge(self.S("sDve"), dv0 + mm - 2 + 1)
                        self.inc(t.transpose(pwps[mm % 2][:, :],
                                             self.R1[:, 8192 + (mm % 4) * 128:8192 + (mm % 4) * 128 + 128],
                                             self.ident_f), "sPe")

                @blk.vector
                def _(v, kk=kk, pe0=pe0):
                    if kk > 0:
                        v.wait_ge(self.S("cPrS"), self.val("cPrS"))
                    for mm in range(8):
                        v.wait_ge(self.S("sPe"), pe0 + mm + 1)
                        self.inc(v.tensor_copy(self.PT[:, mm * 128:(mm + 1) * 128],
                                               pwps[mm % 2][:, :]), "sDve")
                dv_done = self.val("sDve")

                @blk.sync
                def _(sy, kk=kk, dv_done=dv_done):
                    sy.wait_ge(self.S("sDve"), dv_done)
                    self.inc(sy.dma_start(agin[2048 + kk * 128:2048 + (kk + 1) * 128, :],
                                          self.PT[:, 0:1024]), "cPrS")

            for mc in range(4):
                pe0 = self.val("sPe")

                @blk.sync
                def _(sy, mc=mc):
                    self.inc(sy.dma_start(self.R1[:, 8192:8192 + 4096],
                                          self.wob_rows[mc * 128:(mc + 1) * 128, :]), "cWo")

                ld_done = self.val("cWo")

                @blk.vector
                def _(v, mc=mc, ld_done=ld_done, pe0=pe0):
                    v.wait_ge(self.S("cWo"), ld_done)
                    if mc >= 1:
                        v.wait_ge(self.S("sPe"), pe0)
                        v.wait_ge(self.S("cPoS"), self.val("cPoS"))
                    src = self.R1[:, 8192:8192 + 4096]
                    vw = lambda g: bass.AP(src.tensor, src.offset + g * 128,
                                           [[src.ap[0][0], 128], [512, 8], [1, 128]])
                    fold = self.R1[:, 12288:12288 + 1024]
                    self.inc(v.tensor_add(fold, vw(0), vw(1)), "sDve")
                    self.wait(v, "sDve")
                    self.inc(v.tensor_add(fold, fold, vw(2)), "sDve")
                    self.wait(v, "sDve")
                    self.inc(v.tensor_add(fold, fold, vw(3)), "sDve")
                fold_done = self.val("sDve")

                @blk.tensor
                def _(t, mc=mc, fold_done=fold_done):
                    t.wait_ge(self.S("sDve"), fold_done)
                    fold = self.R1[:, 12288:12288 + 1024]
                    for jc in range(8):
                        if jc >= 2:
                            t.wait_ge(self.S("sDve"), fold_done + jc - 2 + 1)
                        self.inc(t.transpose(pwps[jc % 2][:, :],
                                             fold[:, jc * 128:(jc + 1) * 128],
                                             self.ident_f), "sPe")

                @blk.vector
                def _(v, mc=mc, pe0=pe0):
                    for jc in range(8):
                        v.wait_ge(self.S("sPe"), pe0 + jc + 1)
                        self.inc(v.tensor_copy(self.PT[:, 1024 + jc * 128:1024 + (jc + 1) * 128],
                                               pwps[jc % 2][:, :]), "sDve")
                ev_done = self.val("sDve")

                @blk.sync
                def _(sy, mc=mc, ev_done=ev_done):
                    sy.wait_ge(self.S("sDve"), ev_done)
                    base = 2560 * 1024 + mc * 128
                    dst = bass.AP(agin.tensor, base, [[512, 128], [65536, 8], [1, 128]])
                    srcp = bass.AP(self.PT, self.PT[0:1, 1024:1025].offset,
                                   [[self.PT[0:1, 0:1].ap[0][0], 128], [128, 8], [1, 128]])
                    self.inc(sy.dma_start(dst, srcp), "cPoS")

    def emit_kv_proj(self, blk, half, xnsrc, xn_wait=None):
        agin = self.agin.ap()
        KVB = 24576
        R1b = self.R1[:, :].bitcast(BF16)
        for pr in range(2):
            s0, s1 = half * 4 + 2 * pr, half * 4 + 2 * pr + 1
            tb0, tb1 = 2 * pr, 2 * pr + 1
            pe0 = self.val("sPe")
            ev0 = self.val("sDve")
            prev_pe = self._kv_prev_pe
            slabv = []

            @blk.sync
            def _(sy, pr=pr, half=half, pe0=pe0, slabv=slabv, prev_pe=prev_pe):
                if pr == 0 and half == 0:
                    sy.wait_ge(self.S("sCc"), 1)
                if xn_wait is not None and pr == 0:
                    sy.wait_ge(self.S("sDve"), xn_wait)
                if prev_pe is not None:
                    sy.wait_ge(self.S("sPe"), prev_pe)
                for k in range(DC):
                    if k >= 3:
                        sy.wait_ge(self.S("sPe"), pe0 + (k - 3) + 1)
                    sl = KVB + (k % 3) * 2048
                    self.inc(sy.dma_start(R1b[:, sl:sl + 1024], self.w_slab("wk", k, 1024)), f"cSl{k % 3}")
                    slabv.append(self.inc(sy.dma_start(R1b[:, sl + 1024:sl + 2048], self.w_slab("wv", k, 1024)), f"cSl{k % 3}"))

            @blk.tensor
            def _(t, pr=pr, tb0=tb0, tb1=tb1, ev0=ev0, slabv=slabv):
                if xn_wait is not None:
                    t.wait_ge(self.S("sDve"), xn_wait)
                if pr == 1:
                    t.wait_ge(self.S("sDve"), ev0)
                for k in range(DC):
                    t.wait_ge(self.S(f"cSl{k % 3}"), slabv[k])
                    sl = KVB + (k % 3) * 2048
                    st, sp = (k == 0), (k == DC - 1)
                    xt0 = xnsrc[:, k * 512 + tb0 * 128:k * 512 + tb0 * 128 + 128]
                    xt1 = xnsrc[:, k * 512 + tb1 * 128:k * 512 + tb1 * 128 + 128]
                    t.matmul(self.pAB[:, 0:512], xt0, R1b[:, sl:sl + 512], start=st, stop=sp)
                    t.matmul(self.pAB[:, 512:1024], xt0, R1b[:, sl + 512:sl + 1024], start=st, stop=sp)
                    t.matmul(self.pCD[:, 0:512], xt0, R1b[:, sl + 1024:sl + 1536], start=st, stop=sp)
                    t.matmul(self.pCD[:, 512:1024], xt0, R1b[:, sl + 1536:sl + 2048], start=st, stop=sp)
                    t.matmul(self.pEF[:, 0:512], xt1, R1b[:, sl:sl + 512], start=st, stop=sp)
                    t.matmul(self.pEF[:, 512:1024], xt1, R1b[:, sl + 512:sl + 1024], start=st, stop=sp)
                    t.matmul(self.pGH[:, 0:512], xt1, R1b[:, sl + 1024:sl + 1536], start=st, stop=sp)
                    ins = t.matmul(self.pGH[:, 512:1024], xt1, R1b[:, sl + 1536:sl + 2048], start=st, stop=sp)
                    self.inc(ins, "sPe")
            self._kv_prev_pe = self.val("sPe")
            pe_done = self.val("sPe")

            @blk.vector
            def _(v, s0=s0, s1=s1, pe_done=pe_done):
                v.wait_ge(self.S("sPe"), pe_done)
                v.wait_ge(self.S("cVs0"), self.val("cVs0"))
                v.wait_ge(self.S("cVs1"), self.val("cVs1"))
                v.tensor_copy(self.QB[:, s0 * 1024:(s0 + 1) * 1024], self.pAB[:, :])
                self.inc(v.tensor_copy(self.PT[:, 2048:3072], self.pCD[:, :]), "sDve")
                v.tensor_copy(self.QB[:, s1 * 1024:(s1 + 1) * 1024], self.pEF[:, :])
                self.inc(v.tensor_copy(self.PT[:, 3072:4096], self.pGH[:, :]), "sDve")

            dv_now = self.val("sDve")
            @blk.sync
            def _(sy, s0=s0, s1=s1, dv_now=dv_now):
                sy.wait_ge(self.S("sDve"), dv_now - 1)
                self.inc(sy.dma_start(agin[1024 + s0 * 128:1024 + (s0 + 1) * 128, :],
                                      self.PT[:, 2048:3072]), "cVs0")
                sy.wait_ge(self.S("sDve"), dv_now)
                self.inc(sy.dma_start(agin[1024 + s1 * 128:1024 + (s1 + 1) * 128, :],
                                      self.PT[:, 3072:4096]), "cVs1")

    def emit_k_transpose(self, blk):
        nc = self.nc
        agin = self.agin.ap()
        with nc.psum_tensor("ptrb", [128, 1024], BF16) as ptrb:
            pe0 = self.val("sPe")
            dv0 = self.val("sDve")
            stv = []

            @blk.tensor
            def _(t):
                for s in range(NSLOT):
                    if s == 0:
                        t.wait_ge(self.S("sDve"), dv0)
                    if s >= 1:
                        t.wait_ge(self.S("sDve"), dv0 + s)
                    for h in range(HKV):
                        self.inc(t.transpose(ptrb[:, h * 128:(h + 1) * 128],
                                             self.QB[:, s * 1024 + h * 128:s * 1024 + (h + 1) * 128],
                                             self.identb), "sPe")

            @blk.sync
            def _(sy):
                for s in range(NSLOT):
                    sy.wait_ge(self.S("sDve"), dv0 + s + 1)
                    stv.append(self.inc(sy.dma_start(agin[s * 128:(s + 1) * 128, :],
                                                     self.PT[:, (s % 2) * 1024:(s % 2) * 1024 + 1024]),
                                        f"cKb{s % 2}"))

            @blk.vector
            def _(v):
                for s in range(NSLOT):
                    v.wait_ge(self.S("sPe"), pe0 + 8 * (s + 1))
                    if s >= 2:
                        v.wait_ge(self.S(f"cKb{s % 2}"), stv[s - 2])
                    self.inc(v.tensor_copy(self.PT[:, (s % 2) * 1024:(s % 2) * 1024 + 1024],
                                           ptrb[:, :]), "sDve")

    def emit_qproj(self, blk, xnsrc, qp, xn_wait=None, extra_pe_waits=()):
        pe0 = self.val("sPe")
        dv0 = self.val("sDve")
        wv = []

        @blk.sync
        def _(sy):
            sy.wait_ge(self.S("sCc"), 2)
            i = 0
            for oct_ in range(4):
                for kb in range(8):
                    if i >= 2:
                        sy.wait_ge(self.S("sPe"), pe0 + (i - 1))
                    srcw = self.w_window("wq", oct_, kb)
                    wv.append(self.inc(sy.dma_start(
                        self.WS[:, (i % 2) * 4096:(i % 2) * 4096 + 4096], srcw), f"cQw{i % 2}"))
                    i += 1

        @blk.tensor
        def _(t):
            if xn_wait is not None:
                t.wait_ge(self.S("sDve"), xn_wait)
            for (sm, vvv) in extra_pe_waits:
                t.wait_ge(self.S(sm), vvv)
            i = 0
            for oct_ in range(4):
                if oct_ >= 1:
                    t.wait_ge(self.S("sDve"), dv0 + 8 * oct_)
                for kb in range(8):
                    t.wait_ge(self.S(f"cQw{i % 2}"), wv[i])
                    base = (i % 2) * 4096
                    for m in range(8):
                        for k in range(4):
                            st = (kb == 0 and k == 0)
                            sp = (kb == 7 and k == 3)
                            ins = t.matmul(qp[:, m * 512:(m + 1) * 512],
                                           self.WS[:, base + k * 1024 + m * 128:base + k * 1024 + (m + 1) * 128],
                                           xnsrc[:, (kb * 4 + k) * 512:(kb * 4 + k) * 512 + 512],
                                           start=st, stop=sp)
                    self.inc(ins, "sPe")
                    i += 1

        @blk.vector
        def _(v):
            for oct_ in range(4):
                v.wait_ge(self.S("sPe"), pe0 + 8 * (oct_ + 1))
                for m in range(8):
                    self.inc(v.tensor_copy(
                        self.QB[:, (oct_ * 8 + m) * 512:(oct_ * 8 + m) * 512 + 512],
                        qp[:, m * 512:(m + 1) * 512]), "sDve")

    def emit_prologue(self, blk):
        nc = self.nc
        self.emit_coop_cast(blk)
        self.emit_setup(blk)
        if self.stage in ("s1", "s0a", "s0b", "s0c"):
            return
        self.emit_wprep(blk)
        if self.stage == "s2":
            return

        wprep_pe = self.val("sPe")
        wprep_dve = self.val("sDve")

        with (nc.psum_tensor("pAB", [128, 1024], F32) as pAB,
              nc.psum_tensor("pCD", [128, 1024], F32) as pCD,
              nc.psum_tensor("pEF", [128, 1024], F32) as pEF,
              nc.psum_tensor("pGH", [128, 1024], F32) as pGH):
            self.pAB, self.pCD, self.pEF, self.pGH = pAB, pCD, pEF, pGH
            self.pE, self.pF = pEF[:, 0:512], pEF[:, 512:1024]

            @blk.sync
            def _(sy):
                sy.wait_ge(self.S("sPe"), wprep_pe)
                sy.wait_ge(self.S("sDve"), wprep_dve)
                src = bass.AP(self.xin.tensor, 0, [[1024, 128], [131072, 32], [1, 512]])
                self.inc(sy.dma_start(self.R1[:, :], src), "cX0")
            x0_ld = self.val("cX0")

            @blk.vector
            def _(v):
                v.wait_ge(self.S("cX0"), x0_ld)
            pw = [("cPrS", self.val("cPrS")), ("cPoS", self.val("cPoS"))]
            self.emit_ln(blk, self.R1, self.AO, self.epsln,
                         act_waits=pw, dve_waits=pw)
            xn0b0_done = self.val("sDve")
            if self.stage == "s3":
                return
            self.emit_kv_proj(blk, 0, self.AO, xn_wait=xn0b0_done)
            if self.stage == "s4":
                return
            kvb0_evac = self.val("sDve")
            kv_waits = [("cVs0", self.val("cVs0")), ("cVs1", self.val("cVs1"))]

            kv0_pe = self._kv_prev_pe
            @blk.sync
            def _(sy):
                sy.wait_ge(self.S("sDve"), xn0b0_done)
                sy.wait_ge(self.S("sPe"), kv0_pe)
                src = bass.AP(self.xin.tensor, 512, [[1024, 128], [131072, 32], [1, 512]])
                self.inc(sy.dma_start(self.R1[:, :], src), "cX0")
            x1_ld = self.val("cX0")

            @blk.vector
            def _(v):
                v.wait_ge(self.S("cX0"), x1_ld)
            self.emit_ln(blk, self.R1, self.XN, self.epsln, extra_dst_dram=self.xn0b1,
                         act_waits=kv_waits, dve_waits=kv_waits,
                         pe_waits=[("sDve", kvb0_evac)])

            xn0b1_done = self.val("sDve")
            self.emit_kv_proj(blk, 1, self.XN, xn_wait=xn0b1_done)

        if self.stage == "s5":
            return
        self.emit_k_transpose(blk)
        ktr_dve = self.val("sDve")
        if self.stage == "s6":
            return

        if self.stage == "s6b":
            return
        ag_waits = [(s, self.val(s)) for s in ["cPrS", "cPoS", "cVs0", "cVs1", "cKb0", "cKb1"]]
        @blk.gpsimd
        def _(g):
            for (sm, vv) in ag_waits:
                g.wait_ge(self.S(sm), vv)
            self.inc(g.collective_compute(
                "AllGather", ALU.bypass,
                replica_groups=[list(range(NCORE))],
                ins=[self.agin.ap().opt()],
                outs=[self.agout.ap().opt()]), "sCc")
        if self.stage == "s7":
            return
        with nc.psum_tensor("qp", [128, 4096], F32) as qp:
            self.emit_qproj(blk, self.AO, qp, xn_wait=ktr_dve)

    def emit_debug_stores(self, stores):
        nc = self.nc
        self.dbg_outputs_names = [s[0] for s in stores]
        with nc.Block() as blk:
            @blk.sync
            def _(sy):
                for (name, region, shape, dt) in stores:
                    ap = self.nc.dram_tensor(name, shape, dt, kind="ExternalOutput").ap()
                    self.inc(sy.dma_start(ap, region), "cDbg")
                sy.wait_ge(self.S("cDbg"), self.val("cDbg"))

    def build(self):
        nc = self.nc
        if self.stage == "full":
            nc2 = self.build_full()
            self.dbg_outputs_names = ["yout"]
            self.es.close()
            return nc2
        with nc.Block() as blk:
            self.emit_prologue(blk)
        if self.stage.startswith("s"):
            self.emit_debug_stores([
                ("d_mark", self.consts[:, 0:128], [128, 128], F32),
            ])
            self.es.close()
            return nc
        if self.stage == "xn0":
            self.emit_debug_stores([
                ("d_xn0b0", self.AO[:, :], [128, 16384], BF16),
                ("d_xn0b1", self.XN[:, :], [128, 16384], BF16),
            ])
            self.es.close()
            return nc
        if self.stage == "att":
            with nc.Block() as blk2:
                self.emit_attention(blk2, 0, q_ready=self.val("sDve"), wait_cc=True)
            self.emit_debug_stores([
                ("d_ao", self.AO[:, :], [128, 16384], BF16),
            ])
            self.es.close()
            return nc
        if self.stage == "kv":
            with nc.Block() as blk2:
                @blk2.gpsimd
                def _(g):
                    g.wait_ge(self.S("sCc"), self.val("sCc"))
            self.emit_debug_stores([
                ("d_agin", self.agin.ap(), [3072, 1024], BF16),
                ("d_q", self.QB[:, :], [128, 16384], BF16),
            ])
            self.es.close()
            return nc
        if self.stage == "full":
            nc2 = self.build_full()
            self.dbg_outputs_names = ["yout"]
            self.es.close()
            return nc2
        if self.stage in ("x1", "x2", "x3", "x4"):
            with nc.Block() as blk:
                pass
            self.es.close()
            return nc
        raise NotImplementedError(self.stage)

    def emit_attention(self, blk, half, q_ready, wait_cc=False, r1_wait=None):
        _sfx = str(half)
        nc = self.nc
        R1b = self.R1[:, :].bitcast(BF16)
        XNf = self.XN[:, :].bitcast(F32)
        AVACC = XNf[:, 0:4096]
        DEN = XNf[0:1, 4096:8192]
        PTf = self.PT[:, :].bitcast(F32)

        xnst = self.val("cXnS")
        kv_ctr = 0
        kv_gate = [None, None]
        ebank_prev = None
        ev_last = None
        last_bc_pe = [None, None]
        last_bc_tt = [None, None]
        first_pe_wait = q_ready
        RSTRIDE = 3072 * 1024

        with (nc.psum_tensor("pSCa" + _sfx, [128, 1024], F32) as pSCa,
              nc.psum_tensor("pSCb" + _sfx, [128, 1024], F32) as pSCb,
              nc.psum_tensor("pav" + _sfx, [128, 512], F32) as pav,
              nc.psum_tensor("pden" + _sfx, [128, 512], F32) as pden,
              nc.psum_tensor("pbc" + _sfx, [128, 512], F32) as pbc,
              nc.psum_tensor("pbcB" + _sfx, [128, 512], F32) as pbcB):
            pSC = [pSCa, pSCb]
            pbc2 = [pbc, pbcB]

            def emit_av(t, prev, ev2, jlist):
                for j in jlist:
                    jp = j // 2
                    if prev["lastg"]:
                        t.wait_ge(self.S("sDve"),
                                  max(prev["msk_base"] + j + 1,
                                      ev2 if (j == 0 and ev2 is not None) else 0))
                    else:
                        if j % 2 == 0:
                            t.wait_ge(self.S("sAct"), prev["exp_base"] + jp + 1)
                        if j == 0 and ev2 is not None:
                            t.wait_ge(self.S("sDve"), ev2)
                    ro = j if prev["b"] == 0 else 7 - j
                    lhsT = R1b[:, prev["vbase"] + ro * 1024 + prev["h"] * 128:
                               prev["vbase"] + ro * 1024 + (prev["h"] + 1) * 128]
                    pt_t = self.PT[:, j * 512:(j + 1) * 512]
                    t.matmul(pav[:, :], lhsT, pt_t, start=(j == 0), stop=(j == 7))
                    self.inc(t.matmul(pden[:, :], self.onesb[:, 0:128], pt_t,
                                      start=(j == 0), stop=(j == 7)), "sPe")

            def emit_evac(prev, av_end):
                @blk.vector
                def _(v, prev=prev, av_end=av_end):
                    if prev["first_evac"]:
                        v.wait_ge(self.S("cXnS"), xnst)
                    v.wait_ge(self.S("sPe"), av_end)
                    dst = AVACC[:, prev["h"] * 512:(prev["h"] + 1) * 512]
                    dden = DEN[0:1, prev["h"] * 512:(prev["h"] + 1) * 512]
                    self.wait(v, "sDve")
                    if prev["g"] == 0:
                        self.inc(v.tensor_copy(dst, pav[:, :]), "sDve")
                        self.inc(v.tensor_copy(dden, pden[0:1, :]), "sDve")
                    else:
                        self.inc(v.tensor_add(dst, dst, pav[:, :]), "sDve")
                        self.inc(v.tensor_add(dden, dden, pden[0:1, :]), "sDve")
                return self.val("sDve")

            for si in range(4):
                s = half * 4 + si
                b = half
                ngrp = NGRP[s]

                @blk.sync
                def _(sy, s=s, si=si):
                    if wait_cc and si == 0:
                        sy.wait_ge(self.S("sCc"), 3)
                    sy.wait_ge(self.S("sDve"), self.val("sDve"))
                    self.inc(sy.dma_start(
                        self.mask8[:, :],
                        bass.AP(self.maskt.tensor, s * 131072, [[1024, 128], [1, 1024]])),
                        "cMsk")
                mask_ld = self.val("cMsk")

                prev = None
                for g in range(ngrp):
                    p = kv_ctr % 2
                    kv_ctr += 1
                    kbase = p * 16384
                    vbase = p * 16384 + 8192
                    lastg = (g == ngrp - 1)
                    slotp = g if b == 0 else 4 + g

                    gate = kv_gate[p]
                    first_kv = (si == 0 and g == 0)
                    @blk.sync
                    def _(sy, kbase=kbase, vbase=vbase, slotp=slotp, gate=gate,
                          first_kv=first_kv):
                        if first_kv and r1_wait is not None:
                            sy.wait_ge(self.S(r1_wait[0]), r1_wait[1])
                        if gate is not None:
                            sy.wait_ge(self.S("sPe"), gate)
                        kdst = R1b[:, kbase:kbase + 8192].rearrange("p (r c) -> p r c", r=8)
                        vdst = R1b[:, vbase:vbase + 8192].rearrange("p (r c) -> p r c", r=8)
                        ksrc = bass.AP(self.agout, slotp * 128 * 1024,
                                       [[1024, 128], [RSTRIDE, 8], [1, 1024]])
                        vsrc = bass.AP(self.agout, 1024 * 1024 + slotp * 128 * 1024,
                                       [[1024, 128], [RSTRIDE, 8], [1, 1024]])
                        self.inc(sy.dma_start(kdst, ksrc), f"cKV{p}")
                        self.inc(sy.dma_start(vdst, vsrc), f"cKV{p}")
                    kv_ld = self.val(f"cKV{p}")

                    for h in range(HKV):
                        pe0 = self.val("sPe")
                        ebu = self.val("sAct")

                        @blk.tensor
                        def _(t, si=si, b=b, h=h, kbase=kbase, kv_ld=kv_ld, p=p,
                              prev=prev, ev2=ev_last, ebu=ebu, ebp=ebank_prev,
                              fpw=first_pe_wait, h0=(h == 0)):
                            if fpw is not None:
                                t.wait_ge(self.S("sDve"), fpw)
                            if h0:
                                t.wait_ge(self.S(f"cKV{p}"), kv_ld)

                            def sc_pair(jp):
                                bank = jp % 2
                                if jp >= 2:
                                    t.wait_ge(self.S("sAct"), ebu + (jp - 2) + 1)
                                elif ebp is not None:
                                    t.wait_ge(self.S("sAct"), ebp + 3 + jp)
                                for jj in range(2):
                                    j = 2 * jp + jj
                                    ro = j if b == 0 else 7 - j
                                    lhsT = R1b[:, kbase + ro * 1024 + h * 128:
                                               kbase + ro * 1024 + (h + 1) * 128]
                                    rhs = bass.AP(self.QB, (4 * h) * 512 + si * 128,
                                                  [[self.QB[0:1, 0:1].ap[0][0], 128], [512, 4], [1, 128]])
                                    self.inc(t.matmul(pSC[bank][:, jj * 512:(jj + 1) * 512],
                                                      lhsT, rhs, start=True, stop=True), "sPe")

                            if prev is None:
                                for jp in range(4):
                                    sc_pair(jp)
                            else:
                                emit_av(t, prev, ev2, [0, 1, 2, 3, 4, 5])
                                sc_pair(0)
                                emit_av(t, prev, ev2, [6, 7])
                                sc_pair(1)
                                sc_pair(2)
                                sc_pair(3)
                        first_pe_wait = None
                        if prev is not None and prev["h"] == HKV - 1:
                            kv_gate[prev["p"]] = pe0 + 10

                        @blk.scalar
                        def _(s_, pe0=pe0, gates=((8, 12, 14, 16) if prev is not None
                                                  else (2, 4, 6, 8))):
                            for jp in range(4):
                                s_.wait_ge(self.S("sPe"), pe0 + gates[jp])
                                self.inc(s_.activation(self.PT[:, jp * 1024:(jp + 1) * 1024],
                                                       pSC[jp % 2][:, :], AF.Exp,
                                                       scale=ATT_SCALE), "sAct")
                        ebank_prev = ebu

                        msk_base = self.val("sDve")
                        if lastg:
                            @blk.vector
                            def _(v, ebu=ebu, mask_ld=mask_ld):
                                v.wait_ge(self.S("cMsk"), mask_ld)
                                for j in range(8):
                                    jp = j // 2
                                    v.wait_ge(self.S("sAct"), ebu + jp + 1)
                                    pt_t = self.PT[:, j * 512:(j + 1) * 512]
                                    mview = bass.AP(self.mask8,
                                                    self.mask8[0:1, 0:1].offset + j * 128,
                                                    [[self.mask8[0:1, 0:1].ap[0][0], 128], [0, 4], [1, 128]])
                                    self.inc(v.tensor_mul(pt_t, pt_t, mview), "sDve")

                        if prev is not None:
                            ev_last = emit_evac(prev, pe0 + 10)

                        prev = {"h": h, "g": g, "p": p, "b": b, "vbase": vbase,
                                "lastg": lastg, "msk_base": msk_base, "exp_base": ebu,
                                "first_evac": (si == 0 and g == 0 and h == 0)}

                fl0 = self.val("sPe")
                @blk.tensor
                def _(t, prev=prev, ev2=ev_last):
                    emit_av(t, prev, ev2, [0, 1, 2, 3, 4, 5, 6, 7])
                kv_gate[prev["p"]] = fl0 + 8
                ev_last = emit_evac(prev, fl0 + 8)

                for h in range(HKV):
                    q = h % 2
                    @blk.vector
                    def _(v, h=h, q=q, lbp=last_bc_pe[q]):
                        if lbp is not None:
                            v.wait_ge(self.S("sPe"), lbp)
                        self.wait(v, "sDve")
                        self.inc(v.reciprocal(PTf[0:1, q * 512:q * 512 + 512],
                                              DEN[0:1, h * 512:(h + 1) * 512]), "sDve")
                    rc_done = self.val("sDve")

                    @blk.tensor
                    def _(t, q=q, rc_done=rc_done, lbt=last_bc_tt[q]):
                        t.wait_ge(self.S("sDve"), rc_done)
                        if lbt is not None:
                            t.wait_ge(self.S("sDve"), lbt)
                        self.inc(t.matmul(pbc2[q][:, :], self.consts[0:1, 512:640],
                                          PTf[0:1, q * 512:q * 512 + 512],
                                          start=True, stop=True), "sPe")
                    last_bc_pe[q] = self.val("sPe")

                    @blk.vector
                    def _(v, h=h, si=si, q=q, lbp=last_bc_pe[q]):
                        v.wait_ge(self.S("sPe"), lbp)
                        out = bass.AP(self.AO, (4 * h) * 512 + si * 128,
                                      [[self.AO[0:1, 0:1].ap[0][0], 128], [512, 4], [1, 128]])
                        self.inc(v.tensor_mul(out, AVACC[:, h * 512:(h + 1) * 512],
                                              pbc2[q][:, :]), "sDve")
                    last_bc_tt[q] = self.val("sDve")
        return self.val("sDve")

    def emit_oproj(self, blk, half, pp, ao_ready, r1_free_pe):
        wv = []
        pe0 = self.val("sPe")
        dv0 = self.val("sDve")

        @blk.sync
        def _(sy):
            sy.wait_ge(self.S("sPe"), r1_free_pe)
            src = bass.AP(self.xin.tensor, half * 512,
                          [[1024, 128], [131072, 32], [1, 512]])
            self.inc(sy.dma_start(self.R1[:, :], src), "cX0")
            i = 0
            for oct_ in range(4):
                for kb in range(8):
                    if i >= 2:
                        sy.wait_ge(self.S("sPe"), pe0 + (i - 1))
                    srcw = self.w_window("wo", oct_, kb)
                    wv.append(self.inc(sy.dma_start(
                        self.WS[:, (i % 2) * 4096:(i % 2) * 4096 + 4096], srcw), f"cQw{i % 2}"))
                    i += 1
        x0_ld = self.val("cX0")

        @blk.tensor
        def _(t):
            t.wait_ge(self.S("sDve"), ao_ready)
            i = 0
            for oct_ in range(4):
                if oct_ >= 1:
                    t.wait_ge(self.S("sDve"), dv0 + 8 * oct_)
                for kb in range(8):
                    t.wait_ge(self.S(f"cQw{i % 2}"), wv[i])
                    base = (i % 2) * 4096
                    for m in range(8):
                        for k in range(4):
                            st = (kb == 0 and k == 0)
                            sp = (kb == 7 and k == 3)
                            ins = t.matmul(pp[:, m * 512:(m + 1) * 512],
                                           self.WS[:, base + k * 1024 + m * 128:base + k * 1024 + (m + 1) * 128],
                                           self.AO[:, (kb * 4 + k) * 512:(kb * 4 + k) * 512 + 512],
                                           start=st, stop=sp)
                    self.inc(ins, "sPe")
                    i += 1

        @blk.vector
        def _(v):
            v.wait_ge(self.S("cX0"), x0_ld)
            for oct_ in range(4):
                v.wait_ge(self.S("sPe"), pe0 + 8 * (oct_ + 1))
                for m in range(8):
                    mg = oct_ * 8 + m
                    r1c = self.R1[:, mg * 512:(mg + 1) * 512]
                    self.inc(v.tensor_add(r1c, r1c, pp[:, m * 512:(m + 1) * 512]), "sDve")
        return self.val("sDve")

    def emit_ffn(self, blk, half, pp, xn_ready, pp_free_dve, ws_free_pe=0):
        self.ws_free_pe = ws_free_pe
        self.y_ev = []
        self.bb_free = 0
        C = self.consts
        small = self.small
        RT, RMS = pp[:, 0:512], pp[:, 512:1024]
        TRa, TRb = pp[:, 1024:1152], pp[:, 1536:1664]
        BC, Y = pp[:, 2048:2560], pp[:, 2560:3584]
        T_in, T_r = small[:, 0:512], small[:, 512:1024]
        SC = lambda a, b: small[:, a:b]
        Z = SC(1024, 1088)
        UP, VP = SC(1088, 1104), SC(1104, 1120)
        AT = SC(1152, 1216)
        ET = SC(1216, 1280)
        RC = SC(1280, 1284)
        SS = SC(1288, 1304)
        HH = SC(1312, 1376)
        B1 = SC(1376, 1440)
        PRER, POSTR = SC(1440, 1456), SC(1456, 1472)
        PREE = SC(1472, 1488)
        PRES, PRERS = SC(1488, 1492), SC(1492, 1496)
        HPRE, HPOST = SC(1496, 1512), SC(1512, 1528)
        POSTE = SC(1528, 1544)
        POSTS, POSTRS = SC(1544, 1548), SC(1548, 1552)
        Bt_in = small[:, 1536 + 512:1536 + 1024] if False else small[:, 2048:2560]
        XPRE = self.AO[:, 0:4096]
        YB = self.AO[:, 4096:8192]
        BB = lambda r: self.AO[:, 8192 + r * 512:8192 + (r + 1) * 512]
        BROWS = self.PT[:, 3584:4096]

        pe0 = self.val("sPe")
        act0 = self.val("sAct")

        @blk.scalar
        def _(s_):
            s_.wait_ge(self.S("sDve"), xn_ready)
            for k in range(DC):
                x2t = self.PT[:, 2048 + (k % 2) * 512:2048 + (k % 2) * 512 + 512]
                if k >= 2:
                    s_.wait_ge(self.S("sPe"), pe0 + (k - 2) + 1)
                self.inc(s_.activation(x2t, self.XN[:, k * 512:(k + 1) * 512], AF.Square),
                         "sAct")

        @blk.tensor
        def _(t):
            t.wait_ge(self.S("sDve"), pp_free_dve)
            for k in range(DC):
                x2t = self.PT[:, 2048 + (k % 2) * 512:2048 + (k % 2) * 512 + 512]
                t.wait_ge(self.S("sAct"), act0 + k + 1)
                t.matmul(RT, self.wrouter[:, k * 128:(k + 1) * 128],
                         self.XN[:, k * 512:(k + 1) * 512],
                         start=(k == 0), stop=(k == DC - 1))
                self.inc(t.matmul(RMS, self.onesb[:, 0:128], x2t,
                                  start=(k == 0), stop=(k == DC - 1)), "sPe")
        rt_done = self.val("sPe")

        @blk.vector
        def _(v):
            v.wait_ge(self.S("sPe"), rt_done)
            self.wait(v, "sDve")
            v.tensor_copy(T_in[0:16, :], RT[0:16, :])
            v.tensor_copy(T_in[32:36, :], RT[32:36, :])
            v.tensor_copy(T_in[64:68, :], RT[64:68, :])
            self.inc(v.tensor_copy(T_r[0:1, :], RMS[0:1, :]), "sDve")
        tin_done = self.val("sDve")

        tr_dve = []
        for a in range(4):
            pe_a = self.val("sPe")
            @blk.tensor
            def _(t, a=a, tin_done=tin_done):
                t.wait_ge(self.S("sDve"), tin_done)
                if a >= 1:
                    t.wait_ge(self.S("sDve"), tr_dve[a - 1])
                t.transpose(TRa, T_in[:, a * 128:(a + 1) * 128], self.ident_f)
                self.inc(t.transpose(TRb, T_r[:, a * 128:(a + 1) * 128], self.ident_f), "sPe")
            pe_a1 = self.val("sPe")

            @blk.scalar
            def _(s_, a=a, pe_a1=pe_a1):
                s_.wait_ge(self.S("sPe"), pe_a1)
                self.inc(s_.activation(RC[:, a:a + 1], bass.AP(TRb.tensor, TRb.offset, [[TRb.ap[0][0], 128], [1, 1]]),
                                       AF.Ln, bias=C[:, 149:150], scale=1.0 / D), "sAct")
                self.wait(s_, "sAct")
                self.inc(s_.activation(RC[:, a:a + 1], RC[:, a:a + 1], AF.Exp, scale=-0.5), "sAct")
            rc_done = self.val("sAct")

            @blk.vector
            def _(v, a=a, rc_done=rc_done):
                v.wait_ge(self.S("sAct"), rc_done)
                self.inc(v.tensor_scalar(Z[:, a * 16:(a + 1) * 16],
                                         bass.AP(TRa.tensor, TRa.offset, [[TRa.ap[0][0], 128], [1, 16]]),
                                         RC[:, a:a + 1], 10.0, ALU.mult, ALU.mult), "sDve")
                self.wait(v, "sDve")
                self.inc(v.tensor_add(Z[:, a * 16:(a + 1) * 16], Z[:, a * 16:(a + 1) * 16],
                                      C[:, 440:456]), "sDve")
                self.inc(v.tensor_scalar(PRER[:, a * 4:(a + 1) * 4],
                                         bass.AP(TRa.tensor, TRa.offset + 32, [[TRa.ap[0][0], 128], [1, 4]]),
                                         RC[:, a:a + 1], None, ALU.mult), "sDve")
                self.wait(v, "sDve")
                self.inc(v.tensor_add(PRER[:, a * 4:(a + 1) * 4], PRER[:, a * 4:(a + 1) * 4],
                                      C[:, 432:436]), "sDve")
                self.inc(v.tensor_scalar(POSTR[:, a * 4:(a + 1) * 4],
                                         bass.AP(TRa.tensor, TRa.offset + 64, [[TRa.ap[0][0], 128], [1, 4]]),
                                         RC[:, a:a + 1], None, ALU.mult), "sDve")
                self.wait(v, "sDve")
                self.inc(v.tensor_add(POSTR[:, a * 4:(a + 1) * 4], POSTR[:, a * 4:(a + 1) * 4],
                                      C[:, 436:440]), "sDve")
            tr_dve.append(self.val("sDve"))
        z_ready0 = self.val("sDve")

        zs = Z[0:1, 0:1]
        zst = zs.ap[0][0]
        MU = SC(1552, 1568)
        mu_aij = bass.AP(Z.tensor, MU.offset, [[zst, 128], [4, 4], [1, 4], [0, 4]])
        z_aij = bass.AP(Z.tensor, Z.offset, [[zst, 128], [16, 4], [4, 4], [1, 4]])
        z_aji = bass.AP(Z.tensor, Z.offset, [[zst, 128], [16, 4], [1, 4], [4, 4]])
        et_red = bass.AP(Z.tensor, ET.offset, [[zst, 128], [16, 4], [4, 4], [1, 4]])
        up_aij = bass.AP(Z.tensor, UP.offset, [[zst, 128], [4, 4], [1, 4], [0, 4]])
        up_aji = bass.AP(Z.tensor, UP.offset, [[zst, 128], [4, 4], [0, 4], [1, 4]])
        vp_aij = bass.AP(Z.tensor, VP.offset, [[zst, 128], [4, 4], [0, 4], [1, 4]])

        @blk.vector
        def _(v):
            v.wait_ge(self.S("sDve"), z_ready0)
            self.inc(v.reduce_max(MU[:, :], z_aij, axis=AX.X), "sDve")
            self.wait(v, "sDve")
            self.inc(v.tensor_tensor(Z[:, :], Z[:, :], mu_aij, ALU.subtract), "sDve")
        z_ready = self.val("sDve")

        @blk.scalar
        def _(s_):
            s_.wait_ge(self.S("sDve"), z_ready)
            self.inc(s_.activation(Z[:, :], Z[:, :], AF.Exp), "sAct")
        ez_done = self.val("sAct")

        for it in range(50):
            @blk.vector
            def _(v, it=it):
                if it == 0:
                    v.wait_ge(self.S("sAct"), ez_done)
                    self.wait(v, "sDve")
                    self.inc(v.reduce_sum(SS[:, :], z_aij, axis=AX.X), "sDve")
                else:
                    self.wait(v, "sDve")
                    self.inc(v.tensor_tensor(ET[:, :], z_aij, vp_aij, ALU.mult), "sDve")
                    self.wait(v, "sDve")
                    self.inc(v.reduce_sum(SS[:, :], et_red, axis=AX.X), "sDve")
                self.wait(v, "sDve")
                self.inc(v.reciprocal(UP[:, :], SS[:, :]), "sDve")
                self.wait(v, "sDve")
                self.inc(v.tensor_tensor(ET[:, :], z_aji, up_aji, ALU.mult), "sDve")
                self.wait(v, "sDve")
                self.inc(v.reduce_sum(SS[:, :], et_red, axis=AX.X), "sDve")
                self.wait(v, "sDve")
                self.inc(v.reciprocal(VP[:, :], SS[:, :]), "sDve")
        sk_av = self.val("sDve")

        @blk.vector
        def _(v):
            self.wait(v, "sDve")
            self.inc(v.tensor_tensor(B1[:, :], z_aij, vp_aij, ALU.mult), "sDve")
            self.wait(v, "sDve")
            self.inc(v.tensor_tensor(HH[:, :], B1[:, :], up_aij, ALU.mult), "sDve")
        h_done = self.val("sDve")

        @blk.vector
        def _(v):
            v.wait_ge(self.S("sDve"), h_done)
            self.wait(v, "sDve")
            self.inc(v.tensor_scalar(B1[:, :], HH[:, :], C[:, 144:145], None, ALU.mult), "sDve")
            self.wait(v, "sDve")
            ipv = bass.AP(C, C[0:1, 128:129].offset, [[C[0:1, 0:1].ap[0][0], 128], [0, 4], [1, 16]])
            self.inc(v.tensor_scalar(HH[:, :], ipv, C[:, 145:146], None, ALU.mult), "sDve")
            self.wait(v, "sDve")
            self.inc(v.tensor_add(B1[:, :], B1[:, :], HH[:, :]), "sDve")
        bres_done = self.val("sDve")

        PM = SC(1304, 1312)
        @blk.vector
        def _(v):
            v.wait_ge(self.S("sDve"), z_ready)
            self.inc(v.reduce_max(PM[:, 0:4],
                                  bass.AP(PRER.tensor, PRER.offset, [[PRER.ap[0][0], 128], [4, 4], [1, 4]]),
                                  axis=AX.X), "sDve")
            self.inc(v.reduce_max(PM[:, 4:8],
                                  bass.AP(POSTR.tensor, POSTR.offset, [[POSTR.ap[0][0], 128], [4, 4], [1, 4]]),
                                  axis=AX.X), "sDve")
            self.wait(v, "sDve")
            pmv = bass.AP(PM.tensor, PM.offset, [[PM.ap[0][0], 128], [1, 4], [0, 4]])
            pmv2 = bass.AP(PM.tensor, PM.offset + 4, [[PM.ap[0][0], 128], [1, 4], [0, 4]])
            self.inc(v.tensor_tensor(PRER[:, :], PRER[:, :], pmv, ALU.subtract), "sDve")
            self.inc(v.tensor_tensor(POSTR[:, :], POSTR[:, :], pmv2, ALU.subtract), "sDve")
        sm_stab = self.val("sDve")

        @blk.scalar
        def _(s_):
            s_.wait_ge(self.S("sDve"), sm_stab)
            self.inc(s_.activation(PREE[:, :], PRER[:, :], AF.Exp), "sAct")
            self.inc(s_.activation(POSTE[:, :], POSTR[:, :], AF.Exp), "sAct")
        pp_e = self.val("sAct")

        @blk.vector
        def _(v):
            v.wait_ge(self.S("sAct"), pp_e)
            self.inc(v.reduce_sum(PRES[:, :],
                                  bass.AP(PREE.tensor, PREE.offset, [[PREE.ap[0][0], 128], [4, 4], [1, 4]]),
                                  axis=AX.X), "sDve")
            self.inc(v.reduce_sum(POSTS[:, :],
                                  bass.AP(POSTE.tensor, POSTE.offset, [[POSTE.ap[0][0], 128], [4, 4], [1, 4]]),
                                  axis=AX.X), "sDve")
            self.wait(v, "sDve")
            self.inc(v.reciprocal(PRERS[:, :], PRES[:, :]), "sDve")
            self.inc(v.reciprocal(POSTRS[:, :], POSTS[:, :]), "sDve")
            self.wait(v, "sDve")
            pre_rs_v = bass.AP(PRERS.tensor, PRERS.offset, [[PRERS.ap[0][0], 128], [1, 4], [0, 4]])
            post_rs_v = bass.AP(POSTRS.tensor, POSTRS.offset, [[POSTRS.ap[0][0], 128], [1, 4], [0, 4]])
            self.inc(v.tensor_tensor(HPRE[:, :], PREE[:, :], pre_rs_v, ALU.mult), "sDve")
            self.inc(v.tensor_tensor(HPOST[:, :], POSTE[:, :], post_rs_v, ALU.mult), "sDve")
        hp_done = self.val("sDve")

        @blk.vector
        def _(v):
            v.wait_ge(self.S("sDve"), max(bres_done, hp_done))
            for a in range(4):
                bt = Bt_in[:, a * 128:(a + 1) * 128]
                v.tensor_copy(bt[:, 0:16], B1[:, a * 16:(a + 1) * 16])
                v.tensor_copy(bt[:, 16:20], HPRE[:, a * 4:(a + 1) * 4])
                v.tensor_copy(bt[:, 20:24], HPOST[:, a * 4:(a + 1) * 4])
            self.inc(v.tensor_copy(Bt_in[:, 24:25], B1[:, 0:1]), "sDve")
        bt_done = self.val("sDve")

        br_pe0 = self.val("sPe")
        br_dv0 = self.val("sDve")
        @blk.tensor
        def _(t):
            t.wait_ge(self.S("sDve"), bt_done)
            for a in range(4):
                if a >= 1:
                    t.wait_ge(self.S("sDve"), br_dv0 + a)
                self.inc(t.transpose(TRa, Bt_in[:, a * 128:(a + 1) * 128], self.ident_f), "sPe")

        @blk.vector
        def _(v):
            for a in range(4):
                v.wait_ge(self.S("sPe"), br_pe0 + a + 1)
                self.inc(v.tensor_copy(BROWS[0:24, a * 128:(a + 1) * 128],
                                       bass.AP(TRa.tensor, TRa.offset, [[TRa.ap[0][0], 24], [1, 128]])), "sDve")
        brows_done = self.val("sDve")

        bb_pe0 = self.val("sPe")
        bb_dv0 = self.val("sDve")
        @blk.tensor
        def _(t):
            t.wait_ge(self.S("sDve"), brows_done)
            for n in range(4):
                if n >= 1:
                    t.wait_ge(self.S("sDve"), bb_dv0 + n)
                self.inc(t.matmul(BC, self.selt_s[:, (16 + n) * 128:(17 + n) * 128],
                                  self.PT[:, 3584:4096], start=True, stop=True), "sPe")

        @blk.vector
        def _(v):
            self.wait(v, "sDve")
            for n in range(4):
                v.wait_ge(self.S("sPe"), bb_pe0 + n + 1)
                self.inc(v.tensor_copy(BB(n), BC[:, :]), "sDve")
        bbpre_done = self.val("sDve")

        @blk.vector
        def _(v):
            v.wait_ge(self.S("sDve"), bbpre_done)
            for c in range(8):
                t0 = self.PT[:, 0:512]
                t1 = self.PT[:, 512:1024]
                t2 = self.PT[:, 1024:1536]
                t3 = self.PT[:, 1536:2048]
                self.wait(v, "sDve")
                v.tensor_mul(t0, self.XN[:, (0 * 8 + c) * 512:(0 * 8 + c) * 512 + 512], BB(0))
                v.tensor_mul(t1, self.XN[:, (1 * 8 + c) * 512:(1 * 8 + c) * 512 + 512], BB(1))
                v.tensor_mul(t2, self.XN[:, (2 * 8 + c) * 512:(2 * 8 + c) * 512 + 512], BB(2))
                self.inc(v.tensor_mul(t3, self.XN[:, (3 * 8 + c) * 512:(3 * 8 + c) * 512 + 512], BB(3)), "sDve")
                self.wait(v, "sDve")
                v.tensor_add(t0, t0, t1)
                self.inc(v.tensor_add(t2, t2, t3), "sDve")
                self.wait(v, "sDve")
                self.inc(v.tensor_add(XPRE[:, c * 512:(c + 1) * 512], t0, t2), "sDve")
        xpre_done = self.val("sDve")

        @blk.sync
        def _(sy):
            sy.wait_ge(self.S("sPe"), self.ws_free_pe)
            src = bass.AP(self.ag2aout, 1024 * 1024,
                          [[1024, 128], [self.AG2AR, 8], [1, 1024]])
            self.inc(sy.dma_start(self.WS[:, 0:8192], src), "cWbr")
        wbr_ld = self.val("cWbr")

        y_pe0 = self.val("sPe")
        y_dv0 = self.val("sDve")
        @blk.tensor
        def _(t):
            t.wait_ge(self.S("cWbr"), wbr_ld)
            t.wait_ge(self.S("sDve"), xpre_done)
            for co in range(8):
                yb = Y[:, (co % 2) * 512:(co % 2) * 512 + 512]
                if co >= 2:
                    t.wait_ge(self.S("sDve"), y_dv0 + co - 1)
                for kc in range(8):
                    ins = t.matmul(yb, self.WS[:, kc * 1024 + co * 128:kc * 1024 + (co + 1) * 128],
                                   XPRE[:, kc * 512:(kc + 1) * 512],
                                   start=(kc == 0), stop=(kc == 7))
                self.inc(ins, "sPe")

        @blk.vector
        def _(v):
            for co in range(8):
                v.wait_ge(self.S("sPe"), y_pe0 + co + 1)
                yb = Y[:, (co % 2) * 512:(co % 2) * 512 + 512]
                self.inc(v.tensor_copy(YB[:, co * 512:(co + 1) * 512], yb), "sDve")
                self.y_ev.append(self.val("sDve"))
        y_done = self.val("sDve")

        for i in range(4):
            bm_pe0 = self.val("sPe")
            bbf = self.bb_free
            bm_dv0 = self.val("sDve")
            @blk.tensor
            def _(t, i=i, bbf=bbf, bm_dv0=bm_dv0):
                if bbf:
                    t.wait_ge(self.S("sDve"), bbf)
                rows = [i * 4 + 0, i * 4 + 1, i * 4 + 2, i * 4 + 3, 20 + i]
                for ri, r in enumerate(rows):
                    if ri >= 1:
                        t.wait_ge(self.S("sDve"), bm_dv0 + ri)
                    self.inc(t.matmul(BC, self.selt_s[:, r * 128:(r + 1) * 128],
                                      self.PT[:, 3584:4096], start=True, stop=True), "sPe")

            @blk.vector
            def _(v, i=i):
                self.wait(v, "sDve")
                for ri in range(5):
                    v.wait_ge(self.S("sPe"), bm_pe0 + ri + 1)
                    self.inc(v.tensor_copy(BB(ri), BC[:, :]), "sDve")
            bb_done = self.val("sDve")

            @blk.vector
            def _(v, i=i, bb_done=bb_done, y_done=y_done):
                v.wait_ge(self.S("sDve"), max(bb_done, y_done))
                for c in range(8):
                    t0 = self.PT[:, 0:512]
                    t1 = self.PT[:, 512:1024]
                    t2 = self.PT[:, 1024:1536]
                    t3 = self.PT[:, 1536:2048]
                    t4 = self.PT[:, 2048:2560]
                    self.wait(v, "sDve")
                    v.tensor_mul(t0, self.XN[:, (0 * 8 + c) * 512:(0 * 8 + c) * 512 + 512], BB(0))
                    v.tensor_mul(t1, self.XN[:, (1 * 8 + c) * 512:(1 * 8 + c) * 512 + 512], BB(1))
                    v.tensor_mul(t2, self.XN[:, (2 * 8 + c) * 512:(2 * 8 + c) * 512 + 512], BB(2))
                    v.tensor_mul(t3, self.XN[:, (3 * 8 + c) * 512:(3 * 8 + c) * 512 + 512], BB(3))
                    self.inc(v.tensor_mul(t4, YB[:, c * 512:(c + 1) * 512], BB(4)), "sDve")
                    self.wait(v, "sDve")
                    mg = i * 8 + c
                    r1c = self.R1[:, mg * 512:(mg + 1) * 512]
                    self.inc(v.tensor_add(r1c, r1c, t0), "sDve")
                    self.wait(v, "sDve")
                    v.tensor_add(t1, t1, t2)
                    self.inc(v.tensor_add(t3, t3, t4), "sDve")
                    self.wait(v, "sDve")
                    self.inc(v.tensor_add(t1, t1, t3), "sDve")
                    self.wait(v, "sDve")
                    self.inc(v.tensor_add(r1c, r1c, t1), "sDve")
            self.bb_free = self.val("sDve")
        return self.val("sDve")

    def emit_attnb(self, blk, half, pp, xn_ready, pp_free_dve, ws_free_pe2=0):
        self.ws_free_pe2 = ws_free_pe2
        self.elu_act = []
        _sfx = str(half)
        agout = self.agout.ap()
        PTf = self.PT[:, :].bitcast(F32)
        SIGQ = self.AO[:, 0:4096]
        RETB = self.AO[:, 4096:8192]
        RSTRIDE = 3072 * 1024

        pe0 = self.val("sPe")
        slabv = []
        @blk.sync
        def _(sy):
            sy.wait_ge(self.S("sPe"), self.ws_free_pe2)
            for kc in range(DC):
                if kc >= 3:
                    sy.wait_ge(self.S("sPe"), pe0 + (kc - 3) + 1)
                rank, kl = kc // 4, kc % 4
                src = bass.AP(self.agout, rank * RSTRIDE + (2048 + kl * 128) * 1024,
                              [[1024, 128], [1, 1024]])
                slabv.append(self.inc(sy.dma_start(
                    self.WS[:, (kc % 3) * 1024:(kc % 3) * 1024 + 1024], src), f"cSb{kc % 3}"))

        @blk.tensor
        def _(t):
            t.wait_ge(self.S("sDve"), max(xn_ready, pp_free_dve))
            for kc in range(DC):
                t.wait_ge(self.S(f"cSb{kc % 3}"), slabv[kc])
                for m in range(8):
                    ins = t.matmul(pp[:, m * 512:(m + 1) * 512],
                                   self.WS[:, (kc % 3) * 1024 + m * 128:(kc % 3) * 1024 + (m + 1) * 128],
                                   self.XN[:, kc * 512:(kc + 1) * 512],
                                   start=(kc == 0), stop=(kc == DC - 1))
                self.inc(ins, "sPe")
        qs_done = self.val("sPe")

        elu_dv = []
        for m in range(8):
            @blk.vector
            def _(v, m=m, qs_done=qs_done):
                v.wait_ge(self.S("sPe"), qs_done)
                self.wait(v, "sDve")
                if m >= 2:
                    v.wait_ge(self.S("sAct"), self.elu_act[m - 2])
                u = PTf[:, (m % 2) * 1024:(m % 2) * 1024 + 512]
                r = PTf[:, (m % 2) * 1024 + 512:(m % 2) * 1024 + 1024]
                self.inc(v.tensor_scalar(u, pp[:, m * 512:(m + 1) * 512],
                                         self.bqs[:, m:m + 1], None, ALU.add), "sDve")
                self.wait(v, "sDve")
                self.inc(v.tensor_scalar(r, u, 0.0, None, ALU.max), "sDve")
                self.wait(v, "sDve")
                self.inc(v.tensor_sub(u, u, r), "sDve")
            t_dv = self.val("sDve")

            @blk.scalar
            def _(s_, m=m, t_dv=t_dv):
                s_.wait_ge(self.S("sDve"), t_dv)
                u = PTf[:, (m % 2) * 1024:(m % 2) * 1024 + 512]
                self.inc(s_.activation(u, u, AF.Exp), "sAct")
            self.elu_act.append(self.val("sAct"))
            e_act = self.val("sAct")

            @blk.vector
            def _(v, m=m, e_act=e_act):
                v.wait_ge(self.S("sAct"), e_act)
                u = PTf[:, (m % 2) * 1024:(m % 2) * 1024 + 512]
                r = PTf[:, (m % 2) * 1024 + 512:(m % 2) * 1024 + 1024]
                self.inc(v.tensor_add(SIGQ[:, m * 512:(m + 1) * 512], u, r), "sDve")
            elu_dv.append(self.val("sDve"))
        sig_done = self.val("sDve")

        NUM = [pp[:, 0:512], pp[:, 1024:1536]]
        DEN = [pp[:, 512:1024], pp[:, 1536:2048]]
        ret_dv = []
        for h in range(8):
            par = h % 2
            pe_h0 = self.val("sPe")
            @blk.tensor
            def _(t, h=h, par=par):
                if h == 0:
                    t.wait_ge(self.S("cMZ"), self.m_done)
                t.wait_ge(self.S("sDve"), elu_dv[min(2 * h + 1, 7)])
                if h >= 2:
                    t.wait_ge(self.S("sDve"), ret_dv[h - 2])

                t.matmul(NUM[par], self.mz[:, h * 128:(h + 1) * 128],
                         SIGQ[:, h * 512:(h + 1) * 512], start=True, stop=True)
                self.inc(t.matmul(DEN[par], self.mz[:, 1024 + h * 128:1024 + (h + 1) * 128],
                                  SIGQ[:, h * 512:(h + 1) * 512], start=True, stop=True), "sPe")
            nd_pe = self.val("sPe")

            @blk.vector
            def _(v, h=h, par=par, nd_pe=nd_pe):
                v.wait_ge(self.S("sPe"), nd_pe)
                dn = PTf[:, 1536:2048] if par else PTf[:, 512:1024]
                self.wait(v, "sDve")
                self.inc(v.tensor_scalar(dn, DEN[par], 1e-6, None, ALU.add), "sDve")
                self.wait(v, "sDve")
                self.inc(v.reciprocal(dn, dn), "sDve")
                self.wait(v, "sDve")
                self.inc(v.tensor_mul(RETB[:, h * 512:(h + 1) * 512], NUM[par], dn), "sDve")
            ret_dv.append(self.val("sDve"))
        ret_done = self.val("sDve")

        opj_pe0 = self.val("sPe")
        winv = []
        @blk.sync
        def _(sy):
            i = 0
            for oct_ in range(4):
                for jc in range(8):
                    if i >= 2:
                        sy.wait_ge(self.S("sPe"), opj_pe0 + (i - 1))
                    src = bass.AP(self.agout, (oct_ * 2) * RSTRIDE + (2560 + jc * 64) * 1024,
                                  [[512, 128], [RSTRIDE, 2], [1, 512]])
                    winv.append(self.inc(sy.dma_start(
                        self.WS[:, 4096 + (i % 2) * 1024:4096 + (i % 2) * 1024 + 1024], src),
                        f"cWf{i % 2}"))
                    i += 1

        dv0 = self.val("sDve")
        @blk.tensor
        def _(t):
            t.wait_ge(self.S("sDve"), ret_done)
            i = 0
            for oct_ in range(4):
                if oct_ >= 1:
                    t.wait_ge(self.S("sDve"), dv0 + 8 * oct_)
                for jc in range(8):
                    t.wait_ge(self.S(f"cWf{i % 2}"), winv[i])
                    base = 4096 + (i % 2) * 1024
                    for m in range(8):
                        mc = oct_ * 8 + m
                        col = ((mc // 4) - oct_ * 2) * 512 + (mc % 4) * 128
                        ins = t.matmul(pp[:, m * 512:(m + 1) * 512],
                                       self.WS[:, base + col:base + col + 128],
                                       RETB[:, jc * 512:(jc + 1) * 512],
                                       start=(jc == 0), stop=(jc == 7))
                    self.inc(ins, "sPe")
                    i += 1
        ob_pe0 = opj_pe0

        @blk.vector
        def _(v):
            for oct_ in range(4):
                v.wait_ge(self.S("sPe"), opj_pe0 + 8 * (oct_ + 1))
                for m in range(8):
                    mg = oct_ * 8 + m
                    r1c = self.R1[:, mg * 512:(mg + 1) * 512]
                    self.inc(v.tensor_add(r1c, r1c, pp[:, m * 512:(m + 1) * 512]), "sDve")
        return self.val("sDve")

    def emit_pipe(self, blk, half, ao_ready, r1_free_pe):
        nc = self.nc
        with nc.psum_tensor("pp%d" % half, [128, 4096], F32) as pp:
            self.pE, self.pF = pp[:, 0:512], pp[:, 512:1024]
            x1_done = self.emit_oproj(blk, half, pp, ao_ready, r1_free_pe)
            op_evac = self.val("sDve")
            xn1 = self.emit_ln(blk, self.R1, self.XN, self.epsln,
                               pe_waits=[("sDve", op_evac)])
            x2_done = self.emit_ffn(blk, half, pp, xn1, self.val("sDve"),
                                    ws_free_pe=self.val("sPe"))
            xn2 = self.emit_ln(blk, self.R1, self.XN, self.epsln,
                               pe_waits=[("sDve", x2_done)],
                               dve_waits=[("sDve", x2_done)])
            x3_done = self.emit_attnb(blk, half, pp, xn2, self.val("sDve"),
                                      ws_free_pe2=self.val("sPe"))
            xn3 = self.emit_ln(blk, self.R1, self.XN, self.epsln,
                               pe_waits=[("sDve", x3_done)],
                               dve_waits=[("sDve", x3_done)])
            x4_done = self.emit_ffn(blk, half, pp, xn3, self.val("sDve"),
                                    ws_free_pe=self.val("sPe"))

        @blk.sync
        def _(sy, half=half):
            sy.wait_ge(self.S("sDve"), x4_done)
            dst = bass.AP(self.yout.tensor, half * 512,
                          [[1024, 128], [131072, 32], [1, 512]])
            src = bass.AP(self.R1, self.R1[0:1, 0:1].offset,
                          [[self.R1[0:1, 0:1].ap[0][0], 128], [512, 32], [1, 512]])
            self.inc(sy.dma_start(dst, src), "cOut")
        return self.val("cOut")

    def build_full(self):
        nc = self.nc
        with nc.Block() as blk:
            self.emit_prologue(blk)
            q0_ready = self.val("sDve")
            self.emit_attention(blk, 0, q_ready=q0_ready, wait_cc=True)
            att0_pe = self.val("sPe")
            ao0 = self.val("sDve")
            self.emit_pipe(blk, 0, ao_ready=ao0, r1_free_pe=att0_pe)

            xnld0 = self.val("cXn1")
            @blk.sync
            def _(sy):
                sy.wait_ge(self.S("sDve"), self.val("sDve"))
                src = bass.AP(self.xn0b1.tensor, 0, [[512, 128], [65536, 32], [1, 512]])
                dst = bass.AP(self.XN, self.XN[0:1, 0:1].offset,
                              [[self.XN[0:1, 0:1].ap[0][0], 128], [512, 32], [1, 512]])
                self.inc(sy.dma_start(dst, src), "cXn1")
            xn1_ld = self.val("cXn1")

            @blk.vector
            def _(v):
                v.wait_ge(self.S("cXn1"), xn1_ld)
                self.inc(v.engine_nop(), "sDve")
            xnb1_ready = self.val("sDve")

            with nc.psum_tensor("qp1", [128, 4096], F32) as qp1:
                self.emit_qproj(blk, self.XN, qp1, xn_wait=xnb1_ready)
            q1_ready = self.val("sDve")
            self.emit_attention(blk, 1, q_ready=q1_ready, wait_cc=False,
                                r1_wait=("cOut", self.val("cOut")))
            att1_pe = self.val("sPe")
            ao1 = self.val("sDve")
            self.emit_pipe(blk, 1, ao_ready=ao1, r1_free_pe=att1_pe)

        with nc.Block() as blk2:
            @blk2.sync
            def _(sy):
                sy.wait_ge(self.S("cOut"), self.val("cOut"))
        return nc



def kernel(**inputs):
    m = MK("full")
    nc = m.build()
    in_maps = host_build_inputs(inputs)
    res = run_bass_kernel_spmd(nc, in_maps, core_ids=list(range(NCORE)))
    return host_gather_output(res.results)
